# version 14
# baseline (speedup 1.0000x reference)
from contextlib import ExitStack

import numpy as np
import concourse.bass as bass
import concourse.mybir as mybir
from concourse.bass_utils import run_bass_kernel_spmd

F32 = mybir.dt.float32
BF16 = mybir.dt.bfloat16
AF = mybir.ActivationFunctionType
ALU = mybir.AluOpType

B, S, D = 2, 8192, 1024
NCORES = 8
TPC = 2048
CH = 128
NCH = TPC // CH
ST = 256
NST = TPC // ST
CPS = ST // CH
DIN = 2048
NH = 32
HD = 64
NEC = 18
P1E = NEC * 128 + NH
KD = D // 128
DFF = 4096
NFC = DFF // 128
EPS = 1e-5

TAPS = {}
LIMIT_PHASE = 3


def _spike_thresholds():
    def count(y):
        y = np.float32(y)
        mem = np.float32(0.0)
        c = 0
        for _ in range(5):
            reset = np.float32(1.0) if mem > np.float32(1.0) else np.float32(0.0)
            mem = np.float32(np.float32(np.float32(0.9) * mem) + y) - reset
            c += 1 if mem > np.float32(1.0) else 0
        return c

    bs = []
    for j in range(1, 6):
        lo, hi = -4.0, 4.0
        for _ in range(60):
            mid = (lo + hi) / 2
            if count(mid) >= j:
                hi = mid
            else:
                lo = mid
        bs.append(float(np.float32(lo)))
    return bs


SPIKE_B = _spike_thresholds()


class Sched:

    ENG = ("pe", "dv", "ac", "po", "sp")

    def __init__(self):
        self.ops = {e: [] for e in self.ENG}
        self.counts = {}
        self.waited = {e: {} for e in self.ENG}

    def sem(self, name):
        self.counts.setdefault(name, 0)
        return name

    def emit(self, eng, fn, waits=(), inc=None, amt=1):
        pruned = []
        for w in waits:
            if w is None:
                continue
            sm, v = w
            assert v <= self.counts[sm], f"wait on future {sm}:{v}>{self.counts[sm]}"
            if v > self.waited[eng].get(sm, 0):
                self.waited[eng][sm] = v
                pruned.append((sm, v))
        self.ops[eng].append((fn, pruned, inc, amt))
        if inc is not None:
            self.counts[inc] += amt
            return (inc, self.counts[inc])
        return None

    def tok(self, sem):
        return (sem, self.counts[sem])


def build_nc():
    nc = bass.Bass("TRN2", target_bir_lowering=False, debug=False)
    s = Sched()
    oes = ExitStack()

    xs = nc.dram_tensor("xs", [TPC, D], F32, kind="ExternalInput").ap()
    xprev = nc.dram_tensor("xprev", [3, D], F32, kind="ExternalInput").ap()
    w_p1 = nc.dram_tensor("w_p1", [D, P1E], F32, kind="ExternalInput").ap()
    w_z = nc.dram_tensor("w_z", [D, DIN], F32, kind="ExternalInput").ap()
    w_out = nc.dram_tensor("w_out", [DIN, D], F32, kind="ExternalInput").ap()
    w_fc = nc.dram_tensor("w_fc", [D, DFF], F32, kind="ExternalInput").ap()
    w_pr = nc.dram_tensor("w_pr", [DFF, D], F32, kind="ExternalInput").ap()
    cw_in = nc.dram_tensor("cw", [128, NEC * 4], F32, kind="ExternalInput").ap()
    cb_in = nc.dram_tensor("cb", [128, NEC], F32, kind="ExternalInput").ap()
    dtb_in = nc.dram_tensor("dtb", [NH, 1], F32, kind="ExternalInput").ap()
    alog_in = nc.dram_tensor("alog", [NH, 1], F32, kind="ExternalInput").ap()
    db_in = nc.dram_tensor("dvec", [128, NH], F32, kind="ExternalInput").ap()
    nw_in = nc.dram_tensor("nw", [128, 16], F32, kind="ExternalInput").ap()
    sel_in = nc.dram_tensor("sel", [128, 8], F32, kind="ExternalInput").ap()
    selm_in = nc.dram_tensor("selm", [128, 8], F32, kind="ExternalInput").ap()
    out = nc.dram_tensor("out", [TPC, D], F32, kind="ExternalOutput").ap()

    spk = "ExternalOutput" if "spills" in TAPS else "Internal"
    y_sp = nc.dram_tensor("y_sp", [NCH, 128, DIN], F32, kind=spk).ap()
    xm_sp = nc.dram_tensor("xm_sp", [NCH, 128, D], F32, kind=spk).ap()
    mout = nc.dram_tensor("mout", [128, DIN + NH], F32,
                          kind="ExternalOutput" if "mout" in TAPS else "Internal").ap()
    mg = nc.dram_tensor("mg", [4 * 128, DIN + NH], F32, kind="Internal").ap()

    tapd = {}
    for name, shape in TAPS.items():
        if name in ("spills", "mout"):
            continue
        tapd[name] = nc.dram_tensor("tap_" + name, list(shape), F32,
                                    kind="ExternalOutput").ap()

    PE = s.sem("pe"); DV = s.sem("dv"); AC = s.sem("ac"); PO = s.sem("po")
    DSP = s.sem("dsp"); DPO = s.sem("dpo"); CC = s.sem("cc")
    DX = [s.sem("dx0"), s.sem("dx1")]
    DY = [s.sem("dy0"), s.sem("dy1")]
    DZ = [s.sem("dz0"), s.sem("dz1")]

    def tap(name, src_fn, waits=(), q="sp"):
        if name in tapd:
            s.emit(q, lambda e, n=name: e.dma_start(out=tapd[n], in_=src_fn()),
                   waits=waits, inc=DSP if q == "sp" else DPO, amt=16)

    def sb(es, name, shape, dt):
        return es.enter_context(nc.sbuf_tensor(name, shape, dt))

    identb = sb(oes, "identb", [128, 128], BF16)
    identf = sb(oes, "identf", [128, 128], F32)
    stepb = sb(oes, "stepb", [128, 128], BF16)
    cmaskb = sb(oes, "cmaskb", [128, 128], BF16)
    e127 = sb(oes, "e127", [128, 128], F32)
    zero32 = sb(oes, "zero32", [128, 32], F32)
    onescol = sb(oes, "onescol", [32, 1], F32)
    smallcol = sb(oes, "smallcol", [NH, 4], F32)
    db32 = sb(oes, "db32", [128, NH], F32)
    selb = sb(oes, "selb", [128, 8], F32)
    selb1m = sb(oes, "selb1m", [128, 8], F32)
    cw_sb = sb(oes, "cw_sb", [128, NEC * 4], F32)
    cb_sb = sb(oes, "cb_sb", [128, NEC], F32)
    nw_sb = sb(oes, "nw_sb", [128, 16], F32)
    stat = sb(oes, "stat", [128, 40], F32)
    carry = sb(oes, "carry", [32, 1], F32)
    eps_t = sb(oes, "eps_t", [128, 1], F32)
    p12 = ExitStack()
    cadt = sb(p12, "cadt", [128, NCH * 96], F32)
    c_all = sb(p12, "c_all", [128, NCH * 128], BF16)
    psA = oes.enter_context(nc.psum_tensor("psA", [128, 512], F32))
    psB = oes.enter_context(nc.psum_tensor("psB", [128, 512], F32))
    psY1 = oes.enter_context(nc.psum_tensor("psY1", [128, 512], F32))
    psY2 = oes.enter_context(nc.psum_tensor("psY2", [128, 512], F32))
    psM = oes.enter_context(nc.psum_tensor("psM", [128, 512], F32))
    psT = oes.enter_context(nc.psum_tensor("psT", [128, 2048], BF16))
    psT2 = oes.enter_context(nc.psum_tensor("psT2", [128, 1024], BF16))
    sems = {}
    for nm in ("pe", "dv", "ac", "po", "dsp", "dpo", "cc", "dx0", "dx1",
               "dy0", "dy1", "dz0", "dz1"):
        sems[nm] = oes.enter_context(nc.semaphore(nm))

    cadt_v = cadt.rearrange("p (c w) -> p c w", w=96)
    call_v = c_all.rearrange("p (c t) -> p c t", t=128)

    rotst = {"i": 0, "tok": [None, None]}

    def rot_ps():
        i = rotst["i"]
        rotst["i"] ^= 1
        return (psA, psB)[i], rotst["tok"][i], i

    def rot_done(i, token):
        rotst["tok"][i] = token

    psT_g = [None]
    psT2_g = [None]

    s.emit("po", lambda e: e.memset(identb[:], 0.0))
    s.emit("po", lambda e: e.affine_select(out=identb[:], in_=identb[:],
           compare_op=ALU.not_equal, fill=1.0, base=0,
           pattern=[[-1, 128]], channel_multiplier=1))
    s.emit("po", lambda e: e.memset(identf[:], 0.0))
    s.emit("po", lambda e: e.affine_select(out=identf[:], in_=identf[:],
           compare_op=ALU.not_equal, fill=1.0, base=0,
           pattern=[[-1, 128]], channel_multiplier=1))
    s.emit("po", lambda e: e.memset(stepb[:], 1.0))
    s.emit("po", lambda e: e.affine_select(out=stepb[:], in_=stepb[:],
           compare_op=ALU.is_gt, fill=0.0, base=0,
           pattern=[[-1, 128]], channel_multiplier=1))
    s.emit("po", lambda e: e.memset(cmaskb[:], 1.0))
    s.emit("po", lambda e: e.affine_select(out=cmaskb[:], in_=cmaskb[:],
           compare_op=ALU.is_ge, fill=0.0, base=0,
           pattern=[[1, 128]], channel_multiplier=-1))
    s.emit("po", lambda e: e.memset(e127[:], 0.0))
    s.emit("po", lambda e: e.affine_select(out=e127[:], in_=e127[:],
           compare_op=ALU.not_equal, fill=1.0, base=-127,
           pattern=[[0, 128]], channel_multiplier=1))
    s.emit("po", lambda e: e.memset(zero32[:], 0.0))
    s.emit("po", lambda e: e.memset(onescol[:], 1.0))
    s.emit("po", lambda e: e.memset(carry[:], 0.0))
    s.emit("po", lambda e: e.memset(eps_t[:], EPS), inc=PO)
    CPO = s.tok(PO)

    s.emit("sp", lambda e: e.dma_start(out=cw_sb[:], in_=cw_in), inc=DSP, amt=16)
    s.emit("sp", lambda e: e.dma_start(out=cb_sb[:], in_=cb_in), inc=DSP, amt=16)
    s.emit("sp", lambda e: e.dma_start(out=nw_sb[:], in_=nw_in), inc=DSP, amt=16)
    s.emit("sp", lambda e: e.dma_start(out=db32[:], in_=db_in), inc=DSP, amt=16)
    s.emit("sp", lambda e: e.dma_start(out=selb[:], in_=sel_in), inc=DSP, amt=16)
    s.emit("sp", lambda e: e.dma_start(out=selb1m[:], in_=selm_in), inc=DSP, amt=16)
    s.emit("sp", lambda e: e.dma_start(out=smallcol[:, 0:1], in_=dtb_in,
           allow_slow_non_contiguous=True), inc=DSP, amt=16)
    s.emit("sp", lambda e: e.dma_start(out=smallcol[:, 1:2], in_=alog_in,
           allow_slow_non_contiguous=True), inc=DSP, amt=16)
    LD0 = s.tok(DSP)
    t = s.emit("ac", lambda e: e.activation(smallcol[:, 2:3],
               smallcol[:, 1:2], AF.Exp), waits=[LD0], inc=AC)
    A_TOK = s.emit("dv", lambda e: e.tensor_scalar(out=smallcol[:, 2:3],
                   in0=smallcol[:, 2:3], scalar1=-1.0, scalar2=None,
                   op0=ALU.mult), waits=[t], inc=DV)
    ln_prev = {}

    def ln_chunk(xin_ap, dst, par, waits):
        c0 = 16 * par
        nf = xin_ap.shape[-1]
        prev = ln_prev.get(par)
        wl = list(waits) + ([prev[0]] if prev else [])
        h = nf // 2
        t1 = s.emit("dv", lambda e: e.bn_stats(stat[:xin_ap.shape[0], c0:c0 + 6],
                    xin_ap[:, 0:h]), waits=wl, inc=DV)
        t1 = s.emit("dv", lambda e: e.bn_stats(
                    stat[:xin_ap.shape[0], c0 + 6:c0 + 12], xin_ap[:, h:nf]),
                    waits=[t1], inc=DV)
        t2 = s.emit("dv", lambda e: e.bn_aggr(
                    stat[:xin_ap.shape[0], c0 + 12:c0 + 14],
                    stat[:xin_ap.shape[0], c0:c0 + 12]), waits=[t1], inc=DV)
        t3 = s.emit("ac", lambda e: e.activation(
                    stat[:xin_ap.shape[0], c0 + 13:c0 + 14],
                    stat[:xin_ap.shape[0], c0 + 13:c0 + 14], AF.Ln,
                    bias=eps_t[:xin_ap.shape[0], 0:1]),
                    waits=[t2, CPO], inc=AC)
        t4 = s.emit("ac", lambda e: e.activation(
                    stat[:xin_ap.shape[0], c0 + 13:c0 + 14],
                    stat[:xin_ap.shape[0], c0 + 13:c0 + 14], AF.Exp,
                    scale=-0.5), waits=[t3], inc=AC)
        t5 = s.emit("dv", lambda e: e.scalar_tensor_tensor(
            out=stat[:xin_ap.shape[0], c0 + 12:c0 + 13],
            in0=stat[:xin_ap.shape[0], c0 + 12:c0 + 13], scalar=-1.0,
            in1=stat[:xin_ap.shape[0], c0 + 13:c0 + 14], op0=ALU.mult,
            op1=ALU.mult), waits=[t4], inc=DV)
        t6 = s.emit("dv", lambda e: e.tensor_scalar(
            out=dst, in0=xin_ap,
            scalar1=stat[:xin_ap.shape[0], c0 + 13:c0 + 14],
            scalar2=stat[:xin_ap.shape[0], c0 + 12:c0 + 13],
            op0=ALU.mult, op1=ALU.add), waits=[t5], inc=DV)
        ln_prev[par] = (t4, t6)
        return t6

    G = {}

    def g(*key):
        return G.get(key)

    p1 = ExitStack()
    convd = sb(p1, "convd", [128, NEC * 4 * 128], BF16)
    Hst = sb(p1, "H", [128, DIN], F32)
    hb0 = sb(p1, "hb0", [128, DIN], BF16)
    hb1 = sb(p1, "hb1", [128, DIN], BF16)
    s_ca = sb(p1, "s_ca", [32, ST], F32)
    s_dt = sb(p1, "s_dt", [32, ST], F32)
    s_dta = sb(p1, "s_dta", [32, ST], F32)
    wp1 = sb(p1, "wp1", [128, KD * P1E], BF16)
    xin0 = sb(p1, "xin0", [128, D], F32)
    xin1 = sb(p1, "xin1", [128, D], F32)
    xln0 = sb(p1, "xln0", [128, D], BF16)
    xln1 = sb(p1, "xln1", [128, D], BF16)
    xpl = sb(p1, "xpl", [3, D], BF16)
    xlf0 = sb(p1, "xlf0", [128, KD * 259], BF16)
    xlf1 = sb(p1, "xlf1", [128, KD * 259], BF16)
    xbc = sb(p1, "xbc", [128, NEC * 259], BF16)
    xc = sb(p1, "xc", [128, 16 * ST], BF16)
    b_bf = sb(p1, "b_bf", [128, ST], BF16)
    b_tma = sb(p1, "b_tma", [128, 128], BF16)
    b_tmb = sb(p1, "b_tmb", [128, 128], BF16)
    sp_scr = sb(p1, "sp_scr", [32, ST], F32)
    sp_scr2 = sb(p1, "sp_scr2", [32, ST], F32)
    sc32a = sb(p1, "sc32a", [128, 96], F32)
    sc32b = sb(p1, "sc32b", [128, 96], F32)
    eo_bfa = sb(p1, "eo_bfa", [128, NH], BF16)
    eo_bfb = sb(p1, "eo_bfb", [128, NH], BF16)
    erc_bfa = sb(p1, "erc_bfa", [128, NH], BF16)
    erc_bfb = sb(p1, "erc_bfb", [128, NH], BF16)
    gama = sb(p1, "gama", [128, NH], F32)
    gamb = sb(p1, "gamb", [128, NH], F32)
    dt_bfa = sb(p1, "dt_bfa", [128, NH], BF16)
    dt_bfb = sb(p1, "dt_bfb", [128, NH], BF16)
    dta_bfa = sb(p1, "dta_bfa", [128, NH], BF16)
    dta_bfb = sb(p1, "dta_bfb", [128, NH], BF16)
    rhsT = sb(p1, "rhsT", [128, 4096], BF16)
    expd = sb(p1, "expd", [128, 4096], BF16)
    ghat0 = sb(p1, "ghat0", [128, 4096], BF16)
    ghat1 = sb(p1, "ghat1", [128, 4096], BF16)
    gm_bfa = sb(p1, "gm_bfa", [128, 128], BF16)
    gm_bfb = sb(p1, "gm_bfb", [128, 128], BF16)
    xtm0 = sb(p1, "xtm0", [128, DIN], BF16)
    xtm1 = sb(p1, "xtm1", [128, DIN], BF16)
    xda = sb(p1, "xda", [128, DIN], BF16)
    xdb = sb(p1, "xdb", [128, DIN], BF16)
    xdza = sb(p1, "xdza", [128, DIN], BF16)
    xdzb = sb(p1, "xdzb", [128, DIN], BF16)
    ytm0 = sb(p1, "ytm0", [128, DIN], F32)
    ytm1 = sb(p1, "ytm1", [128, DIN], F32)
    yscr = sb(p1, "yscr", [128, 512], F32)
    hscr = sb(p1, "hscr", [128, 512], F32)
    atile = sb(p1, "atile", [128, NH], F32)

    hb = [hb0, hb1]
    convd_v = convd.rearrange("p (i t) -> p i t", t=128)
    s.emit("po", lambda e: e.memset(Hst[:], 0.0))
    s.emit("po", lambda e: e.memset(hb0[:], 0.0), inc=PO)
    P1Z = s.tok(PO)
    cd = None
    for i in range(NEC * 4):
        cd = s.emit("dv", lambda e, i=i: e.tensor_scalar(
            out=convd_v[:, i, :], in0=identb[:],
            scalar1=cw_sb[:, i:i + 1], scalar2=None, op0=ALU.mult),
            waits=[LD0, CPO], inc=DV)
    CONVD_TOK = cd
    wp1_v = wp1.rearrange("p (k e) -> p k e", k=KD)
    xlf_vs = [xlf0.rearrange("p (k t) -> p k t", k=KD),
              xlf1.rearrange("p (k t) -> p k t", k=KD)]
    xbc_v = xbc.rearrange("p (c t) -> p c t", t=259)
    xc_v = xc.rearrange("p (c t) -> p c t", t=ST)
    xins = [xin0, xin1]
    xlns = [xln0, xln1]
    xtms = [xtm0, xtm1]
    ytms = [ytm0, ytm1]
    ghats = [ghat0, ghat1]
    sc32s = [sc32a, sc32b]
    eo_bfs = [eo_bfa, eo_bfb]
    erc_bfs = [erc_bfa, erc_bfb]
    gams = [gama, gamb]
    dt_bfs = [dt_bfa, dt_bfb]
    dta_bfs = [dta_bfa, dta_bfb]
    gm_bfs = [gm_bfa, gm_bfb]
    b_tms = [b_tma, b_tmb]
    xds = [xda, xdb]
    xdzs = [xdza, xdzb]

    for k in range(KD):
        s.emit("po", lambda e, k=k: e.dma_start(
            out=wp1_v[:, k, :], in_=w_p1[k * 128:(k + 1) * 128, :]),
            waits=[CPO], inc=DPO, amt=16)
    WP1 = s.tok(DPO)

    s.emit("sp", lambda e: e.dma_start(out=xin0[0:3, :], in_=xprev),
           inc=DSP, amt=16)
    XPL = s.tok(DSP)
    tp6 = ln_chunk(xin0[0:3, :], xpl[0:3, :], 0, [XPL])
    lastp = None
    for k in range(KD):
        lastp = s.emit("pe", lambda e, k=k: e.transpose(
            psT2[:, k * 128:k * 128 + 3], xpl[0:3, k * 128:(k + 1) * 128],
            identb[0:3, 0:3]), waits=[tp6] if k == 0 else (),
            inc=PE if k == KD - 1 else None)
    ev = s.emit("dv", lambda e: e.tensor_copy(
        xlf_vs[0][:, :, 0:3],
        psT2.rearrange("p (k t) -> p k t", t=128)[:, 0:KD, 0:3]),
        waits=[lastp], inc=DV)
    psT2_g[0] = ev
    XPREV_EV = ev
    G[("xin_r", -2)] = tp6

    def p1_front(st):
        ch0 = st * CPS
        xlf_v = xlf_vs[st % 2]
        evs = []
        for c2 in range(CPS):
            cg = ch0 + c2
            par = cg % 2
            xin, xln = xins[par], xlns[par]
            s.emit("sp", lambda e, xin=xin, cg=cg: e.dma_start(
                out=xin[:], in_=xs[cg * CH:(cg + 1) * CH, :]),
                waits=[g("xin_r", cg - 2)], inc=DX[par], amt=16)
            xt = s.tok(DX[par])
            lt = ln_chunk(xin[:], xln[:], par, [xt])
            G[("xin_r", cg)] = lt
            wl = [lt, g("xlf_r", st - 2), psT2_g[0], g("xln_r", cg - 2)]
            last = None
            for k in range(KD):
                last = s.emit("pe", lambda e, k=k, xln=xln: e.transpose(
                    psT2[:, k * 128:(k + 1) * 128],
                    xln[:, k * 128:(k + 1) * 128], identb[:]),
                    waits=wl if k == 0 else (),
                    inc=PE if k == KD - 1 else None)
            G[("xln_r", cg)] = last
            ev = s.emit("dv", lambda e, c2=c2, xlf_v=xlf_v: e.tensor_copy(
                xlf_v[:, :, 3 + c2 * CH:3 + (c2 + 1) * CH],
                psT2.rearrange("p (k t) -> p k t", t=128)),
                waits=[last], inc=DV)
            psT2_g[0] = ev
            evs.append(ev)
        if st == 0:
            evs.append(XPREV_EV)
        G[("p1front", st)] = evs

    p1_front(0)
    for st in range(NST):
        ch0 = st * CPS
        xlf_v = xlf_vs[st % 2]
        ln_evs = G[("p1front", st)]

        margin = None
        if st > 0:
            margin = s.emit("dv", lambda e: e.tensor_copy(
                xbc_v[:, :, 0:3], xbc_v[:, :, 256:259]),
                waits=[g("conv_r", st - 1)], inc=DV)

        co = 0 if st == 0 else 3
        N = 259 if st == 0 else 256
        doff = 0 if st == 0 else 3
        evac = []
        dt_info = None
        for ei in range(NEC + 1):
            ps, guard, ri = rot_ps()
            m = NH if ei == NEC else 128
            wl = ln_evs + [WP1, guard]
            last = None
            for k in range(KD):
                last = s.emit("pe",
                    lambda e, k=k, ei=ei, ps=ps, m=m, co=co, N=N,
                    xlf_v=xlf_v: e.matmul(
                        ps[0:m, 0:N], wp1_v[:, k, ei * 128:ei * 128 + m],
                        xlf_v[:, k, co:co + N],
                        start=(k == 0), stop=(k == KD - 1)),
                    waits=wl if k == 0 else (),
                    inc=PE if k == KD - 1 else None)
            if ei < NEC:
                ev = s.emit("ac", lambda e, ei=ei, ps=ps, doff=doff, N=N:
                            e.copy(xbc_v[:, ei, doff:doff + N], ps[:, 0:N]),
                            waits=[last, margin], inc=AC)
                evac.append(ev)
                rot_done(ri, ev)
            else:
                dt_info = (ps, ri, last)
        G[("xlf_r", st)] = s.tok(PE)
        if st + 1 < NST:
            p1_front(st + 1)

        ps, ri, mmtok = dt_info
        off = 3 if st == 0 else 0
        stc = slice(0, ST)
        t1 = s.emit("dv", lambda e, ps=ps, off=off: e.tensor_scalar(
            out=sp_scr[:, :], in0=ps[0:NH, off:off + ST],
            scalar1=smallcol[0:NH, 0:1], scalar2=None, op0=ALU.add),
            waits=[mmtok, A_TOK, g("spscr_r", st - 1)], inc=DV)
        rot_done(ri, t1)
        t2 = s.emit("ac", lambda e: e.activation(sp_scr2[:, :], sp_scr[:, :],
                    AF.Abs), waits=[t1], inc=AC)
        t3 = s.emit("ac", lambda e: e.activation(sp_scr2[:, :], sp_scr2[:, :],
                    AF.Exp, scale=-1.0), waits=[t2], inc=AC)
        t4 = s.emit("ac", lambda e: e.activation(sp_scr2[:, :], sp_scr2[:, :],
                    AF.Ln, bias=1.0), waits=[t3], inc=AC)
        t5 = s.emit("dv", lambda e: e.tensor_scalar(out=sp_scr[:, :],
                    in0=sp_scr[:, :], scalar1=0.0, scalar2=None,
                    op0=ALU.max), waits=[t4], inc=DV)
        t6 = s.emit("dv", lambda e, stc=stc: e.tensor_tensor(
            s_dt[:, stc], sp_scr[:, :], sp_scr2[:, :], ALU.add),
            waits=[t5, g("stk_r", st - 1)], inc=DV)
        G[("spscr_r", st)] = t6
        t7 = s.emit("dv", lambda e, stc=stc: e.tensor_scalar(
            out=s_dta[:, stc], in0=s_dt[:, stc],
            scalar1=smallcol[0:NH, 2:3], scalar2=None, op0=ALU.mult),
            waits=[t6], inc=DV)
        t8 = s.emit("dv", lambda e, stc=stc: e.tensor_tensor_scan(
            s_ca[:, stc], onescol[:, 0:1].broadcast_to([32, ST]),
            s_dta[:, stc], carry[:, 0:1], ALU.mult, ALU.add),
            waits=[t7, CPO], inc=DV)
        SCAN = s.emit("dv", lambda e: e.tensor_copy(
            carry[:, 0:1], s_ca[:, ST - 1:ST]), waits=[t8], inc=DV)

        conv_ev = []
        conv_last = None
        for ei in range(NEC):
            ps, guard, ri = rot_ps()
            wl = [evac[ei], CONVD_TOK, guard]
            last = None
            for tp in range(4):
                last = s.emit("pe", lambda e, ei=ei, tp=tp, ps=ps: e.matmul(
                    ps[:, 0:ST], convd_v[:, ei * 4 + tp, :],
                    xbc_v[:, ei, tp:tp + ST], start=(tp == 0), stop=(tp == 3)),
                    waits=wl if tp == 0 else (), inc=PE if tp == 3 else None)
            conv_last = last
            if ei < 16:
                dst = xc_v[:, ei, :]
                wl2 = [last, g("xc_r", st - 1)]
            elif ei == 16:
                dst = b_bf[:, :]
                wl2 = [last, g("bbf_r", st - 1)]
            else:
                dst = c_all[:, st * ST:(st + 1) * ST]
                wl2 = [last]
            ev = s.emit("ac", lambda e, ps=ps, dst=dst, ei=ei: e.activation(
                dst, ps[:, 0:ST], AF.Silu, bias=cb_sb[:, ei:ei + 1]),
                waits=wl2, inc=AC)
            conv_ev.append(ev)
            rot_done(ri, ev)
        G[("conv_r", st)] = conv_last

        def p1_prep(c2):
            cg = ch0 + c2
            par = cg % 2
            ccols = slice(c2 * CH, (c2 + 1) * CH)
            csl = ccols
            xtm = xtms[par]
            ghat = ghats[par]
            sc32 = sc32s[par]
            eo_bf, erc_bf, gam = eo_bfs[par], erc_bfs[par], gams[par]
            dt_bf, dta_bf = dt_bfs[par], dta_bfs[par]
            gm_bf, b_tm = gm_bfs[par], b_tms[par]
            xd, xdz = xds[par], xdzs[par]

            s.emit("pe", lambda e, ccols=ccols: e.transpose(
                psM[:, 0:32], s_ca[:, ccols], identf[0:32, 0:32]),
                waits=[SCAN, g("psm_ra", cg - 1)])
            s.emit("pe", lambda e, ccols=ccols: e.transpose(
                psM[:, 32:64], s_dt[:, ccols], identf[0:32, 0:32]))
            tT = s.emit("pe", lambda e, ccols=ccols: e.transpose(
                psM[:, 64:96], s_dta[:, ccols], identf[0:32, 0:32]), inc=PE)
            evc = s.emit("ac", lambda e, cg=cg: e.copy(
                cadt_v[:, cg, :], psM[:, 0:96]), waits=[tT], inc=AC)
            G[("p_evc", cg)] = evc
            prev_rhs = cadt_v[:, cg - 1, 0:32] if cg > 0 else zero32[:, 0:32]
            sp1 = s.emit("pe", lambda e, prev_rhs=prev_rhs: e.matmul(
                psM[:, 96:128], e127[:], prev_rhs, start=True, stop=True),
                waits=[evc], inc=PE)
            sp2 = s.emit("pe", lambda e, cg=cg: e.matmul(
                psM[:, 128:160], e127[:], cadt_v[:, cg, 0:32],
                start=True, stop=True), waits=[evc], inc=PE)
            wl = [sp1, sp2]
            a1 = s.emit("dv", lambda e, cg=cg, sc32=sc32: e.tensor_tensor(
                sc32[:, 0:32], cadt_v[:, cg, 0:32], psM[:, 96:128],
                ALU.subtract), waits=wl, inc=DV)
            a2 = s.emit("dv", lambda e, cg=cg, sc32=sc32: e.tensor_tensor(
                sc32[:, 32:64], psM[:, 128:160], cadt_v[:, cg, 0:32],
                ALU.subtract), waits=wl, inc=DV)
            a3 = s.emit("dv", lambda e, sc32=sc32: e.tensor_tensor(
                sc32[:, 64:96], sc32[:, 0:32], sc32[:, 32:64], ALU.add),
                waits=[a1, a2], inc=DV)
            wl_e = [g("qend_dv", cg - 2), g("qend_po", cg - 2)]
            e1 = s.emit("ac", lambda e, eo_bf=eo_bf, sc32=sc32: e.activation(
                eo_bf[:], sc32[:, 0:32], AF.Exp), waits=[a1] + wl_e, inc=AC)
            e2 = s.emit("ac", lambda e, erc_bf=erc_bf, sc32=sc32: e.activation(
                erc_bf[:], sc32[:, 32:64], AF.Exp), waits=[a2] + wl_e, inc=AC)
            e3 = s.emit("ac", lambda e, gam=gam, sc32=sc32: e.activation(
                gam[:], sc32[:, 64:96], AF.Exp), waits=[a3] + wl_e, inc=AC)
            d1 = s.emit("ac", lambda e, cg=cg, dt_bf=dt_bf: e.copy(
                dt_bf[:], cadt_v[:, cg, 32:64]), waits=[evc], inc=AC)
            d2 = s.emit("ac", lambda e, cg=cg, dta_bf=dta_bf: e.copy(
                dta_bf[:], cadt_v[:, cg, 64:96]), waits=[evc], inc=AC)
            rb0 = s.emit("po", lambda e, dta_bf=dta_bf: e.tensor_tensor(
                rhsT.rearrange("p (h t) -> p h t", h=NH)[:, 0:16, :],
                dta_bf[:, 0:16, None].broadcast_to([128, 16, 128]),
                cmaskb[:, None, :].broadcast_to([128, 16, 128]),
                ALU.mult), waits=[d2, g("rhsT_r", cg - 1)], inc=PO)
            rb = s.emit("dv", lambda e, dta_bf=dta_bf: e.tensor_tensor(
                rhsT.rearrange("p (h t) -> p h t", h=NH)[:, 16:32, :],
                dta_bf[:, 16:32, None].broadcast_to([128, 16, 128]),
                cmaskb[:, None, :].broadcast_to([128, 16, 128]),
                ALU.mult), waits=[d2, g("rhsT_r", cg - 1)], inc=DV)
            gmm = s.emit("pe", lambda e, csl=csl, cg=cg: e.matmul(
                psM[:, 160:288], b_bf[:, csl], call_v[:, cg, :],
                start=True, stop=True),
                waits=[conv_ev[16], conv_ev[17], a1, a2], inc=PE)
            gmb = s.emit("dv", lambda e, gm_bf=gm_bf: e.tensor_tensor(
                gm_bf[:], psM[:, 160:288], cmaskb[:], ALU.mult),
                waits=[gmm], inc=DV)
            G[("psm_ra", cg)] = gmb
            if cg == NCH - 1:
                G[("la",)] = s.emit("ac", lambda e: e.activation(
                    atile[:], psM[:, 128:160], AF.Exp), waits=[a1, a2], inc=AC)
            gh_dv = None
            gh_po = None
            am_last = None
            for gi in range(8):
                ps, guard, ri = rot_ps()
                am = s.emit("pe", lambda e, gi=gi, ps=ps: e.matmul(
                    ps[:, 0:512], stepb[:], rhsT[:, gi * 512:(gi + 1) * 512],
                    start=True, stop=True),
                    waits=[rb0 if gi < 4 else rb, guard, CPO], inc=PE)
                am_last = am
                ex = s.emit("ac", lambda e, gi=gi, ps=ps: e.activation(
                    expd[:, gi * 512:(gi + 1) * 512], ps[:, 0:512], AF.Exp),
                    waits=[am, g("expd_r", cg - 1), g("expd_rp", cg - 1)],
                    inc=AC)
                rot_done(ri, ex)
                qeng = "dv" if gi % 2 == 0 else "po"
                gh = s.emit(qeng, lambda e, gi=gi, ghat=ghat, gm_bf=gm_bf:
                    e.tensor_tensor(
                    ghat.rearrange("p (h t) -> p h t", h=NH)[:, gi * 4:(gi + 1) * 4, :],
                    expd.rearrange("p (h t) -> p h t", h=NH)[:, gi * 4:(gi + 1) * 4, :],
                    gm_bf[:, None, :].broadcast_to([128, 4, 128]),
                    ALU.mult), waits=[ex, gmb, g("ghat_r", cg - 2)],
                    inc=DV if gi % 2 == 0 else PO)
                if gi % 2 == 0:
                    gh_dv = gh
                else:
                    gh_po = gh
            G[("rhsT_r", cg)] = am_last
            G[("expd_r", cg)] = gh_dv
            G[("expd_rp", cg)] = gh_po
            bt = s.emit("pe", lambda e, csl=csl: e.transpose(
                psT[:, 0:128], b_bf[:, csl], identb[:]),
                waits=[conv_ev[16], psT_g[0]], inc=PE)
            if c2 == CPS - 1:
                G[("bbf_r", st)] = bt
            bte = s.emit("dv", lambda e, b_tm=b_tm: e.tensor_copy(
                b_tm[:], psT[:, 0:128]),
                waits=[bt, g("xdz_r", cg - 2)], inc=DV)
            psT_g[0] = bte
            wl = [conv_ev[15], bte, g("xtm_r", cg - 2)]
            last = None
            for i in range(16):
                last = s.emit("pe", lambda e, i=i, csl=csl: e.transpose(
                    psT[:, i * 128:(i + 1) * 128], xc_v[:, i, csl], identb[:]),
                    waits=wl if i == 0 else (), inc=PE if i == 15 else None)
            if c2 == CPS - 1:
                G[("xc_r", st)] = last
            xte = s.emit("ac", lambda e, xtm=xtm: e.copy(xtm[:], psT[:]),
                         waits=[last], inc=AC)
            psT_g[0] = xte
            xdt = s.emit("dv", lambda e, xtm=xtm, xd=xd, dt_bf=dt_bf:
                e.tensor_tensor(
                xd.rearrange("p (h q) -> p h q", h=NH),
                xtm.rearrange("p (h q) -> p h q", h=NH),
                dt_bf[:, :, None].broadcast_to([128, NH, HD]),
                ALU.mult), waits=[xte, d1, g("xd_r", cg - 2)], inc=DV)
            xzt = s.emit("dv", lambda e, xd=xd, xdz=xdz, erc_bf=erc_bf:
                e.tensor_tensor(
                xdz.rearrange("p (h q) -> p h q", h=NH),
                xd.rearrange("p (h q) -> p h q", h=NH),
                erc_bf[:, :, None].broadcast_to([128, NH, HD]),
                ALU.mult), waits=[xdt, e2, g("dxbf_r", cg - 2)], inc=DV)
            G[("p_tok", cg)] = (gh_dv, gh_po, xdt, xzt, e1, e3, bte)
            if c2 == CPS - 1:
                G[("stk_r", st)] = tT

        def p1_quarters(c2):
            cg = ch0 + c2
            par = cg % 2
            xtm, ytm = xtms[par], ytms[par]
            ghat = ghats[par]
            hbr, hbw = hb[par], hb[1 - par]
            eo_bf, gam = eo_bfs[par], gams[par]
            b_tm = b_tms[par]
            xd, xdz = xds[par], xdzs[par]
            gh_dv, gh_po, xdt, xzt, e1, e3, bte = G[("p_tok", cg)]
            for q in range(4):
                qs = slice(q * 512, (q + 1) * 512)
                pa, pb = (psY1, psY2) if q % 2 == 0 else (psA, psB)
                wl1 = [gh_dv, gh_po, xdt, g("q_ya", cg * 4 + q - 2)]
                if q % 2 == 1:
                    wl1 += [rotst["tok"][0]]
                l1 = None
                for j in range(8):
                    h = q * 8 + j
                    l1 = s.emit("pe", lambda e, h=h, j=j, ghat=ghat, xd=xd,
                        pa=pa: e.matmul(
                        pa[:, j * 64:(j + 1) * 64],
                        ghat[:, h * 128:(h + 1) * 128],
                        xd[:, h * HD:(h + 1) * HD], start=True, stop=True),
                        waits=wl1 if j == 0 else (), inc=PE if j == 7 else None)
                G[("ghat_r", cg)] = l1
                G[("xd_r", cg)] = l1
                wl2 = [g("q_yb", cg * 4 + q - 2), g("hb_w", cg - 1, q), P1Z]
                if q % 2 == 1:
                    wl2 += [rotst["tok"][1]]
                l2 = None
                for j in range(8):
                    h = q * 8 + j
                    l2 = s.emit("pe", lambda e, h=h, j=j, hbr=hbr, cg=cg,
                        pb=pb: e.matmul(
                        pb[:, j * 64:(j + 1) * 64], call_v[:, cg, :],
                        hbr[:, h * HD:(h + 1) * HD], start=True, stop=True),
                        waits=wl2 if j == 0 else (), inc=PE if j == 7 else None)
                G[("hb_r", cg, q)] = l2
                l3 = None
                s_extra = []
                if q == 0:
                    other = ch0 + 1 if c2 == 0 else cg
                    s_extra = [g("psm_ra", other), g("p_evc", other),
                               g("la")]
                for j in range(8):
                    h = q * 8 + j
                    l3 = s.emit("pe", lambda e, h=h, j=j, b_tm=b_tm,
                        xdz=xdz: e.matmul(
                        psM[:, j * 64:(j + 1) * 64], b_tm[:],
                        xdz[:, h * HD:(h + 1) * HD], start=True, stop=True),
                        waits=[bte, xzt, g("q_ha", cg * 4 + q - 1)] + s_extra
                        if j == 0 else (), inc=PE if j == 7 else None)
                G[("xdz_r", cg)] = l3
                yb = s.emit("dv", lambda e, q=q, eo_bf=eo_bf, pb=pb:
                    e.tensor_tensor(
                    yscr.rearrange("p (j w) -> p j w", j=8),
                    pb.rearrange("p (j w) -> p j w", j=8),
                    eo_bf[:, q * 8:(q + 1) * 8, None].broadcast_to([128, 8, HD]),
                    ALU.mult), waits=[l2, e1, g("ytm_sp", cg - 2)], inc=DV)
                G[("q_yb", cg * 4 + q)] = yb
                ya = s.emit("dv", lambda e, ytm=ytm, qs=qs, pa=pa:
                    e.tensor_tensor(
                    ytm[:, qs], pa[:, 0:512], yscr[:, 0:512], ALU.add),
                    waits=[yb, l1], inc=DV)
                G[("q_ya", cg * 4 + q)] = ya
                hm = s.emit("po", lambda e, q=q, qs=qs, gam=gam: e.tensor_tensor(
                    hscr.rearrange("p (j w) -> p j w", j=8),
                    Hst[:, qs].rearrange("p (j w) -> p j w", j=8),
                    gam[:, q * 8:(q + 1) * 8, None].broadcast_to([128, 8, HD]),
                    ALU.mult), waits=[e3, g("hadd_prev"), P1Z], inc=PO)
                ha = s.emit("dv", lambda e, qs=qs: e.tensor_tensor(
                    Hst[:, qs], hscr[:, 0:512], psM[:, 0:512], ALU.add),
                    waits=[hm, l3], inc=DV)
                G[("q_ha", cg * 4 + q)] = ha
                G[("hadd_prev",)] = ha
                hw = s.emit("ac", lambda e, qs=qs, hbw=hbw: e.copy(
                    hbw[:, qs], Hst[:, qs]),
                    waits=[ha, g("hb_r", cg - 1, q)], inc=AC)
                G[("hb_w", cg, q)] = hw
                if cg == NCH - 1 and q == 3:
                    G[("h_final",)] = ha
            rotst["tok"][0] = G[("q_ya", cg * 4 + 3)]
            rotst["tok"][1] = G[("q_yb", cg * 4 + 3)]
            G[("psm_ra", cg)] = G[("q_ha", cg * 4 + 3)]
            G[("qend_dv", cg)] = s.tok(DV)
            dxt = s.emit("po", lambda e, xtm=xtm, xdz=xdz: e.tensor_tensor(
                xdz.rearrange("p (h w) -> p h w", h=NH),
                xtm.rearrange("p (h w) -> p h w", h=NH),
                db32[:, :, None].broadcast_to([128, NH, HD]), ALU.mult),
                waits=[LD0, g("xdz_r", cg)], inc=PO)
            G[("xtm_r", cg)] = dxt
            ydx = s.emit("po", lambda e, ytm=ytm, xdz=xdz: e.tensor_tensor(
                ytm[:], ytm[:], xdz[:], ALU.add),
                waits=[dxt, s.tok(DV)], inc=PO)
            G[("dxbf_r", cg)] = ydx
            G[("qend_po", cg)] = s.tok(PO)
            s.emit("sp", lambda e, ytm=ytm, cg=cg: e.dma_start(
                out=y_sp[cg], in_=ytm[:]), waits=[ydx], inc=DY[par], amt=16)
            G[("ytm_sp", cg)] = s.tok(DY[par])

        p1_prep(0)
        p1_prep(1)
        p1_quarters(0)
        p1_quarters(1)

    wz_v = wp1_v[:, :, 0:DIN]
    if LIMIT_PHASE >= 2:
        for k in range(KD):
            s.emit("po", lambda e, k=k: e.dma_start(
                out=wz_v[:, k, :], in_=w_z[k * 128:(k + 1) * 128, :]),
                waits=[g("xlf_r", NST - 1)], inc=DPO, amt=16)
    WZTOK = s.tok(DPO)
    s.emit("sp", lambda e: e.dma_start(out=mout[:, 0:DIN], in_=Hst[:]),
           waits=[g("h_final")], inc=DSP, amt=16)
    s.emit("sp", lambda e: e.dma_start(out=mout[:, DIN:DIN + NH],
           in_=atile[:]), waits=[g("la")], inc=DSP, amt=16)
    MOUT = s.tok(DSP)
    tap("sdt", lambda: s_dt[:], waits=[s.tok(DV)])
    tap("sca", lambda: s_ca[:], waits=[s.tok(DV)])
    tap("cadt", lambda: cadt[:], waits=[s.tok(DV)])
    p1.close()

    if LIMIT_PHASE >= 2:
        PH1 = [s.tok(sm) for sm in ("pe", "dv", "ac", "po", "dsp",
                                    "dx0", "dx1", "dy0", "dy1")]
        for eng in ("pe", "dv", "ac", "po", "sp"):
            s.emit(eng, lambda e: e.nop(), waits=PH1)
        ccd = s.emit("po", lambda e: e.collective_compute(
            "AllGather", ALU.bypass,
            replica_groups=[[0, 1, 2, 3], [4, 5, 6, 7]],
            ins=[mout.opt()], outs=[mg.opt()]), waits=[MOUT], inc=CC)

        p2 = ExitStack()
        DEAD_B = 18432 + 8192 + 4096 + 4096 + 3 * 1024
        mga = sb(p2, "mga", [128, 4 * (DIN + NH)], F32)
        hinb = sb(p2, "hinb", [128, DIN], BF16)
        used = 4 * (DIN + NH) * 4 + 4096
        assert used <= DEAD_B, used
        if DEAD_B - used:
            sb(p2, "p2pad0", [128, (DEAD_B - used) // 2], BF16)
        sb(p2, "p2pad1", [128, KD * P1E], BF16)
        wo = sb(p2, "wo", [128, 16 * D], BF16)
        hin = sb(p2, "hin", [128, DIN], F32)
        msel = sb(p2, "msel", [128, NH], F32)
        p2xin0 = sb(p2, "p2xin0", [128, D], F32)
        p2xin1 = sb(p2, "p2xin1", [128, D], F32)
        p2xln0 = sb(p2, "p2xln0", [128, D], BF16)
        p2xln1 = sb(p2, "p2xln1", [128, D], BF16)
        xlf2a = sb(p2, "xlf2a", [128, KD * 128], BF16)
        xlf2b = sb(p2, "xlf2b", [128, KD * 128], BF16)
        yld0 = sb(p2, "yld0", [128, DIN], F32)
        yld1 = sb(p2, "yld1", [128, DIN], F32)
        gtm0 = sb(p2, "gtm0", [128, DIN], F32)
        gtm1 = sb(p2, "gtm1", [128, DIN], F32)
        eog = sb(p2, "eog", [128, NH], BF16)
        yg = sb(p2, "yg", [128, DIN], F32)
        yscr2 = sb(p2, "yscr2", [128, 512], F32)
        yn = sb(p2, "yn", [128, DIN], BF16)
        ynf = sb(p2, "ynf", [128, 16 * 128], BF16)
        xmid0 = sb(p2, "xmid0", [128, D], F32)
        xmid1 = sb(p2, "xmid1", [128, D], F32)

        wo_v = wo.rearrange("p (c d) -> p c d", c=16)
        ynf_v = ynf.rearrange("p (c t) -> p c t", c=16)
        xlf2_vs = [xlf2a.rearrange("p (k t) -> p k t", k=KD),
                   xlf2b.rearrange("p (k t) -> p k t", k=KD)]
        p2xins = [p2xin0, p2xin1]
        p2xlns = [p2xln0, p2xln1]
        ylds = [yld0, yld1]
        xmids = [xmid0, xmid1]
        mga_v = mga.rearrange("p (r c) -> p r c", r=4)
        hz = s.emit("po", lambda e: e.memset(hin[:], 0.0), inc=PO)

        for c in range(16):
            s.emit("po", lambda e, c=c: e.dma_start(
                out=wo_v[:, c, :], in_=w_out[c * 128:(c + 1) * 128, :]),
                inc=DPO, amt=16)
        W2 = s.tok(DPO)
        nwt = None
        for c in range(16):
            nwt = s.emit("dv", lambda e, c=c: e.tensor_scalar(
                out=wo_v[:, c, :], in0=wo_v[:, c, :],
                scalar1=nw_sb[:, c:c + 1], scalar2=None, op0=ALU.mult),
                waits=[W2, LD0], inc=DV)
        WOUT = nwt

        def p2_combine():
            s.emit("po", lambda e: e.dma_start(
                out=mga_v[:, :, :],
                in_=mg.rearrange("(r p) c -> p r c", p=128)),
                waits=[ccd], inc=DPO, amt=16)
            mjt = s.tok(DPO)
            prev = None
            for j in range(4):
                ms = s.emit("dv", lambda e, j=j: e.tensor_scalar(
                    out=msel[:], in0=mga_v[:, j, DIN:DIN + NH],
                    scalar1=selb[:, 2 * j:2 * j + 1],
                    scalar2=selb1m[:, 2 * j:2 * j + 1],
                    op0=ALU.mult, op1=ALU.add), waits=[mjt, LD0], inc=DV)
                hm2 = s.emit("dv", lambda e: e.tensor_tensor(
                    hin.rearrange("p (h w) -> p h w", h=NH),
                    hin.rearrange("p (h w) -> p h w", h=NH),
                    msel[:, :, None].broadcast_to([128, NH, HD]), ALU.mult),
                    waits=[ms, hz], inc=DV)
                prev = s.emit("dv", lambda e, j=j: e.scalar_tensor_tensor(
                    out=hin[:], in0=mga_v[:, j, 0:DIN],
                    scalar=selb[:, 2 * j + 1:2 * j + 2], in1=hin[:],
                    op0=ALU.mult, op1=ALU.add), waits=[hm2], inc=DV)
            hib = s.emit("dv", lambda e: e.tensor_copy(hinb[:], hin[:]),
                         waits=[prev], inc=DV)
            tap("hin", lambda: hin[:], waits=[prev])
            return hib

        def p2_front(cg):
            par = cg % 2
            xin, xln = p2xins[par], p2xlns[par]
            yld = ylds[par]
            s.emit("sp", lambda e, xin=xin, cg=cg: e.dma_start(
                out=xin[:], in_=xs[cg * CH:(cg + 1) * CH, :]),
                waits=[g("p2xin_r", cg - 2)], inc=DX[par], amt=16)
            s.emit("sp", lambda e, yld=yld, cg=cg: e.dma_start(
                out=yld[:], in_=y_sp[cg]),
                waits=[g("yld_r", cg - 2)], inc=DY[par], amt=16)
            G[("p2ylt", cg)] = s.tok(DY[par])
            xt = s.tok(DX[par])
            lt = ln_chunk(xin[:], xln[:], par, [xt])
            wl = [lt, psT2_g[0], g("xlf2_r", cg - 2), g("p2xln_r", cg - 2)]
            last = None
            for k in range(KD):
                last = s.emit("pe", lambda e, k=k, xln=xln: e.transpose(
                    psT2[:, k * 128:(k + 1) * 128],
                    xln[:, k * 128:(k + 1) * 128], identb[:]),
                    waits=wl if k == 0 else (), inc=PE if k == KD - 1 else None)
            G[("p2xln_r", cg)] = last
            ev = s.emit("dv", lambda e, cg=cg: e.tensor_copy(
                xlf2_vs[cg % 2][:, :, :],
                psT2.rearrange("p (k t) -> p k t", t=128)),
                waits=[last], inc=DV)
            psT2_g[0] = ev
            G[("p2ev", cg)] = ev

        def p2_z(cg):
            par = cg % 2
            xlf2_v = xlf2_vs[par]
            ev = G[("p2ev", cg)]
            gtm = [gtm0, gtm1][par]
            zev = []
            for gi in range(4):
                ps, guard, ri = rot_ps()
                lastz = None
                for k in range(KD):
                    lastz = s.emit("pe", lambda e, k=k, gi=gi, ps=ps,
                                   xlf2_v=xlf2_v: e.matmul(
                        ps[:, 0:512], xlf2_v[:, k, :],
                        wz_v[:, k, gi * 512:(gi + 1) * 512],
                        start=(k == 0), stop=(k == KD - 1)),
                        waits=[ev, WZTOK, guard] if k == 0 else (),
                        inc=PE if k == KD - 1 else None)
                zv = s.emit("ac", lambda e, gi=gi, ps=ps, gtm=gtm: e.activation(
                    gtm[:, gi * 512:(gi + 1) * 512], ps[:, 0:512], AF.Silu),
                    waits=[lastz, g("gtm_r", cg - 2)], inc=AC)
                zev.append(zv)
                rot_done(ri, zv)
            G[("xlf2_r", cg)] = s.tok(PE)
            G[("p2zev", cg)] = zev

        p2_front(0)
        p2_z(0)
        p2_front(1)
        p2_z(1)
        hib = p2_combine()
        for cg in range(NCH):
            par = cg % 2
            xin = p2xins[par]
            yld = ylds[par]
            xmid = xmids[par]
            gtm = [gtm0, gtm1][par]
            ev = G[("p2ev", cg)]
            ylt = G[("p2ylt", cg)]
            zev = G[("p2zev", cg)]
            if cg + 1 < NCH and ("p2zev", cg + 1) not in G:
                p2_z(cg + 1)
            eg = s.emit("ac", lambda e, cg=cg: e.activation(
                eog[:], cadt_v[:, cg, 0:32], AF.Exp),
                waits=[g("eog_r", cg - 1)], inc=AC)
            ygt = None
            for q in range(4):
                qs = slice(q * 512, (q + 1) * 512)
                psq = psY1 if q % 2 == 0 else psY2
                wl = [hib, g("psq_r", q % 2)]
                l2 = None
                for j in range(8):
                    h = q * 8 + j
                    l2 = s.emit("pe", lambda e, h=h, j=j, psq=psq, cg=cg:
                        e.matmul(psq[:, j * 64:(j + 1) * 64], call_v[:, cg, :],
                        hinb[:, h * HD:(h + 1) * HD], start=True, stop=True),
                        waits=wl if j == 0 else (), inc=PE if j == 7 else None)
                t1q = s.emit("dv", lambda e, q=q, psq=psq: e.tensor_tensor(
                    yscr2.rearrange("p (j w) -> p j w", j=8),
                    psq.rearrange("p (j w) -> p j w", j=8),
                    eog[:, q * 8:(q + 1) * 8, None].broadcast_to([128, 8, HD]),
                    ALU.mult), waits=[l2, eg], inc=DV)
                G[("psq_r", q % 2)] = t1q
                t2q = s.emit("dv", lambda e, qs=qs, yld=yld: e.tensor_tensor(
                    yscr2[:, 0:512], yscr2[:, 0:512], yld[:, qs], ALU.add),
                    waits=[t1q, ylt], inc=DV)
                ygt = s.emit("dv", lambda e, qs=qs, gtm=gtm: e.tensor_tensor(
                    yg[:, qs], yscr2[:, 0:512], gtm[:, qs], ALU.mult),
                    waits=[t2q, zev[q], g("yg_r", cg - 1)], inc=DV)
            G[("yld_r", cg)] = ygt
            G[("eog_r", cg)] = ygt
            G[("gtm_r", cg)] = ygt
            sq = s.emit("ac", lambda e: e.activation(
                yn[:], yg[:], AF.Square, accum_out=stat[:, 32:33]),
                waits=[ygt, g("rms_r", cg - 1), g("yn_r", cg - 1)], inc=AC)
            r1 = s.emit("dv", lambda e: e.tensor_scalar(
                out=stat[:, 33:34], in0=stat[:, 32:33], scalar1=1.0 / DIN,
                scalar2=EPS, op0=ALU.mult, op1=ALU.add), waits=[sq], inc=DV)
            r2 = s.emit("ac", lambda e: e.activation(stat[:, 33:34],
                        stat[:, 33:34], AF.Ln), waits=[r1], inc=AC)
            r3 = s.emit("ac", lambda e: e.activation(stat[:, 33:34],
                        stat[:, 33:34], AF.Exp, scale=-0.5), waits=[r2], inc=AC)
            G[("rms_r", cg)] = r3
            ynt = s.emit("dv", lambda e: e.tensor_scalar(
                out=yn[:], in0=yg[:], scalar1=stat[:, 33:34], scalar2=None,
                op0=ALU.mult), waits=[r3, g("yn_r", cg - 1)], inc=DV)
            G[("yg_r", cg)] = ynt
            wl = [ynt, psT_g[0]]
            last = None
            for i in range(16):
                last = s.emit("pe", lambda e, i=i: e.transpose(
                    psT[:, i * 128:(i + 1) * 128], yn[:, i * 128:(i + 1) * 128],
                    identb[:]), waits=wl if i == 0 else (),
                    inc=PE if i == 15 else None)
            G[("yn_r", cg)] = last
            evy = s.emit("dv", lambda e: e.tensor_copy(ynf[:], psT[:]),
                         waits=[last], inc=DV)
            psT_g[0] = evy
            xr = None
            for gi in range(2):
                ps, guard, ri = rot_ps()
                lasto = None
                for c in range(16):
                    lasto = s.emit("pe", lambda e, c=c, gi=gi, ps=ps: e.matmul(
                        ps[:, 0:512], ynf_v[:, c, :],
                        wo_v[:, c, gi * 512:(gi + 1) * 512],
                        start=(c == 0), stop=(c == 15)),
                        waits=[evy, WOUT, guard] if c == 0 else (),
                        inc=PE if c == 15 else None)
                xr = s.emit("dv", lambda e, gi=gi, ps=ps, xmid=xmid, xin=xin:
                    e.tensor_tensor(xmid[:, gi * 512:(gi + 1) * 512],
                    ps[:, 0:512], xin[:, gi * 512:(gi + 1) * 512], ALU.add),
                    waits=[lasto, g("xmid_sp", cg - 2)], inc=DV)
                rot_done(ri, xr)
            G[("p2xin_r", cg)] = xr
            s.emit("sp", lambda e, xmid=xmid, cg=cg: e.dma_start(
                out=xm_sp[cg], in_=xmid[:]), waits=[xr], inc=DZ[par], amt=16)
            G[("xmid_sp", cg)] = s.tok(DZ[par])
            if cg + 2 < NCH:
                p2_front(cg + 2)
        tap("gtm", lambda: gtm1[:], waits=[s.tok(AC)])
        tap("yg", lambda: yg[:], waits=[s.tok(DV)])
        p2.close()
        p12.close()

        if LIMIT_PHASE >= 3:
            PH2 = [s.tok(sm) for sm in ("pe", "dv", "ac", "po", "dsp",
                                        "dx0", "dx1", "dy0", "dy1",
                                        "dz0", "dz1")]
            for eng in ("pe", "dv", "ac", "po", "sp"):
                s.emit(eng, lambda e: e.nop(), waits=PH2)
            p3 = ExitStack()
            wfc = sb(p3, "wfc", [128, KD * DFF], BF16)
            wpr = sb(p3, "wpr", [128, NFC * D], BF16)
            xm0 = sb(p3, "xm0", [128, D], F32)
            xm1 = sb(p3, "xm1", [128, D], F32)
            xm2 = sb(p3, "xm2", [128, D], F32)
            xm3 = sb(p3, "xm3", [128, D], F32)
            xl20 = sb(p3, "xl20", [128, D], BF16)
            xl21 = sb(p3, "xl21", [128, D], BF16)
            xlf3a = sb(p3, "xlf3a", [128, KD * ST], BF16)
            xlf3b = sb(p3, "xlf3b", [128, KD * ST], BF16)
            rlb = sb(p3, "rlb", [128, ST], BF16)
            hbuf = sb(p3, "hbuf", [128, NFC * ST], BF16)
            y2 = sb(p3, "y2", [128, D], F32)
            spka = sb(p3, "spka", [128, D], BF16)
            spkb = sb(p3, "spkb", [128, D], BF16)
            outb0 = sb(p3, "outb0", [128, D], F32)
            outb1 = sb(p3, "outb1", [128, D], F32)

            wfc_v = wfc.rearrange("p (k f) -> p k f", k=KD)
            wpr_v = wpr.rearrange("p (f d) -> p f d", f=NFC)
            xlf3_vs = [xlf3a.rearrange("p (k t) -> p k t", k=KD),
                       xlf3b.rearrange("p (k t) -> p k t", k=KD)]
            hbuf_v = hbuf.rearrange("p (f t) -> p f t", f=NFC)
            xms = [xm0, xm1, xm2, xm3]
            xl2s = [xl20, xl21]
            outbs = [outb0, outb1]

            for k in range(KD):
                s.emit("po", lambda e, k=k: e.dma_start(
                    out=wfc_v[:, k, :], in_=w_fc[k * 128:(k + 1) * 128, :]),
                    inc=DPO, amt=16)
            WFC = s.tok(DPO)
            for f in range(NFC):
                s.emit("po", lambda e, f=f: e.dma_start(
                    out=wpr_v[:, f, :], in_=w_pr[f * 128:(f + 1) * 128, :]),
                    inc=DPO, amt=16)
            WPR = s.tok(DPO)

            def p3_front(st):
                ch0 = st * CPS
                xlf3_v = xlf3_vs[st % 2]
                evs = []
                for c2 in range(CPS):
                    cg = ch0 + c2
                    par = cg % 2
                    xm, xl2 = xms[cg % 4], xl2s[par]
                    s.emit("sp", lambda e, xm=xm, cg=cg: e.dma_start(
                        out=xm[:], in_=xm_sp[cg]),
                        waits=[g("xm_r", cg - 4)], inc=DY[par], amt=16)
                    xt = s.tok(DY[par])
                    lt = ln_chunk(xm[:], xl2[:], par, [xt])
                    wl = [lt, psT2_g[0], g("xlf3_r", st - 2), g("xl2_r", cg - 2)]
                    last = None
                    for k in range(KD):
                        last = s.emit("pe", lambda e, k=k, xl2=xl2: e.transpose(
                            psT2[:, k * 128:(k + 1) * 128],
                            xl2[:, k * 128:(k + 1) * 128], identb[:]),
                            waits=wl if k == 0 else (),
                            inc=PE if k == KD - 1 else None)
                    G[("xl2_r", cg)] = last
                    ev = s.emit("dv", lambda e, c2=c2, xlf3_v=xlf3_v:
                        e.tensor_copy(
                        xlf3_v[:, :, c2 * CH:(c2 + 1) * CH],
                        psT2.rearrange("p (k t) -> p k t", t=128)),
                        waits=[last], inc=DV)
                    psT2_g[0] = ev
                    evs.append(ev)
                G[("p3front", st)] = evs

            p3_front(0)
            for st in range(NST):
                ch0 = st * CPS
                xlf3_v = xlf3_vs[st % 2]
                evs = G[("p3front", st)]
                for f in range(NFC):
                    ps, guard, ri = rot_ps()
                    lastf = None
                    for k in range(KD):
                        lastf = s.emit("pe", lambda e, k=k, f=f, ps=ps,
                                       xlf3_v=xlf3_v: e.matmul(
                            ps[:, 0:ST], wfc_v[:, k, f * 128:(f + 1) * 128],
                            xlf3_v[:, k, :], start=(k == 0), stop=(k == KD - 1)),
                            waits=evs + [WFC, guard] if k == 0 else (),
                            inc=PE if k == KD - 1 else None)
                    rl = s.emit("ac", lambda e, ps=ps: e.activation(
                        rlb[:], ps[:, 0:ST], AF.Relu),
                        waits=[lastf, g("rlb_r")], inc=AC)
                    hev = s.emit("dv", lambda e, f=f: e.tensor_tensor(
                        hbuf_v[:, f, :], rlb[:], rlb[:], ALU.mult),
                        waits=[rl, g("hbuf_r", st - 1)], inc=DV)
                    G[("rlb_r",)] = hev
                    rot_done(ri, hev)
                if st + 1 < NST:
                    p3_front(st + 1)
                HEV = s.tok(DV)
                for c2 in range(CPS):
                    cg = ch0 + c2
                    par = cg % 2
                    xm = xms[cg % 4]
                    outb = outbs[par]
                    for gi in range(2):
                        psq = psY1 if gi == 0 else psY2
                        lasto = None
                        for f in range(NFC):
                            lasto = s.emit("pe",
                                lambda e, f=f, gi=gi, psq=psq, c2=c2: e.matmul(
                                    psq[:, 0:512],
                                    hbuf_v[:, f, c2 * CH:(c2 + 1) * CH],
                                    wpr_v[:, f, gi * 512:(gi + 1) * 512],
                                    start=(f == 0), stop=(f == NFC - 1)),
                                waits=[HEV, WPR, g("p3ps_r", gi)]
                                if f == 0 else (),
                                inc=PE if f == NFC - 1 else None)
                        yv = s.emit("ac", lambda e, gi=gi, psq=psq: e.copy(
                            y2[:, gi * 512:(gi + 1) * 512], psq[:, 0:512]),
                            waits=[lasto, g("y2_r", cg - 1)], inc=AC)
                        G[("p3ps_r", gi)] = yv
                    Y2 = s.tok(AC)
                    sp0 = s.emit("dv", lambda e: e.tensor_scalar(
                        out=spka[:], in0=y2[:], scalar1=SPIKE_B[0],
                        scalar2=None, op0=ALU.is_gt),
                        waits=[Y2, g("spka_r", cg - 2)], inc=DV)
                    acc = sp0
                    for j in range(1, 5):
                        sj = s.emit("dv", lambda e, j=j: e.tensor_scalar(
                            out=spkb[:], in0=y2[:], scalar1=SPIKE_B[j],
                            scalar2=None, op0=ALU.is_gt),
                            waits=[Y2, g("spkb_r")], inc=DV)
                        acc = s.emit("dv", lambda e: e.tensor_tensor(
                            spka[:], spka[:], spkb[:], ALU.add),
                            waits=[sj, acc], inc=DV)
                        G[("spkb_r",)] = acc
                    G[("y2_r", cg)] = acc
                    ob = s.emit("dv", lambda e, outb=outb, xm=xm:
                        e.scalar_tensor_tensor(out=outb[:], in0=spka[:],
                        scalar=0.2, in1=xm[:], op0=ALU.mult, op1=ALU.add),
                        waits=[acc, g("outb_sp", cg - 2)], inc=DV)
                    G[("spka_r", cg)] = ob
                    G[("xm_r", cg)] = ob
                    s.emit("sp", lambda e, outb=outb, cg=cg: e.dma_start(
                        out=out[cg * CH:(cg + 1) * CH, :], in_=outb[:]),
                        waits=[ob], inc=DZ[par], amt=16)
                    G[("outb_sp", cg)] = s.tok(DZ[par])
                G[("xlf3_r", st)] = s.tok(PE)
                G[("hbuf_r", st)] = s.tok(PE)
            tap("y2t", lambda: y2[:], waits=[s.tok(AC)])
            tap("spk", lambda: spka[:], waits=[s.tok(DV)], q="po")
            tap("hbt", lambda: hbuf[:], waits=[s.tok(DV)], q="po")
            p3.close()

    if LIMIT_PHASE < 2:
        p12.close()
    fin = [s.tok(sm) for sm in ("dsp", "dx0", "dx1", "dy0", "dy1",
                                "dz0", "dz1", "dpo")]
    if LIMIT_PHASE >= 2:
        fin.append(s.tok("cc"))
    s.emit("sp", lambda e: e.nop(), waits=fin)

    with nc.Block() as block:
        def mk(eng_name):
            def body(eng):
                for fn, waits, inc, amt in s.ops[eng_name]:
                    for sm, v in waits:
                        eng.wait_ge(sems[sm], v)
                    r = fn(eng)
                    if inc is not None:
                        r.then_inc(sems[inc], amt)
            return body
        block.tensor(mk("pe"))
        block.vector(mk("dv"))
        block.scalar(mk("ac"))
        block.gpsimd(mk("po"))
        block.sync(mk("sp"))

    oes.close()
    return nc


_CACHE = {}


def _get_nc():
    key = (LIMIT_PHASE, tuple(sorted(map(str, TAPS))))
    if key not in _CACHE:
        _CACHE[key] = build_nc()
    return _CACHE[key]


def kernel(L0_attn_X, L0_attn_W_in, L0_attn_conv_w, L0_attn_conv_b,
           L0_attn_dt_bias, L0_attn_A_log, L0_attn_D, L0_attn_norm_w,
           L0_attn_W_out, L0_ffn_W_fc, L0_ffn_W_proj):
    X = np.asarray(L0_attn_X, np.float32)
    W_in = np.asarray(L0_attn_W_in, np.float32)
    conv_w = np.asarray(L0_attn_conv_w, np.float32)
    conv_b = np.asarray(L0_attn_conv_b, np.float32)
    dt_bias = np.asarray(L0_attn_dt_bias, np.float32)
    A_log = np.asarray(L0_attn_A_log, np.float32)
    Dv = np.asarray(L0_attn_D, np.float32)
    norm_w = np.asarray(L0_attn_norm_w, np.float32)
    W_out = np.asarray(L0_attn_W_out, np.float32)
    W_fc = np.asarray(L0_ffn_W_fc, np.float32)
    W_proj = np.asarray(L0_ffn_W_proj, np.float32)

    w_p1h = np.ascontiguousarray(W_in[DIN:, :].T)
    w_zh = np.ascontiguousarray(W_in[:DIN, :].T)
    w_oh = np.ascontiguousarray(W_out.T)
    w_fh = np.ascontiguousarray(W_fc.T)
    w_ph = np.ascontiguousarray(W_proj.T)
    cw = np.ascontiguousarray(
        conv_w.reshape(NEC, 128, 4).transpose(1, 0, 2).reshape(128, NEC * 4))
    cb = np.ascontiguousarray(conv_b.reshape(NEC, 128).T)
    dtb = np.ascontiguousarray(dt_bias.reshape(NH, 1))
    alog = np.ascontiguousarray(A_log.reshape(NH, 1))
    dvec = np.ascontiguousarray(np.broadcast_to(Dv[None, :], (128, NH)))
    nw = np.ascontiguousarray(norm_w.reshape(16, 128).T)

    in_maps = []
    for k in range(NCORES):
        b, q = divmod(k, 4)
        xs_k = np.ascontiguousarray(X[b, q * TPC:(q + 1) * TPC])
        if q == 0:
            xp = np.zeros((3, D), np.float32)
        else:
            xp = np.ascontiguousarray(X[b, q * TPC - 3:q * TPC])
        sel = np.zeros((1, 8), np.float32)
        for j in range(4):
            if j < q:
                sel[0, 2 * j] = 1.0
                sel[0, 2 * j + 1] = 1.0
        selm = 1.0 - sel
        in_maps.append({
            "xs": xs_k, "xprev": xp, "w_p1": w_p1h, "w_z": w_zh,
            "w_out": w_oh, "w_fc": w_fh, "w_pr": w_ph, "cw": cw, "cb": cb,
            "dtb": dtb, "alog": alog, "dvec": dvec, "nw": nw,
            "sel": np.ascontiguousarray(np.broadcast_to(sel, (128, 8))),
            "selm": np.ascontiguousarray(np.broadcast_to(selm, (128, 8))),
        })

    nc = _get_nc()
    res = run_bass_kernel_spmd(nc, in_maps, core_ids=list(range(NCORES)))
    globals()["LAST_RES"] = res
    outp = np.empty((B, S, D), np.float32)
    for k in range(NCORES):
        b, q = divmod(k, 4)
        outp[b, q * TPC:(q + 1) * TPC] = res.results[k]["out"]
    return outp


# version 16
# speedup vs baseline: 1.0633x; 1.0633x over previous
from contextlib import ExitStack

import numpy as np
import concourse.bass as bass
import concourse.mybir as mybir
from concourse.bass_utils import run_bass_kernel_spmd

F32 = mybir.dt.float32
BF16 = mybir.dt.bfloat16
AF = mybir.ActivationFunctionType
ALU = mybir.AluOpType

B, S, D = 2, 8192, 1024
NCORES = 8
TPC = 2048
CH = 128
NCH = TPC // CH
ST = 256
NST = TPC // ST
CPS = ST // CH
DIN = 2048
NH = 32
HD = 64
NEC = 18
P1E = NEC * 128 + NH
KD = D // 128
DFF = 4096
NFC = DFF // 128
EPS = 1e-5

TAPS = {}
LIMIT_PHASE = 3


def _spike_thresholds():
    def count(y):
        y = np.float32(y)
        mem = np.float32(0.0)
        c = 0
        for _ in range(5):
            reset = np.float32(1.0) if mem > np.float32(1.0) else np.float32(0.0)
            mem = np.float32(np.float32(np.float32(0.9) * mem) + y) - reset
            c += 1 if mem > np.float32(1.0) else 0
        return c

    bs = []
    for j in range(1, 6):
        lo, hi = -4.0, 4.0
        for _ in range(60):
            mid = (lo + hi) / 2
            if count(mid) >= j:
                hi = mid
            else:
                lo = mid
        bs.append(float(np.float32(lo)))
    return bs


SPIKE_B = _spike_thresholds()


class Sched:

    ENG = ("pe", "dv", "ac", "po", "sp")

    def __init__(self):
        self.ops = {e: [] for e in self.ENG}
        self.counts = {}
        self.waited = {e: {} for e in self.ENG}

    def sem(self, name):
        self.counts.setdefault(name, 0)
        return name

    def emit(self, eng, fn, waits=(), inc=None, amt=1):
        pruned = []
        for w in waits:
            if w is None:
                continue
            sm, v = w
            assert v <= self.counts[sm], f"wait on future {sm}:{v}>{self.counts[sm]}"
            if v > self.waited[eng].get(sm, 0):
                self.waited[eng][sm] = v
                pruned.append((sm, v))
        self.ops[eng].append((fn, pruned, inc, amt))
        if inc is not None:
            self.counts[inc] += amt
            return (inc, self.counts[inc])
        return None

    def tok(self, sem):
        return (sem, self.counts[sem])


def build_nc():
    nc = bass.Bass("TRN2", target_bir_lowering=False, debug=False)
    s = Sched()
    oes = ExitStack()

    xs = nc.dram_tensor("xs", [TPC, D], F32, kind="ExternalInput").ap()
    xprev = nc.dram_tensor("xprev", [3, D], F32, kind="ExternalInput").ap()
    w_p1 = nc.dram_tensor("w_p1", [D, P1E], F32, kind="ExternalInput").ap()
    w_z = nc.dram_tensor("w_z", [D, DIN], F32, kind="ExternalInput").ap()
    w_out = nc.dram_tensor("w_out", [DIN, D], F32, kind="ExternalInput").ap()
    w_fc = nc.dram_tensor("w_fc", [D, DFF], F32, kind="ExternalInput").ap()
    w_pr = nc.dram_tensor("w_pr", [DFF, D], F32, kind="ExternalInput").ap()
    cw_in = nc.dram_tensor("cw", [128, NEC * 4], F32, kind="ExternalInput").ap()
    cb_in = nc.dram_tensor("cb", [128, NEC], F32, kind="ExternalInput").ap()
    dtb_in = nc.dram_tensor("dtb", [NH, 1], F32, kind="ExternalInput").ap()
    alog_in = nc.dram_tensor("alog", [NH, 1], F32, kind="ExternalInput").ap()
    db_in = nc.dram_tensor("dvec", [128, NH], F32, kind="ExternalInput").ap()
    nw_in = nc.dram_tensor("nw", [128, 16], F32, kind="ExternalInput").ap()
    sel_in = nc.dram_tensor("sel", [128, 8], F32, kind="ExternalInput").ap()
    selm_in = nc.dram_tensor("selm", [128, 8], F32, kind="ExternalInput").ap()
    out = nc.dram_tensor("out", [TPC, D], F32, kind="ExternalOutput").ap()

    spk = "ExternalOutput" if "spills" in TAPS else "Internal"
    y_sp = nc.dram_tensor("y_sp", [NCH, 128, DIN], F32, kind=spk).ap()
    xm_sp = nc.dram_tensor("xm_sp", [NCH, 128, D], F32, kind=spk).ap()
    mout = nc.dram_tensor("mout", [128, DIN + NH], F32,
                          kind="ExternalOutput" if "mout" in TAPS else "Internal").ap()
    mg = nc.dram_tensor("mg", [4 * 128, DIN + NH], F32, kind="Internal").ap()

    tapd = {}
    for name, shape in TAPS.items():
        if name in ("spills", "mout"):
            continue
        tapd[name] = nc.dram_tensor("tap_" + name, list(shape), F32,
                                    kind="ExternalOutput").ap()

    PE = s.sem("pe"); DV = s.sem("dv"); AC = s.sem("ac"); PO = s.sem("po")
    DSP = s.sem("dsp"); DPO = s.sem("dpo"); CC = s.sem("cc")
    DX = [s.sem("dx0"), s.sem("dx1")]
    DY = [s.sem("dy0"), s.sem("dy1")]
    DZ = [s.sem("dz0"), s.sem("dz1")]

    def tap(name, src_fn, waits=(), q="sp"):
        if name in tapd:
            s.emit(q, lambda e, n=name: e.dma_start(out=tapd[n], in_=src_fn()),
                   waits=waits, inc=DSP if q == "sp" else DPO, amt=16)

    def sb(es, name, shape, dt):
        return es.enter_context(nc.sbuf_tensor(name, shape, dt))

    identb = sb(oes, "identb", [128, 128], BF16)
    identf = sb(oes, "identf", [128, 128], F32)
    stepb = sb(oes, "stepb", [128, 128], BF16)
    cmaskb = sb(oes, "cmaskb", [128, 128], BF16)
    e127 = sb(oes, "e127", [128, 128], F32)
    zero32 = sb(oes, "zero32", [128, 32], F32)
    onescol = sb(oes, "onescol", [32, 1], F32)
    smallcol = sb(oes, "smallcol", [NH, 4], F32)
    db32 = sb(oes, "db32", [128, NH], F32)
    selb = sb(oes, "selb", [128, 8], F32)
    selb1m = sb(oes, "selb1m", [128, 8], F32)
    cw_sb = sb(oes, "cw_sb", [128, NEC * 4], F32)
    cb_sb = sb(oes, "cb_sb", [128, NEC], F32)
    nw_sb = sb(oes, "nw_sb", [128, 16], F32)
    stat = sb(oes, "stat", [128, 40], F32)
    carry = sb(oes, "carry", [32, 1], F32)
    eps_t = sb(oes, "eps_t", [128, 1], F32)
    p12 = ExitStack()
    cadt = sb(p12, "cadt", [128, NCH * 96], F32)
    c_all = sb(p12, "c_all", [128, NCH * 128], BF16)
    psA = oes.enter_context(nc.psum_tensor("psA", [128, 512], F32))
    psB = oes.enter_context(nc.psum_tensor("psB", [128, 512], F32))
    psY1 = oes.enter_context(nc.psum_tensor("psY1", [128, 512], F32))
    psY2 = oes.enter_context(nc.psum_tensor("psY2", [128, 512], F32))
    psM = oes.enter_context(nc.psum_tensor("psM", [128, 512], F32))
    psT = oes.enter_context(nc.psum_tensor("psT", [128, 2048], BF16))
    psT2 = oes.enter_context(nc.psum_tensor("psT2", [128, 1024], BF16))
    sems = {}
    for nm in ("pe", "dv", "ac", "po", "dsp", "dpo", "cc", "dx0", "dx1",
               "dy0", "dy1", "dz0", "dz1"):
        sems[nm] = oes.enter_context(nc.semaphore(nm))

    cadt_v = cadt.rearrange("p (c w) -> p c w", w=96)
    call_v = c_all.rearrange("p (c t) -> p c t", t=128)

    rotst = {"i": 0, "tok": [None, None]}

    def rot_ps():
        i = rotst["i"]
        rotst["i"] ^= 1
        return (psA, psB)[i], rotst["tok"][i], i

    def rot_done(i, token):
        rotst["tok"][i] = token

    psT_g = [None]
    psT2_g = [None]

    s.emit("po", lambda e: e.memset(identb[:], 0.0))
    s.emit("po", lambda e: e.affine_select(out=identb[:], in_=identb[:],
           compare_op=ALU.not_equal, fill=1.0, base=0,
           pattern=[[-1, 128]], channel_multiplier=1))
    s.emit("po", lambda e: e.memset(identf[:], 0.0))
    s.emit("po", lambda e: e.affine_select(out=identf[:], in_=identf[:],
           compare_op=ALU.not_equal, fill=1.0, base=0,
           pattern=[[-1, 128]], channel_multiplier=1))
    s.emit("po", lambda e: e.memset(stepb[:], 1.0))
    s.emit("po", lambda e: e.affine_select(out=stepb[:], in_=stepb[:],
           compare_op=ALU.is_gt, fill=0.0, base=0,
           pattern=[[-1, 128]], channel_multiplier=1))
    s.emit("po", lambda e: e.memset(cmaskb[:], 1.0))
    s.emit("po", lambda e: e.affine_select(out=cmaskb[:], in_=cmaskb[:],
           compare_op=ALU.is_ge, fill=0.0, base=0,
           pattern=[[1, 128]], channel_multiplier=-1))
    s.emit("po", lambda e: e.memset(e127[:], 0.0))
    s.emit("po", lambda e: e.affine_select(out=e127[:], in_=e127[:],
           compare_op=ALU.not_equal, fill=1.0, base=-127,
           pattern=[[0, 128]], channel_multiplier=1))
    s.emit("po", lambda e: e.memset(zero32[:], 0.0))
    s.emit("po", lambda e: e.memset(onescol[:], 1.0))
    s.emit("po", lambda e: e.memset(carry[:], 0.0))
    s.emit("po", lambda e: e.memset(eps_t[:], EPS), inc=PO)
    CPO = s.tok(PO)

    s.emit("sp", lambda e: e.dma_start(out=cw_sb[:], in_=cw_in), inc=DSP, amt=16)
    s.emit("sp", lambda e: e.dma_start(out=cb_sb[:], in_=cb_in), inc=DSP, amt=16)
    s.emit("sp", lambda e: e.dma_start(out=nw_sb[:], in_=nw_in), inc=DSP, amt=16)
    s.emit("sp", lambda e: e.dma_start(out=db32[:], in_=db_in), inc=DSP, amt=16)
    s.emit("sp", lambda e: e.dma_start(out=selb[:], in_=sel_in), inc=DSP, amt=16)
    s.emit("sp", lambda e: e.dma_start(out=selb1m[:], in_=selm_in), inc=DSP, amt=16)
    s.emit("sp", lambda e: e.dma_start(out=smallcol[:, 0:1], in_=dtb_in,
           allow_slow_non_contiguous=True), inc=DSP, amt=16)
    s.emit("sp", lambda e: e.dma_start(out=smallcol[:, 1:2], in_=alog_in,
           allow_slow_non_contiguous=True), inc=DSP, amt=16)
    LD0 = s.tok(DSP)
    t = s.emit("ac", lambda e: e.activation(smallcol[:, 2:3],
               smallcol[:, 1:2], AF.Exp), waits=[LD0], inc=AC)
    A_TOK = s.emit("dv", lambda e: e.tensor_scalar(out=smallcol[:, 2:3],
                   in0=smallcol[:, 2:3], scalar1=-1.0, scalar2=None,
                   op0=ALU.mult), waits=[t], inc=DV)
    ln_prev = {}

    def ln_chunk(xin_ap, dst, par, waits):
        c0 = 16 * par
        nf = xin_ap.shape[-1]
        prev = ln_prev.get(par)
        wl = list(waits) + ([prev[0]] if prev else [])
        h = nf // 2
        t1 = s.emit("dv", lambda e: e.bn_stats(stat[:xin_ap.shape[0], c0:c0 + 6],
                    xin_ap[:, 0:h]), waits=wl, inc=DV)
        t1 = s.emit("dv", lambda e: e.bn_stats(
                    stat[:xin_ap.shape[0], c0 + 6:c0 + 12], xin_ap[:, h:nf]),
                    waits=[t1], inc=DV)
        t2 = s.emit("dv", lambda e: e.bn_aggr(
                    stat[:xin_ap.shape[0], c0 + 12:c0 + 14],
                    stat[:xin_ap.shape[0], c0:c0 + 12]), waits=[t1], inc=DV)
        t3 = s.emit("ac", lambda e: e.activation(
                    stat[:xin_ap.shape[0], c0 + 13:c0 + 14],
                    stat[:xin_ap.shape[0], c0 + 13:c0 + 14], AF.Ln,
                    bias=eps_t[:xin_ap.shape[0], 0:1]),
                    waits=[t2, CPO], inc=AC)
        t4 = s.emit("ac", lambda e: e.activation(
                    stat[:xin_ap.shape[0], c0 + 13:c0 + 14],
                    stat[:xin_ap.shape[0], c0 + 13:c0 + 14], AF.Exp,
                    scale=-0.5), waits=[t3], inc=AC)
        t5 = s.emit("dv", lambda e: e.scalar_tensor_tensor(
            out=stat[:xin_ap.shape[0], c0 + 12:c0 + 13],
            in0=stat[:xin_ap.shape[0], c0 + 12:c0 + 13], scalar=-1.0,
            in1=stat[:xin_ap.shape[0], c0 + 13:c0 + 14], op0=ALU.mult,
            op1=ALU.mult), waits=[t4], inc=DV)
        t6 = s.emit("dv", lambda e: e.tensor_scalar(
            out=dst, in0=xin_ap,
            scalar1=stat[:xin_ap.shape[0], c0 + 13:c0 + 14],
            scalar2=stat[:xin_ap.shape[0], c0 + 12:c0 + 13],
            op0=ALU.mult, op1=ALU.add), waits=[t5], inc=DV)
        ln_prev[par] = (t4, t6)
        return t6

    G = {}

    def g(*key):
        return G.get(key)

    p1 = ExitStack()
    convd = sb(p1, "convd", [128, NEC * 4 * 128], BF16)
    Hst = sb(p1, "H", [128, DIN], F32)
    hb0 = sb(p1, "hb0", [128, DIN], BF16)
    hb1 = sb(p1, "hb1", [128, DIN], BF16)
    s_ca = sb(p1, "s_ca", [32, ST], F32)
    s_dt = sb(p1, "s_dt", [32, ST], F32)
    s_dta = sb(p1, "s_dta", [32, ST], F32)
    wp1 = sb(p1, "wp1", [128, KD * P1E], BF16)
    xin0 = sb(p1, "xin0", [128, D], F32)
    xin1 = sb(p1, "xin1", [128, D], F32)
    xln0 = sb(p1, "xln0", [128, D], BF16)
    xln1 = sb(p1, "xln1", [128, D], BF16)
    xpl = sb(p1, "xpl", [3, D], BF16)
    xlf0 = sb(p1, "xlf0", [128, KD * 259], BF16)
    xlf1 = sb(p1, "xlf1", [128, KD * 259], BF16)
    xbc = sb(p1, "xbc", [128, NEC * 259], BF16)
    xc = sb(p1, "xc", [128, 16 * ST], BF16)
    b_bf = sb(p1, "b_bf", [128, ST], BF16)
    b_tma = sb(p1, "b_tma", [128, 128], BF16)
    b_tmb = sb(p1, "b_tmb", [128, 128], BF16)
    sp_scr = sb(p1, "sp_scr", [32, ST], F32)
    sp_scr2 = sb(p1, "sp_scr2", [32, ST], F32)
    sc32a = sb(p1, "sc32a", [128, 96], F32)
    sc32b = sb(p1, "sc32b", [128, 96], F32)
    eo_bfa = sb(p1, "eo_bfa", [128, NH], BF16)
    eo_bfb = sb(p1, "eo_bfb", [128, NH], BF16)
    erc_bfa = sb(p1, "erc_bfa", [128, NH], BF16)
    erc_bfb = sb(p1, "erc_bfb", [128, NH], BF16)
    gama = sb(p1, "gama", [128, NH], F32)
    gamb = sb(p1, "gamb", [128, NH], F32)
    dt_bfa = sb(p1, "dt_bfa", [128, NH], BF16)
    dt_bfb = sb(p1, "dt_bfb", [128, NH], BF16)
    dta_bfa = sb(p1, "dta_bfa", [128, NH], BF16)
    dta_bfb = sb(p1, "dta_bfb", [128, NH], BF16)
    rhsT = sb(p1, "rhsT", [128, 4096], BF16)
    expd = sb(p1, "expd", [128, 4096], BF16)
    ghat0 = sb(p1, "ghat0", [128, 4096], BF16)
    ghat1 = sb(p1, "ghat1", [128, 4096], BF16)
    gm_bfa = sb(p1, "gm_bfa", [128, 128], BF16)
    gm_bfb = sb(p1, "gm_bfb", [128, 128], BF16)
    xtm0 = sb(p1, "xtm0", [128, DIN], BF16)
    xtm1 = sb(p1, "xtm1", [128, DIN], BF16)
    xda = sb(p1, "xda", [128, DIN], BF16)
    xdb = sb(p1, "xdb", [128, DIN], BF16)
    xdza = sb(p1, "xdza", [128, DIN], BF16)
    xdzb = sb(p1, "xdzb", [128, DIN], BF16)
    ytm0 = sb(p1, "ytm0", [128, DIN], F32)
    ytm1 = sb(p1, "ytm1", [128, DIN], F32)
    yscr = sb(p1, "yscr", [128, 512], F32)
    hscr = sb(p1, "hscr", [128, 512], F32)
    atile = sb(p1, "atile", [128, NH], F32)

    hb = [hb0, hb1]
    convd_v = convd.rearrange("p (i t) -> p i t", t=128)
    s.emit("po", lambda e: e.memset(Hst[:], 0.0))
    s.emit("po", lambda e: e.memset(hb0[:], 0.0), inc=PO)
    P1Z = s.tok(PO)
    cd = None
    for i in range(NEC * 4):
        cd = s.emit("dv", lambda e, i=i: e.tensor_scalar(
            out=convd_v[:, i, :], in0=identb[:],
            scalar1=cw_sb[:, i:i + 1], scalar2=None, op0=ALU.mult),
            waits=[LD0, CPO], inc=DV)
    CONVD_TOK = cd
    wp1_v = wp1.rearrange("p (k e) -> p k e", k=KD)
    xlf_vs = [xlf0.rearrange("p (k t) -> p k t", k=KD),
              xlf1.rearrange("p (k t) -> p k t", k=KD)]
    xbc_v = xbc.rearrange("p (c t) -> p c t", t=259)
    xc_v = xc.rearrange("p (c t) -> p c t", t=ST)
    xins = [xin0, xin1]
    xlns = [xln0, xln1]
    xtms = [xtm0, xtm1]
    ytms = [ytm0, ytm1]
    ghats = [ghat0, ghat1]
    sc32s = [sc32a, sc32b]
    eo_bfs = [eo_bfa, eo_bfb]
    erc_bfs = [erc_bfa, erc_bfb]
    gams = [gama, gamb]
    dt_bfs = [dt_bfa, dt_bfb]
    dta_bfs = [dta_bfa, dta_bfb]
    gm_bfs = [gm_bfa, gm_bfb]
    b_tms = [b_tma, b_tmb]
    xds = [xda, xdb]
    xdzs = [xdza, xdzb]

    for k in range(KD):
        s.emit("po", lambda e, k=k: e.dma_start(
            out=wp1_v[:, k, :], in_=w_p1[k * 128:(k + 1) * 128, :]),
            waits=[CPO], inc=DPO, amt=16)
    WP1 = s.tok(DPO)

    s.emit("sp", lambda e: e.dma_start(out=xin0[0:3, :], in_=xprev),
           inc=DSP, amt=16)
    XPL = s.tok(DSP)
    tp6 = ln_chunk(xin0[0:3, :], xpl[0:3, :], 0, [XPL])
    lastp = None
    for k in range(KD):
        lastp = s.emit("pe", lambda e, k=k: e.transpose(
            psT2[:, k * 128:k * 128 + 3], xpl[0:3, k * 128:(k + 1) * 128],
            identb[0:3, 0:3]), waits=[tp6] if k == 0 else (),
            inc=PE if k == KD - 1 else None)
    ev = s.emit("dv", lambda e: e.tensor_copy(
        xlf_vs[0][:, :, 0:3],
        psT2.rearrange("p (k t) -> p k t", t=128)[:, 0:KD, 0:3]),
        waits=[lastp], inc=DV)
    psT2_g[0] = ev
    XPREV_EV = ev
    G[("xin_r", -2)] = tp6

    def p1_front(st):
        ch0 = st * CPS
        xlf_v = xlf_vs[st % 2]
        evs = []
        for c2 in range(CPS):
            cg = ch0 + c2
            par = cg % 2
            xin, xln = xins[par], xlns[par]
            s.emit("sp", lambda e, xin=xin, cg=cg: e.dma_start(
                out=xin[:], in_=xs[cg * CH:(cg + 1) * CH, :]),
                waits=[g("xin_r", cg - 2)], inc=DX[par], amt=16)
            xt = s.tok(DX[par])
            lt = ln_chunk(xin[:], xln[:], par, [xt])
            G[("xin_r", cg)] = lt
            wl = [lt, g("xlf_r", st - 2), psT2_g[0], g("xln_r", cg - 2)]
            last = None
            for k in range(KD):
                last = s.emit("pe", lambda e, k=k, xln=xln: e.transpose(
                    psT2[:, k * 128:(k + 1) * 128],
                    xln[:, k * 128:(k + 1) * 128], identb[:]),
                    waits=wl if k == 0 else (),
                    inc=PE if k == KD - 1 else None)
            G[("xln_r", cg)] = last
            ev = s.emit("dv", lambda e, c2=c2, xlf_v=xlf_v: e.tensor_copy(
                xlf_v[:, :, 3 + c2 * CH:3 + (c2 + 1) * CH],
                psT2.rearrange("p (k t) -> p k t", t=128)),
                waits=[last], inc=DV)
            psT2_g[0] = ev
            evs.append(ev)
        if st == 0:
            evs.append(XPREV_EV)
        G[("p1front", st)] = evs

    p1_front(0)
    for st in range(NST):
        ch0 = st * CPS
        xlf_v = xlf_vs[st % 2]
        ln_evs = G[("p1front", st)]

        margin = None
        if st > 0:
            margin = s.emit("dv", lambda e: e.tensor_copy(
                xbc_v[:, :, 0:3], xbc_v[:, :, 256:259]),
                waits=[g("conv_r", st - 1)], inc=DV)

        co = 0 if st == 0 else 3
        N = 259 if st == 0 else 256
        doff = 0 if st == 0 else 3
        evac = []
        dt_info = None
        for ei in range(NEC + 1):
            ps, guard, ri = rot_ps()
            m = NH if ei == NEC else 128
            wl = ln_evs + [WP1, guard]
            last = None
            for k in range(KD):
                last = s.emit("pe",
                    lambda e, k=k, ei=ei, ps=ps, m=m, co=co, N=N,
                    xlf_v=xlf_v: e.matmul(
                        ps[0:m, 0:N], wp1_v[:, k, ei * 128:ei * 128 + m],
                        xlf_v[:, k, co:co + N],
                        start=(k == 0), stop=(k == KD - 1)),
                    waits=wl if k == 0 else (),
                    inc=PE if k == KD - 1 else None)
            if ei < NEC:
                ev = s.emit("ac", lambda e, ei=ei, ps=ps, doff=doff, N=N:
                            e.copy(xbc_v[:, ei, doff:doff + N], ps[:, 0:N]),
                            waits=[last, margin], inc=AC)
                evac.append(ev)
                rot_done(ri, ev)
            else:
                dt_info = (ps, ri, last)
        G[("xlf_r", st)] = s.tok(PE)
        if st + 1 < NST:
            p1_front(st + 1)

        ps, ri, mmtok = dt_info
        off = 3 if st == 0 else 0
        stc = slice(0, ST)
        t1 = s.emit("dv", lambda e, ps=ps, off=off: e.tensor_scalar(
            out=sp_scr[:, :], in0=ps[0:NH, off:off + ST],
            scalar1=smallcol[0:NH, 0:1], scalar2=None, op0=ALU.add),
            waits=[mmtok, A_TOK, g("spscr_r", st - 1)], inc=DV)
        rot_done(ri, t1)
        t2 = s.emit("ac", lambda e: e.activation(sp_scr2[:, :], sp_scr[:, :],
                    AF.Abs), waits=[t1], inc=AC)
        t3 = s.emit("ac", lambda e: e.activation(sp_scr2[:, :], sp_scr2[:, :],
                    AF.Exp, scale=-1.0), waits=[t2], inc=AC)
        t4 = s.emit("ac", lambda e: e.activation(sp_scr2[:, :], sp_scr2[:, :],
                    AF.Ln, bias=1.0), waits=[t3], inc=AC)
        t5 = s.emit("dv", lambda e: e.tensor_scalar(out=sp_scr[:, :],
                    in0=sp_scr[:, :], scalar1=0.0, scalar2=None,
                    op0=ALU.max), waits=[t4], inc=DV)
        t6 = s.emit("dv", lambda e, stc=stc: e.tensor_tensor(
            s_dt[:, stc], sp_scr[:, :], sp_scr2[:, :], ALU.add),
            waits=[t5, g("stk_r", st - 1)], inc=DV)
        G[("spscr_r", st)] = t6
        t7 = s.emit("dv", lambda e, stc=stc: e.tensor_scalar(
            out=s_dta[:, stc], in0=s_dt[:, stc],
            scalar1=smallcol[0:NH, 2:3], scalar2=None, op0=ALU.mult),
            waits=[t6], inc=DV)
        t8 = s.emit("dv", lambda e, stc=stc: e.tensor_tensor_scan(
            s_ca[:, stc], onescol[:, 0:1].broadcast_to([32, ST]),
            s_dta[:, stc], carry[:, 0:1], ALU.mult, ALU.add),
            waits=[t7, CPO], inc=DV)
        SCAN = s.emit("dv", lambda e: e.tensor_copy(
            carry[:, 0:1], s_ca[:, ST - 1:ST]), waits=[t8], inc=DV)

        conv_ev = []
        conv_last = None
        for ei in range(NEC):
            ps, guard, ri = rot_ps()
            wl = [evac[ei], CONVD_TOK, guard]
            last = None
            for tp in range(4):
                last = s.emit("pe", lambda e, ei=ei, tp=tp, ps=ps: e.matmul(
                    ps[:, 0:ST], convd_v[:, ei * 4 + tp, :],
                    xbc_v[:, ei, tp:tp + ST], start=(tp == 0), stop=(tp == 3)),
                    waits=wl if tp == 0 else (), inc=PE if tp == 3 else None)
            conv_last = last
            if ei < 16:
                dst = xc_v[:, ei, :]
                wl2 = [last, g("xc_r", st - 1)]
            elif ei == 16:
                dst = b_bf[:, :]
                wl2 = [last, g("bbf_r", st - 1)]
            else:
                dst = c_all[:, st * ST:(st + 1) * ST]
                wl2 = [last]
            ev = s.emit("ac", lambda e, ps=ps, dst=dst, ei=ei: e.activation(
                dst, ps[:, 0:ST], AF.Silu, bias=cb_sb[:, ei:ei + 1]),
                waits=wl2, inc=AC)
            conv_ev.append(ev)
            rot_done(ri, ev)
        G[("conv_r", st)] = conv_last

        def p1_prep(c2):
            cg = ch0 + c2
            par = cg % 2
            ccols = slice(c2 * CH, (c2 + 1) * CH)
            csl = ccols
            xtm = xtms[par]
            ghat = ghats[par]
            sc32 = sc32s[par]
            eo_bf, erc_bf, gam = eo_bfs[par], erc_bfs[par], gams[par]
            dt_bf, dta_bf = dt_bfs[par], dta_bfs[par]
            gm_bf, b_tm = gm_bfs[par], b_tms[par]
            xd, xdz = xds[par], xdzs[par]

            s.emit("pe", lambda e, ccols=ccols: e.transpose(
                psM[:, 0:32], s_ca[:, ccols], identf[0:32, 0:32]),
                waits=[SCAN, g("psm_ra", cg - 1)])
            s.emit("pe", lambda e, ccols=ccols: e.transpose(
                psM[:, 32:64], s_dt[:, ccols], identf[0:32, 0:32]))
            tT = s.emit("pe", lambda e, ccols=ccols: e.transpose(
                psM[:, 64:96], s_dta[:, ccols], identf[0:32, 0:32]), inc=PE)
            evc = s.emit("ac", lambda e, cg=cg: e.copy(
                cadt_v[:, cg, :], psM[:, 0:96]), waits=[tT], inc=AC)
            G[("p_evc", cg)] = evc
            prev_rhs = cadt_v[:, cg - 1, 0:32] if cg > 0 else zero32[:, 0:32]
            sp1 = s.emit("pe", lambda e, prev_rhs=prev_rhs: e.matmul(
                psM[:, 96:128], e127[:], prev_rhs, start=True, stop=True),
                waits=[evc], inc=PE)
            sp2 = s.emit("pe", lambda e, cg=cg: e.matmul(
                psM[:, 128:160], e127[:], cadt_v[:, cg, 0:32],
                start=True, stop=True), waits=[evc], inc=PE)
            wl = [sp1, sp2]
            a1 = s.emit("dv", lambda e, cg=cg, sc32=sc32: e.tensor_tensor(
                sc32[:, 0:32], cadt_v[:, cg, 0:32], psM[:, 96:128],
                ALU.subtract), waits=wl, inc=DV)
            a2 = s.emit("dv", lambda e, cg=cg, sc32=sc32: e.tensor_tensor(
                sc32[:, 32:64], psM[:, 128:160], cadt_v[:, cg, 0:32],
                ALU.subtract), waits=wl, inc=DV)
            a3 = s.emit("dv", lambda e, sc32=sc32: e.tensor_tensor(
                sc32[:, 64:96], sc32[:, 0:32], sc32[:, 32:64], ALU.add),
                waits=[a1, a2], inc=DV)
            wl_e = [g("qend_dv", cg - 2), g("qend_po", cg - 2)]
            e1 = s.emit("ac", lambda e, eo_bf=eo_bf, sc32=sc32: e.activation(
                eo_bf[:], sc32[:, 0:32], AF.Exp), waits=[a1] + wl_e, inc=AC)
            e2 = s.emit("ac", lambda e, erc_bf=erc_bf, sc32=sc32: e.activation(
                erc_bf[:], sc32[:, 32:64], AF.Exp), waits=[a2] + wl_e, inc=AC)
            e3 = s.emit("ac", lambda e, gam=gam, sc32=sc32: e.activation(
                gam[:], sc32[:, 64:96], AF.Exp), waits=[a3] + wl_e, inc=AC)
            d1 = s.emit("ac", lambda e, cg=cg, dt_bf=dt_bf: e.copy(
                dt_bf[:], cadt_v[:, cg, 32:64]), waits=[evc], inc=AC)
            d2 = s.emit("ac", lambda e, cg=cg, dta_bf=dta_bf: e.copy(
                dta_bf[:], cadt_v[:, cg, 64:96]), waits=[evc], inc=AC)
            rb0 = s.emit("po", lambda e, dta_bf=dta_bf: e.tensor_tensor(
                rhsT.rearrange("p (h t) -> p h t", h=NH)[:, 0:16, :],
                dta_bf[:, 0:16, None].broadcast_to([128, 16, 128]),
                cmaskb[:, None, :].broadcast_to([128, 16, 128]),
                ALU.mult), waits=[d2, g("rhsT_r", cg - 1)], inc=PO)
            rb = s.emit("dv", lambda e, dta_bf=dta_bf: e.tensor_tensor(
                rhsT.rearrange("p (h t) -> p h t", h=NH)[:, 16:32, :],
                dta_bf[:, 16:32, None].broadcast_to([128, 16, 128]),
                cmaskb[:, None, :].broadcast_to([128, 16, 128]),
                ALU.mult), waits=[d2, g("rhsT_r", cg - 1)], inc=DV)
            gmm = s.emit("pe", lambda e, csl=csl, cg=cg: e.matmul(
                psM[:, 160:288], b_bf[:, csl], call_v[:, cg, :],
                start=True, stop=True),
                waits=[conv_ev[16], conv_ev[17], a1, a2], inc=PE)
            gmb = s.emit("dv", lambda e, gm_bf=gm_bf: e.tensor_tensor(
                gm_bf[:], psM[:, 160:288], cmaskb[:], ALU.mult),
                waits=[gmm], inc=DV)
            G[("psm_ra", cg)] = gmb
            if cg == NCH - 1:
                G[("la",)] = s.emit("ac", lambda e: e.activation(
                    atile[:], psM[:, 128:160], AF.Exp), waits=[a1, a2], inc=AC)
            gh_dv = None
            gh_po = None
            am_last = None
            for gi in range(8):
                ps, guard, ri = rot_ps()
                am = s.emit("pe", lambda e, gi=gi, ps=ps: e.matmul(
                    ps[:, 0:512], stepb[:], rhsT[:, gi * 512:(gi + 1) * 512],
                    start=True, stop=True),
                    waits=[rb0 if gi < 4 else rb, guard, CPO], inc=PE)
                am_last = am
                ex = s.emit("ac", lambda e, gi=gi, ps=ps: e.activation(
                    expd[:, gi * 512:(gi + 1) * 512], ps[:, 0:512], AF.Exp),
                    waits=[am, g("expd_r", cg - 1), g("expd_rp", cg - 1)],
                    inc=AC)
                rot_done(ri, ex)
                qeng = "dv" if gi % 2 == 0 else "po"
                gh = s.emit(qeng, lambda e, gi=gi, ghat=ghat, gm_bf=gm_bf:
                    e.tensor_tensor(
                    ghat.rearrange("p (h t) -> p h t", h=NH)[:, gi * 4:(gi + 1) * 4, :],
                    expd.rearrange("p (h t) -> p h t", h=NH)[:, gi * 4:(gi + 1) * 4, :],
                    gm_bf[:, None, :].broadcast_to([128, 4, 128]),
                    ALU.mult), waits=[ex, gmb, g("ghat_r", cg - 2)],
                    inc=DV if gi % 2 == 0 else PO)
                if gi % 2 == 0:
                    gh_dv = gh
                else:
                    gh_po = gh
            G[("rhsT_r", cg)] = am_last
            G[("expd_r", cg)] = gh_dv
            G[("expd_rp", cg)] = gh_po
            bt = s.emit("pe", lambda e, csl=csl: e.transpose(
                psT[:, 0:128], b_bf[:, csl], identb[:]),
                waits=[conv_ev[16], psT_g[0]], inc=PE)
            if c2 == CPS - 1:
                G[("bbf_r", st)] = bt
            bte = s.emit("dv", lambda e, b_tm=b_tm: e.tensor_copy(
                b_tm[:], psT[:, 0:128]),
                waits=[bt, g("xdz_r", cg - 2)], inc=DV)
            psT_g[0] = bte
            wl = [conv_ev[15], bte, g("xtm_r", cg - 2)]
            last = None
            for i in range(16):
                last = s.emit("pe", lambda e, i=i, csl=csl: e.transpose(
                    psT[:, i * 128:(i + 1) * 128], xc_v[:, i, csl], identb[:]),
                    waits=wl if i == 0 else (), inc=PE if i == 15 else None)
            if c2 == CPS - 1:
                G[("xc_r", st)] = last
            xte = s.emit("ac", lambda e, xtm=xtm: e.copy(xtm[:], psT[:]),
                         waits=[last], inc=AC)
            psT_g[0] = xte
            xdt = s.emit("dv", lambda e, xtm=xtm, xd=xd, dt_bf=dt_bf:
                e.tensor_tensor(
                xd.rearrange("p (h q) -> p h q", h=NH),
                xtm.rearrange("p (h q) -> p h q", h=NH),
                dt_bf[:, :, None].broadcast_to([128, NH, HD]),
                ALU.mult), waits=[xte, d1, g("xd_r", cg - 2)], inc=DV)
            xzt = s.emit("dv", lambda e, xd=xd, xdz=xdz, erc_bf=erc_bf:
                e.tensor_tensor(
                xdz.rearrange("p (h q) -> p h q", h=NH),
                xd.rearrange("p (h q) -> p h q", h=NH),
                erc_bf[:, :, None].broadcast_to([128, NH, HD]),
                ALU.mult), waits=[xdt, e2, g("dxbf_r", cg - 2)], inc=DV)
            G[("p_tok", cg)] = (gh_dv, gh_po, xdt, xzt, e1, e3, bte)
            if c2 == CPS - 1:
                G[("stk_r", st)] = tT

        def p1_quarters(c2):
            cg = ch0 + c2
            par = cg % 2
            xtm, ytm = xtms[par], ytms[par]
            ghat = ghats[par]
            hbr, hbw = hb[par], hb[1 - par]
            eo_bf, gam = eo_bfs[par], gams[par]
            b_tm = b_tms[par]
            xd, xdz = xds[par], xdzs[par]
            gh_dv, gh_po, xdt, xzt, e1, e3, bte = G[("p_tok", cg)]
            for q in range(4):
                qs = slice(q * 512, (q + 1) * 512)
                pa, pb = (psY1, psY2) if q % 2 == 0 else (psA, psB)
                wl1 = [gh_dv, gh_po, xdt, g("q_ya", cg * 4 + q - 2)]
                if q % 2 == 1:
                    wl1 += [rotst["tok"][0]]
                l1 = None
                for j in range(8):
                    h = q * 8 + j
                    l1 = s.emit("pe", lambda e, h=h, j=j, ghat=ghat, xd=xd,
                        pa=pa: e.matmul(
                        pa[:, j * 64:(j + 1) * 64],
                        ghat[:, h * 128:(h + 1) * 128],
                        xd[:, h * HD:(h + 1) * HD], start=True, stop=True),
                        waits=wl1 if j == 0 else (), inc=PE if j == 7 else None)
                G[("ghat_r", cg)] = l1
                G[("xd_r", cg)] = l1
                wl2 = [g("q_yb", cg * 4 + q - 2), g("hb_w", cg - 1, q), P1Z]
                if q % 2 == 1:
                    wl2 += [rotst["tok"][1]]
                l2 = None
                for j in range(8):
                    h = q * 8 + j
                    l2 = s.emit("pe", lambda e, h=h, j=j, hbr=hbr, cg=cg,
                        pb=pb: e.matmul(
                        pb[:, j * 64:(j + 1) * 64], call_v[:, cg, :],
                        hbr[:, h * HD:(h + 1) * HD], start=True, stop=True),
                        waits=wl2 if j == 0 else (), inc=PE if j == 7 else None)
                G[("hb_r", cg, q)] = l2
                l3 = None
                s_extra = []
                if q == 0:
                    other = ch0 + 1 if c2 == 0 else cg
                    s_extra = [g("psm_ra", other), g("p_evc", other),
                               g("la")]
                for j in range(8):
                    h = q * 8 + j
                    l3 = s.emit("pe", lambda e, h=h, j=j, b_tm=b_tm,
                        xdz=xdz: e.matmul(
                        psM[:, j * 64:(j + 1) * 64], b_tm[:],
                        xdz[:, h * HD:(h + 1) * HD], start=True, stop=True),
                        waits=[bte, xzt, g("q_ha", cg * 4 + q - 1)] + s_extra
                        if j == 0 else (), inc=PE if j == 7 else None)
                G[("xdz_r", cg)] = l3
                yb = s.emit("dv", lambda e, q=q, eo_bf=eo_bf, pb=pb:
                    e.tensor_tensor(
                    yscr.rearrange("p (j w) -> p j w", j=8),
                    pb.rearrange("p (j w) -> p j w", j=8),
                    eo_bf[:, q * 8:(q + 1) * 8, None].broadcast_to([128, 8, HD]),
                    ALU.mult), waits=[l2, e1, g("ytm_sp", cg - 2)], inc=DV)
                G[("q_yb", cg * 4 + q)] = yb
                ya = s.emit("dv", lambda e, ytm=ytm, qs=qs, pa=pa:
                    e.tensor_tensor(
                    ytm[:, qs], pa[:, 0:512], yscr[:, 0:512], ALU.add),
                    waits=[yb, l1], inc=DV)
                G[("q_ya", cg * 4 + q)] = ya
                hm = s.emit("po", lambda e, q=q, qs=qs, gam=gam: e.tensor_tensor(
                    hscr.rearrange("p (j w) -> p j w", j=8),
                    Hst[:, qs].rearrange("p (j w) -> p j w", j=8),
                    gam[:, q * 8:(q + 1) * 8, None].broadcast_to([128, 8, HD]),
                    ALU.mult), waits=[e3, g("hadd_prev"), P1Z], inc=PO)
                ha = s.emit("dv", lambda e, qs=qs: e.tensor_tensor(
                    Hst[:, qs], hscr[:, 0:512], psM[:, 0:512], ALU.add),
                    waits=[hm, l3], inc=DV)
                G[("q_ha", cg * 4 + q)] = ha
                G[("hadd_prev",)] = ha
                hw = s.emit("ac", lambda e, qs=qs, hbw=hbw: e.copy(
                    hbw[:, qs], Hst[:, qs]),
                    waits=[ha, g("hb_r", cg - 1, q)], inc=AC)
                G[("hb_w", cg, q)] = hw
                if cg == NCH - 1 and q == 3:
                    G[("h_final",)] = ha
            rotst["tok"][0] = G[("q_ya", cg * 4 + 3)]
            rotst["tok"][1] = G[("q_yb", cg * 4 + 3)]
            G[("psm_ra", cg)] = G[("q_ha", cg * 4 + 3)]
            G[("qend_dv", cg)] = s.tok(DV)
            dxt = s.emit("po", lambda e, xtm=xtm, xdz=xdz: e.tensor_tensor(
                xdz.rearrange("p (h w) -> p h w", h=NH),
                xtm.rearrange("p (h w) -> p h w", h=NH),
                db32[:, :, None].broadcast_to([128, NH, HD]), ALU.mult),
                waits=[LD0, g("xdz_r", cg)], inc=PO)
            G[("xtm_r", cg)] = dxt
            ydx = s.emit("po", lambda e, ytm=ytm, xdz=xdz: e.tensor_tensor(
                ytm[:], ytm[:], xdz[:], ALU.add),
                waits=[dxt, s.tok(DV)], inc=PO)
            G[("dxbf_r", cg)] = ydx
            G[("qend_po", cg)] = s.tok(PO)
            s.emit("sp", lambda e, ytm=ytm, cg=cg: e.dma_start(
                out=y_sp[cg], in_=ytm[:]), waits=[ydx], inc=DY[par], amt=16)
            G[("ytm_sp", cg)] = s.tok(DY[par])

        p1_prep(0)
        p1_prep(1)
        p1_quarters(0)
        p1_quarters(1)

    wz_v = wp1_v[:, :, 0:DIN]
    if LIMIT_PHASE >= 2:
        for k in range(KD):
            s.emit("po", lambda e, k=k: e.dma_start(
                out=wz_v[:, k, :], in_=w_z[k * 128:(k + 1) * 128, :]),
                waits=[g("xlf_r", NST - 1)], inc=DPO, amt=16)
    WZTOK = s.tok(DPO)
    s.emit("sp", lambda e: e.dma_start(out=mout[:, 0:DIN], in_=Hst[:]),
           waits=[g("h_final")], inc=DSP, amt=16)
    s.emit("sp", lambda e: e.dma_start(out=mout[:, DIN:DIN + NH],
           in_=atile[:]), waits=[g("la")], inc=DSP, amt=16)
    MOUT = s.tok(DSP)
    tap("sdt", lambda: s_dt[:], waits=[s.tok(DV)])
    tap("sca", lambda: s_ca[:], waits=[s.tok(DV)])
    tap("cadt", lambda: cadt[:], waits=[s.tok(DV)])
    p1.close()

    if LIMIT_PHASE >= 2:
        PH1 = [s.tok(sm) for sm in ("pe", "dv", "ac", "po", "dsp",
                                    "dx0", "dx1", "dy0", "dy1")]
        for eng in ("pe", "dv", "ac", "po", "sp"):
            s.emit(eng, lambda e: e.nop(), waits=PH1)
        ccd = s.emit("po", lambda e: e.collective_compute(
            "AllGather", ALU.bypass,
            replica_groups=[[0, 1, 2, 3], [4, 5, 6, 7]],
            ins=[mout.opt()], outs=[mg.opt()]), waits=[MOUT], inc=CC)

        p2 = ExitStack()
        DEAD_B = 18432 + 8192 + 4096 + 4096 + 3 * 1024
        mga = sb(p2, "mga", [128, 4 * (DIN + NH)], F32)
        hinb = sb(p2, "hinb", [128, DIN], BF16)
        used = 4 * (DIN + NH) * 4 + 4096
        assert used <= DEAD_B, used
        if DEAD_B - used:
            sb(p2, "p2pad0", [128, (DEAD_B - used) // 2], BF16)
        sb(p2, "p2pad1", [128, KD * P1E], BF16)
        wo = sb(p2, "wo", [128, 16 * D], BF16)
        hin = sb(p2, "hin", [128, DIN], F32)
        msel = sb(p2, "msel", [128, NH], F32)
        p2xin0 = sb(p2, "p2xin0", [128, D], F32)
        p2xin1 = sb(p2, "p2xin1", [128, D], F32)
        p2xln0 = sb(p2, "p2xln0", [128, D], BF16)
        p2xln1 = sb(p2, "p2xln1", [128, D], BF16)
        xlf2a = sb(p2, "xlf2a", [128, KD * 128], BF16)
        xlf2b = sb(p2, "xlf2b", [128, KD * 128], BF16)
        yld0 = sb(p2, "yld0", [128, DIN], F32)
        yld1 = sb(p2, "yld1", [128, DIN], F32)
        gtm0 = sb(p2, "gtm0", [128, DIN], F32)
        gtm1 = sb(p2, "gtm1", [128, DIN], F32)
        eog = sb(p2, "eog", [128, NH], BF16)
        yg = sb(p2, "yg", [128, DIN], F32)
        yscr2 = sb(p2, "yscr2", [128, 512], F32)
        yn = sb(p2, "yn", [128, DIN], BF16)
        ynf = sb(p2, "ynf", [128, 16 * 128], BF16)
        xmid0 = sb(p2, "xmid0", [128, D], F32)
        xmid1 = sb(p2, "xmid1", [128, D], F32)
        stat2 = sb(p2, "stat2", [128, 4], F32)

        wo_v = wo.rearrange("p (c d) -> p c d", c=16)
        ynf_v = ynf.rearrange("p (c t) -> p c t", c=16)
        xlf2_vs = [xlf2a.rearrange("p (k t) -> p k t", k=KD),
                   xlf2b.rearrange("p (k t) -> p k t", k=KD)]
        p2xins = [p2xin0, p2xin1]
        p2xlns = [p2xln0, p2xln1]
        ylds = [yld0, yld1]
        xmids = [xmid0, xmid1]
        mga_v = mga.rearrange("p (r c) -> p r c", r=4)
        hz = s.emit("po", lambda e: e.memset(hin[:], 0.0), inc=PO)

        for c in range(16):
            s.emit("po", lambda e, c=c: e.dma_start(
                out=wo_v[:, c, :], in_=w_out[c * 128:(c + 1) * 128, :]),
                inc=DPO, amt=16)
        W2 = s.tok(DPO)
        nwt = None
        for c in range(16):
            nwt = s.emit("dv", lambda e, c=c: e.tensor_scalar(
                out=wo_v[:, c, :], in0=wo_v[:, c, :],
                scalar1=nw_sb[:, c:c + 1], scalar2=None, op0=ALU.mult),
                waits=[W2, LD0], inc=DV)
        WOUT = nwt

        def p2_combine():
            s.emit("po", lambda e: e.dma_start(
                out=mga_v[:, :, :],
                in_=mg.rearrange("(r p) c -> p r c", p=128)),
                waits=[ccd], inc=DPO, amt=16)
            mjt = s.tok(DPO)
            prev = None
            for j in range(4):
                ms = s.emit("dv", lambda e, j=j: e.tensor_scalar(
                    out=msel[:], in0=mga_v[:, j, DIN:DIN + NH],
                    scalar1=selb[:, 2 * j:2 * j + 1],
                    scalar2=selb1m[:, 2 * j:2 * j + 1],
                    op0=ALU.mult, op1=ALU.add), waits=[mjt, LD0], inc=DV)
                hm2 = s.emit("dv", lambda e: e.tensor_tensor(
                    hin.rearrange("p (h w) -> p h w", h=NH),
                    hin.rearrange("p (h w) -> p h w", h=NH),
                    msel[:, :, None].broadcast_to([128, NH, HD]), ALU.mult),
                    waits=[ms, hz], inc=DV)
                prev = s.emit("dv", lambda e, j=j: e.scalar_tensor_tensor(
                    out=hin[:], in0=mga_v[:, j, 0:DIN],
                    scalar=selb[:, 2 * j + 1:2 * j + 2], in1=hin[:],
                    op0=ALU.mult, op1=ALU.add), waits=[hm2], inc=DV)
            hib = s.emit("dv", lambda e: e.tensor_copy(hinb[:], hin[:]),
                         waits=[prev], inc=DV)
            tap("hin", lambda: hin[:], waits=[prev])
            return hib

        def p2_front(cg):
            par = cg % 2
            xin, xln = p2xins[par], p2xlns[par]
            yld = ylds[par]
            s.emit("sp", lambda e, xin=xin, cg=cg: e.dma_start(
                out=xin[:], in_=xs[cg * CH:(cg + 1) * CH, :]),
                waits=[g("p2xin_r", cg - 2)], inc=DX[par], amt=16)
            s.emit("sp", lambda e, yld=yld, cg=cg: e.dma_start(
                out=yld[:], in_=y_sp[cg]),
                waits=[g("yld_r", cg - 2)], inc=DY[par], amt=16)
            G[("p2ylt", cg)] = s.tok(DY[par])
            xt = s.tok(DX[par])
            lt = ln_chunk(xin[:], xln[:], par, [xt])
            wl = [lt, psT2_g[0], g("xlf2_r", cg - 2), g("p2xln_r", cg - 2)]
            last = None
            for k in range(KD):
                last = s.emit("pe", lambda e, k=k, xln=xln: e.transpose(
                    psT2[:, k * 128:(k + 1) * 128],
                    xln[:, k * 128:(k + 1) * 128], identb[:]),
                    waits=wl if k == 0 else (), inc=PE if k == KD - 1 else None)
            G[("p2xln_r", cg)] = last
            ev = s.emit("dv", lambda e, cg=cg: e.tensor_copy(
                xlf2_vs[cg % 2][:, :, :],
                psT2.rearrange("p (k t) -> p k t", t=128)),
                waits=[last], inc=DV)
            psT2_g[0] = ev
            G[("p2ev", cg)] = ev

        def p2_z(cg):
            par = cg % 2
            xlf2_v = xlf2_vs[par]
            ev = G[("p2ev", cg)]
            gtm = [gtm0, gtm1][par]
            zev = []
            for gi in range(4):
                ps, guard, ri = rot_ps()
                lastz = None
                for k in range(KD):
                    lastz = s.emit("pe", lambda e, k=k, gi=gi, ps=ps,
                                   xlf2_v=xlf2_v: e.matmul(
                        ps[:, 0:512], xlf2_v[:, k, :],
                        wz_v[:, k, gi * 512:(gi + 1) * 512],
                        start=(k == 0), stop=(k == KD - 1)),
                        waits=[ev, WZTOK, guard] if k == 0 else (),
                        inc=PE if k == KD - 1 else None)
                zv = s.emit("ac", lambda e, gi=gi, ps=ps, gtm=gtm: e.activation(
                    gtm[:, gi * 512:(gi + 1) * 512], ps[:, 0:512], AF.Silu),
                    waits=[lastz, g("gtm_r", cg - 2)], inc=AC)
                zev.append(zv)
                rot_done(ri, zv)
            G[("xlf2_r", cg)] = s.tok(PE)
            G[("p2zev", cg)] = zev

        DYPS = (psY1, psY2, psM, psY2)

        def dy_q(cg, q, wl):
            psq = DYPS[q]
            l2 = None
            for j in range(8):
                h = q * 8 + j
                l2 = s.emit("pe", lambda e, h=h, j=j, psq=psq, cg=cg:
                    e.matmul(psq[:, j * 64:(j + 1) * 64], call_v[:, cg, :],
                    hinb[:, h * HD:(h + 1) * HD], start=True, stop=True),
                    waits=wl if j == 0 else (), inc=PE if j == 7 else None)
            G[("dy", cg, q)] = l2

        def dy_pre(cg, hib):
            G[("eg", cg)] = s.emit("ac", lambda e, cg=cg: e.activation(
                eog[:], cadt_v[:, cg, 0:32], AF.Exp),
                waits=[g("eog_r", cg - 1)], inc=AC)
            dy_q(cg, 0, [hib, g("t1", cg - 1, 0)])
            dy_q(cg, 1, [hib, g("t1", cg - 1, 3)])
            dy_q(cg, 2, [hib, g("t1", cg - 1, 2)])

        p2_front(0)
        p2_z(0)
        p2_front(1)
        p2_z(1)
        hib = p2_combine()
        dy_pre(0, hib)
        for cg in range(NCH):
            par = cg % 2
            xin = p2xins[par]
            yld = ylds[par]
            xmid = xmids[par]
            gtm = [gtm0, gtm1][par]
            ev = G[("p2ev", cg)]
            ylt = G[("p2ylt", cg)]
            zev = G[("p2zev", cg)]
            if cg + 1 < NCH and ("p2zev", cg + 1) not in G:
                p2_z(cg + 1)
            eg = G[("eg", cg)]
            ygt = None
            sql = None
            for q in range(4):
                qs = slice(q * 512, (q + 1) * 512)
                psq = DYPS[q]
                t1q = s.emit("dv", lambda e, q=q, psq=psq: e.tensor_tensor(
                    yscr2.rearrange("p (j w) -> p j w", j=8),
                    psq.rearrange("p (j w) -> p j w", j=8),
                    eog[:, q * 8:(q + 1) * 8, None].broadcast_to([128, 8, HD]),
                    ALU.mult), waits=[G[("dy", cg, q)], eg], inc=DV)
                G[("t1", cg, q)] = t1q
                t2q = s.emit("dv", lambda e, qs=qs, yld=yld: e.tensor_tensor(
                    yscr2[:, 0:512], yscr2[:, 0:512], yld[:, qs], ALU.add),
                    waits=[t1q, ylt], inc=DV)
                ygt = s.emit("dv", lambda e, qs=qs, gtm=gtm: e.tensor_tensor(
                    yg[:, qs], yscr2[:, 0:512], gtm[:, qs], ALU.mult),
                    waits=[t2q, zev[q], g("yg_r", cg - 1)], inc=DV)
                sql = s.emit("ac", lambda e, q=q, qs=qs: e.activation(
                    yn[:, qs], yg[:, qs], AF.Square,
                    accum_out=stat[:, 34 + q:35 + q]),
                    waits=[ygt, g("yn_r", cg - 1)] if q == 0 else [ygt],
                    inc=AC)
                if q == 1:
                    dy_q(cg, 3, [G[("t1", cg, 1)]])
            G[("yld_r", cg)] = ygt
            G[("eog_r", cg)] = G[("t1", cg, 3)]
            G[("gtm_r", cg)] = ygt
            if cg + 1 < NCH:
                dy_pre(cg + 1, hib)
            acc = s.emit("ac", lambda e: e.activation(
                stat2[:, 0:4], stat[:, 34:38], AF.Copy,
                accum_out=stat[:, 32:33]), waits=[sql], inc=AC)
            r2 = s.emit("ac", lambda e: e.activation(stat[:, 33:34],
                        stat[:, 32:33], AF.Ln, scale=1.0 / DIN,
                        bias=eps_t[:, 0:1]),
                        waits=[acc, g("rms_r", cg - 1)], inc=AC)
            r3 = s.emit("ac", lambda e: e.activation(stat[:, 33:34],
                        stat[:, 33:34], AF.Exp, scale=-0.5), waits=[r2], inc=AC)
            ynt = s.emit("dv", lambda e: e.tensor_scalar(
                out=yn[:], in0=yg[:], scalar1=stat[:, 33:34], scalar2=None,
                op0=ALU.mult), waits=[r3, g("yn_r", cg - 1)], inc=DV)
            G[("rms_r", cg)] = ynt
            G[("yg_r", cg)] = ynt
            wl = [ynt, psT_g[0]]
            last = None
            for i in range(16):
                last = s.emit("pe", lambda e, i=i: e.transpose(
                    psT[:, i * 128:(i + 1) * 128], yn[:, i * 128:(i + 1) * 128],
                    identb[:]), waits=wl if i == 0 else (),
                    inc=PE if i == 15 else None)
            G[("yn_r", cg)] = last
            evy = s.emit("dv", lambda e: e.tensor_copy(ynf[:], psT[:]),
                         waits=[last], inc=DV)
            psT_g[0] = evy
            xr = None
            for gi in range(2):
                ps, guard, ri = rot_ps()
                lasto = None
                for c in range(16):
                    lasto = s.emit("pe", lambda e, c=c, gi=gi, ps=ps: e.matmul(
                        ps[:, 0:512], ynf_v[:, c, :],
                        wo_v[:, c, gi * 512:(gi + 1) * 512],
                        start=(c == 0), stop=(c == 15)),
                        waits=[evy, WOUT, guard] if c == 0 else (),
                        inc=PE if c == 15 else None)
                xr = s.emit("dv", lambda e, gi=gi, ps=ps, xmid=xmid, xin=xin:
                    e.tensor_tensor(xmid[:, gi * 512:(gi + 1) * 512],
                    ps[:, 0:512], xin[:, gi * 512:(gi + 1) * 512], ALU.add),
                    waits=[lasto, g("xmid_sp", cg - 2)], inc=DV)
                rot_done(ri, xr)
            G[("p2xin_r", cg)] = xr
            s.emit("sp", lambda e, xmid=xmid, cg=cg: e.dma_start(
                out=xm_sp[cg], in_=xmid[:]), waits=[xr], inc=DZ[par], amt=16)
            G[("xmid_sp", cg)] = s.tok(DZ[par])
            if cg + 2 < NCH:
                p2_front(cg + 2)
        tap("gtm", lambda: gtm1[:], waits=[s.tok(AC)])
        tap("yg", lambda: yg[:], waits=[s.tok(DV)])
        p2.close()
        p12.close()

        if LIMIT_PHASE >= 3:
            PH2 = [s.tok(sm) for sm in ("pe", "dv", "ac", "po", "dsp",
                                        "dx0", "dx1", "dy0", "dy1",
                                        "dz0", "dz1")]
            for eng in ("pe", "dv", "ac", "po", "sp"):
                s.emit(eng, lambda e: e.nop(), waits=PH2)
            p3 = ExitStack()
            wfc = sb(p3, "wfc", [128, KD * DFF], BF16)
            wpr = sb(p3, "wpr", [128, NFC * D], BF16)
            xm0 = sb(p3, "xm0", [128, D], F32)
            xm1 = sb(p3, "xm1", [128, D], F32)
            xm2 = sb(p3, "xm2", [128, D], F32)
            xm3 = sb(p3, "xm3", [128, D], F32)
            xl20 = sb(p3, "xl20", [128, D], BF16)
            xl21 = sb(p3, "xl21", [128, D], BF16)
            xlf3a = sb(p3, "xlf3a", [128, KD * ST], BF16)
            xlf3b = sb(p3, "xlf3b", [128, KD * ST], BF16)
            rlb = sb(p3, "rlb", [128, ST], BF16)
            hbuf = sb(p3, "hbuf", [128, NFC * ST], BF16)
            y2 = sb(p3, "y2", [128, D], F32)
            spka = sb(p3, "spka", [128, D], BF16)
            spkb = sb(p3, "spkb", [128, D], BF16)
            outb0 = sb(p3, "outb0", [128, D], F32)
            outb1 = sb(p3, "outb1", [128, D], F32)

            wfc_v = wfc.rearrange("p (k f) -> p k f", k=KD)
            wpr_v = wpr.rearrange("p (f d) -> p f d", f=NFC)
            xlf3_vs = [xlf3a.rearrange("p (k t) -> p k t", k=KD),
                       xlf3b.rearrange("p (k t) -> p k t", k=KD)]
            hbuf_v = hbuf.rearrange("p (f t) -> p f t", f=NFC)
            xms = [xm0, xm1, xm2, xm3]
            xl2s = [xl20, xl21]
            outbs = [outb0, outb1]

            for k in range(KD):
                s.emit("po", lambda e, k=k: e.dma_start(
                    out=wfc_v[:, k, :], in_=w_fc[k * 128:(k + 1) * 128, :]),
                    inc=DPO, amt=16)
            WFC = s.tok(DPO)
            for f in range(NFC):
                s.emit("po", lambda e, f=f: e.dma_start(
                    out=wpr_v[:, f, :], in_=w_pr[f * 128:(f + 1) * 128, :]),
                    inc=DPO, amt=16)
            WPR = s.tok(DPO)

            def p3_front(st):
                ch0 = st * CPS
                xlf3_v = xlf3_vs[st % 2]
                evs = []
                for c2 in range(CPS):
                    cg = ch0 + c2
                    par = cg % 2
                    xm, xl2 = xms[cg % 4], xl2s[par]
                    s.emit("sp", lambda e, xm=xm, cg=cg: e.dma_start(
                        out=xm[:], in_=xm_sp[cg]),
                        waits=[g("xm_r", cg - 4)], inc=DY[par], amt=16)
                    xt = s.tok(DY[par])
                    lt = ln_chunk(xm[:], xl2[:], par, [xt])
                    wl = [lt, psT2_g[0], g("xlf3_r", st - 2), g("xl2_r", cg - 2)]
                    last = None
                    for k in range(KD):
                        last = s.emit("pe", lambda e, k=k, xl2=xl2: e.transpose(
                            psT2[:, k * 128:(k + 1) * 128],
                            xl2[:, k * 128:(k + 1) * 128], identb[:]),
                            waits=wl if k == 0 else (),
                            inc=PE if k == KD - 1 else None)
                    G[("xl2_r", cg)] = last
                    ev = s.emit("dv", lambda e, c2=c2, xlf3_v=xlf3_v:
                        e.tensor_copy(
                        xlf3_v[:, :, c2 * CH:(c2 + 1) * CH],
                        psT2.rearrange("p (k t) -> p k t", t=128)),
                        waits=[last], inc=DV)
                    psT2_g[0] = ev
                    evs.append(ev)
                G[("p3front", st)] = evs

            p3_front(0)
            for st in range(NST):
                ch0 = st * CPS
                xlf3_v = xlf3_vs[st % 2]
                evs = G[("p3front", st)]
                for f in range(NFC):
                    ps, guard, ri = rot_ps()
                    lastf = None
                    for k in range(KD):
                        lastf = s.emit("pe", lambda e, k=k, f=f, ps=ps,
                                       xlf3_v=xlf3_v: e.matmul(
                            ps[:, 0:ST], wfc_v[:, k, f * 128:(f + 1) * 128],
                            xlf3_v[:, k, :], start=(k == 0), stop=(k == KD - 1)),
                            waits=evs + [WFC, guard] if k == 0 else (),
                            inc=PE if k == KD - 1 else None)
                    rl = s.emit("ac", lambda e, ps=ps: e.activation(
                        rlb[:], ps[:, 0:ST], AF.Relu),
                        waits=[lastf, g("rlb_r")], inc=AC)
                    hev = s.emit("dv", lambda e, f=f: e.tensor_tensor(
                        hbuf_v[:, f, :], rlb[:], rlb[:], ALU.mult),
                        waits=[rl, g("hbuf_r", st - 1)], inc=DV)
                    G[("rlb_r",)] = hev
                    rot_done(ri, hev)
                if st + 1 < NST:
                    p3_front(st + 1)
                HEV = s.tok(DV)
                for c2 in range(CPS):
                    cg = ch0 + c2
                    par = cg % 2
                    xm = xms[cg % 4]
                    outb = outbs[par]
                    for gi in range(2):
                        psq = psY1 if gi == 0 else psY2
                        lasto = None
                        for f in range(NFC):
                            lasto = s.emit("pe",
                                lambda e, f=f, gi=gi, psq=psq, c2=c2: e.matmul(
                                    psq[:, 0:512],
                                    hbuf_v[:, f, c2 * CH:(c2 + 1) * CH],
                                    wpr_v[:, f, gi * 512:(gi + 1) * 512],
                                    start=(f == 0), stop=(f == NFC - 1)),
                                waits=[HEV, WPR, g("p3ps_r", gi)]
                                if f == 0 else (),
                                inc=PE if f == NFC - 1 else None)
                        yv = s.emit("ac", lambda e, gi=gi, psq=psq: e.copy(
                            y2[:, gi * 512:(gi + 1) * 512], psq[:, 0:512]),
                            waits=[lasto, g("y2_r", cg - 1)], inc=AC)
                        G[("p3ps_r", gi)] = yv
                    Y2 = s.tok(AC)
                    sp0 = s.emit("dv", lambda e: e.tensor_scalar(
                        out=spka[:], in0=y2[:], scalar1=SPIKE_B[0],
                        scalar2=None, op0=ALU.is_gt),
                        waits=[Y2, g("spka_r", cg - 2)], inc=DV)
                    acc = sp0
                    for j in range(1, 5):
                        sj = s.emit("dv", lambda e, j=j: e.tensor_scalar(
                            out=spkb[:], in0=y2[:], scalar1=SPIKE_B[j],
                            scalar2=None, op0=ALU.is_gt),
                            waits=[Y2, g("spkb_r")], inc=DV)
                        acc = s.emit("dv", lambda e: e.tensor_tensor(
                            spka[:], spka[:], spkb[:], ALU.add),
                            waits=[sj, acc], inc=DV)
                        G[("spkb_r",)] = acc
                    G[("y2_r", cg)] = acc
                    ob = s.emit("dv", lambda e, outb=outb, xm=xm:
                        e.scalar_tensor_tensor(out=outb[:], in0=spka[:],
                        scalar=0.2, in1=xm[:], op0=ALU.mult, op1=ALU.add),
                        waits=[acc, g("outb_sp", cg - 2)], inc=DV)
                    G[("spka_r", cg)] = ob
                    G[("xm_r", cg)] = ob
                    s.emit("sp", lambda e, outb=outb, cg=cg: e.dma_start(
                        out=out[cg * CH:(cg + 1) * CH, :], in_=outb[:]),
                        waits=[ob], inc=DZ[par], amt=16)
                    G[("outb_sp", cg)] = s.tok(DZ[par])
                G[("xlf3_r", st)] = s.tok(PE)
                G[("hbuf_r", st)] = s.tok(PE)
            tap("y2t", lambda: y2[:], waits=[s.tok(AC)])
            tap("spk", lambda: spka[:], waits=[s.tok(DV)], q="po")
            tap("hbt", lambda: hbuf[:], waits=[s.tok(DV)], q="po")
            p3.close()

    if LIMIT_PHASE < 2:
        p12.close()
    fin = [s.tok(sm) for sm in ("dsp", "dx0", "dx1", "dy0", "dy1",
                                "dz0", "dz1", "dpo")]
    if LIMIT_PHASE >= 2:
        fin.append(s.tok("cc"))
    s.emit("sp", lambda e: e.nop(), waits=fin)

    with nc.Block() as block:
        def mk(eng_name):
            def body(eng):
                for fn, waits, inc, amt in s.ops[eng_name]:
                    for sm, v in waits:
                        eng.wait_ge(sems[sm], v)
                    r = fn(eng)
                    if inc is not None:
                        r.then_inc(sems[inc], amt)
            return body
        block.tensor(mk("pe"))
        block.vector(mk("dv"))
        block.scalar(mk("ac"))
        block.gpsimd(mk("po"))
        block.sync(mk("sp"))

    oes.close()
    return nc


_CACHE = {}


def _get_nc():
    key = (LIMIT_PHASE, tuple(sorted(map(str, TAPS))))
    if key not in _CACHE:
        _CACHE[key] = build_nc()
    return _CACHE[key]


def kernel(L0_attn_X, L0_attn_W_in, L0_attn_conv_w, L0_attn_conv_b,
           L0_attn_dt_bias, L0_attn_A_log, L0_attn_D, L0_attn_norm_w,
           L0_attn_W_out, L0_ffn_W_fc, L0_ffn_W_proj):
    X = np.asarray(L0_attn_X, np.float32)
    W_in = np.asarray(L0_attn_W_in, np.float32)
    conv_w = np.asarray(L0_attn_conv_w, np.float32)
    conv_b = np.asarray(L0_attn_conv_b, np.float32)
    dt_bias = np.asarray(L0_attn_dt_bias, np.float32)
    A_log = np.asarray(L0_attn_A_log, np.float32)
    Dv = np.asarray(L0_attn_D, np.float32)
    norm_w = np.asarray(L0_attn_norm_w, np.float32)
    W_out = np.asarray(L0_attn_W_out, np.float32)
    W_fc = np.asarray(L0_ffn_W_fc, np.float32)
    W_proj = np.asarray(L0_ffn_W_proj, np.float32)

    w_p1h = np.ascontiguousarray(W_in[DIN:, :].T)
    w_zh = np.ascontiguousarray(W_in[:DIN, :].T)
    w_oh = np.ascontiguousarray(W_out.T)
    w_fh = np.ascontiguousarray(W_fc.T)
    w_ph = np.ascontiguousarray(W_proj.T)
    cw = np.ascontiguousarray(
        conv_w.reshape(NEC, 128, 4).transpose(1, 0, 2).reshape(128, NEC * 4))
    cb = np.ascontiguousarray(conv_b.reshape(NEC, 128).T)
    dtb = np.ascontiguousarray(dt_bias.reshape(NH, 1))
    alog = np.ascontiguousarray(A_log.reshape(NH, 1))
    dvec = np.ascontiguousarray(np.broadcast_to(Dv[None, :], (128, NH)))
    nw = np.ascontiguousarray(norm_w.reshape(16, 128).T)

    in_maps = []
    for k in range(NCORES):
        b, q = divmod(k, 4)
        xs_k = np.ascontiguousarray(X[b, q * TPC:(q + 1) * TPC])
        if q == 0:
            xp = np.zeros((3, D), np.float32)
        else:
            xp = np.ascontiguousarray(X[b, q * TPC - 3:q * TPC])
        sel = np.zeros((1, 8), np.float32)
        for j in range(4):
            if j < q:
                sel[0, 2 * j] = 1.0
                sel[0, 2 * j + 1] = 1.0
        selm = 1.0 - sel
        in_maps.append({
            "xs": xs_k, "xprev": xp, "w_p1": w_p1h, "w_z": w_zh,
            "w_out": w_oh, "w_fc": w_fh, "w_pr": w_ph, "cw": cw, "cb": cb,
            "dtb": dtb, "alog": alog, "dvec": dvec, "nw": nw,
            "sel": np.ascontiguousarray(np.broadcast_to(sel, (128, 8))),
            "selm": np.ascontiguousarray(np.broadcast_to(selm, (128, 8))),
        })

    nc = _get_nc()
    res = run_bass_kernel_spmd(nc, in_maps, core_ids=list(range(NCORES)))
    globals()["LAST_RES"] = res
    outp = np.empty((B, S, D), np.float32)
    for k in range(NCORES):
        b, q = divmod(k, 4)
        outp[b, q * TPC:(q + 1) * TPC] = res.results[k]["out"]
    return outp


# version 17
# speedup vs baseline: 1.0822x; 1.0178x over previous
from contextlib import ExitStack

import numpy as np
import concourse.bass as bass
import concourse.mybir as mybir
from concourse.bass_utils import run_bass_kernel_spmd

F32 = mybir.dt.float32
BF16 = mybir.dt.bfloat16
AF = mybir.ActivationFunctionType
ALU = mybir.AluOpType

B, S, D = 2, 8192, 1024
NCORES = 8
TPC = 2048
CH = 128
NCH = TPC // CH
ST = 256
NST = TPC // ST
CPS = ST // CH
DIN = 2048
NH = 32
HD = 64
NEC = 18
P1E = NEC * 128 + NH
KD = D // 128
DFF = 4096
NFC = DFF // 128
EPS = 1e-5

TAPS = {}
LIMIT_PHASE = 3


def _spike_thresholds():
    def count(y):
        y = np.float32(y)
        mem = np.float32(0.0)
        c = 0
        for _ in range(5):
            reset = np.float32(1.0) if mem > np.float32(1.0) else np.float32(0.0)
            mem = np.float32(np.float32(np.float32(0.9) * mem) + y) - reset
            c += 1 if mem > np.float32(1.0) else 0
        return c

    bs = []
    for j in range(1, 6):
        lo, hi = -4.0, 4.0
        for _ in range(60):
            mid = (lo + hi) / 2
            if count(mid) >= j:
                hi = mid
            else:
                lo = mid
        bs.append(float(np.float32(lo)))
    return bs


SPIKE_B = _spike_thresholds()


class Sched:

    ENG = ("pe", "dv", "ac", "po", "sp")

    def __init__(self):
        self.ops = {e: [] for e in self.ENG}
        self.counts = {}
        self.waited = {e: {} for e in self.ENG}

    def sem(self, name):
        self.counts.setdefault(name, 0)
        return name

    def emit(self, eng, fn, waits=(), inc=None, amt=1):
        pruned = []
        for w in waits:
            if w is None:
                continue
            sm, v = w
            assert v <= self.counts[sm], f"wait on future {sm}:{v}>{self.counts[sm]}"
            if v > self.waited[eng].get(sm, 0):
                self.waited[eng][sm] = v
                pruned.append((sm, v))
        self.ops[eng].append((fn, pruned, inc, amt))
        if inc is not None:
            self.counts[inc] += amt
            return (inc, self.counts[inc])
        return None

    def tok(self, sem):
        return (sem, self.counts[sem])


def build_nc():
    nc = bass.Bass("TRN2", target_bir_lowering=False, debug=False)
    s = Sched()
    oes = ExitStack()

    xs = nc.dram_tensor("xs", [TPC, D], F32, kind="ExternalInput").ap()
    xprev = nc.dram_tensor("xprev", [3, D], F32, kind="ExternalInput").ap()
    w_p1 = nc.dram_tensor("w_p1", [D, P1E], F32, kind="ExternalInput").ap()
    w_z = nc.dram_tensor("w_z", [D, DIN], F32, kind="ExternalInput").ap()
    w_out = nc.dram_tensor("w_out", [DIN, D], F32, kind="ExternalInput").ap()
    w_fc = nc.dram_tensor("w_fc", [D, DFF], F32, kind="ExternalInput").ap()
    w_pr = nc.dram_tensor("w_pr", [DFF, D], F32, kind="ExternalInput").ap()
    cw_in = nc.dram_tensor("cw", [128, NEC * 4], F32, kind="ExternalInput").ap()
    cb_in = nc.dram_tensor("cb", [128, NEC], F32, kind="ExternalInput").ap()
    dtb_in = nc.dram_tensor("dtb", [NH, 1], F32, kind="ExternalInput").ap()
    alog_in = nc.dram_tensor("alog", [NH, 1], F32, kind="ExternalInput").ap()
    db_in = nc.dram_tensor("dvec", [128, NH], F32, kind="ExternalInput").ap()
    nw_in = nc.dram_tensor("nw", [128, 16], F32, kind="ExternalInput").ap()
    sel_in = nc.dram_tensor("sel", [128, 8], F32, kind="ExternalInput").ap()
    selm_in = nc.dram_tensor("selm", [128, 8], F32, kind="ExternalInput").ap()
    out = nc.dram_tensor("out", [TPC, D], F32, kind="ExternalOutput").ap()

    spk = "ExternalOutput" if "spills" in TAPS else "Internal"
    y_sp = nc.dram_tensor("y_sp", [NCH, 128, DIN], F32, kind=spk).ap()
    xm_sp = nc.dram_tensor("xm_sp", [NCH, 128, D], F32, kind=spk).ap()
    mout = nc.dram_tensor("mout", [128, DIN + NH], F32,
                          kind="ExternalOutput" if "mout" in TAPS else "Internal").ap()
    mg = nc.dram_tensor("mg", [4 * 128, DIN + NH], F32, kind="Internal").ap()

    tapd = {}
    for name, shape in TAPS.items():
        if name in ("spills", "mout"):
            continue
        tapd[name] = nc.dram_tensor("tap_" + name, list(shape), F32,
                                    kind="ExternalOutput").ap()

    PE = s.sem("pe"); DV = s.sem("dv"); AC = s.sem("ac"); PO = s.sem("po")
    DSP = s.sem("dsp"); DPO = s.sem("dpo"); CC = s.sem("cc")
    DX = [s.sem("dx0"), s.sem("dx1")]
    DY = [s.sem("dy0"), s.sem("dy1")]
    DZ = [s.sem("dz0"), s.sem("dz1")]

    def tap(name, src_fn, waits=(), q="sp"):
        if name in tapd:
            s.emit(q, lambda e, n=name: e.dma_start(out=tapd[n], in_=src_fn()),
                   waits=waits, inc=DSP if q == "sp" else DPO, amt=16)

    def sb(es, name, shape, dt):
        return es.enter_context(nc.sbuf_tensor(name, shape, dt))

    identb = sb(oes, "identb", [128, 128], BF16)
    identf = sb(oes, "identf", [128, 128], F32)
    stepb = sb(oes, "stepb", [128, 128], BF16)
    cmaskb = sb(oes, "cmaskb", [128, 128], BF16)
    e127 = sb(oes, "e127", [128, 128], F32)
    zero32 = sb(oes, "zero32", [128, 32], F32)
    onescol = sb(oes, "onescol", [32, 1], F32)
    smallcol = sb(oes, "smallcol", [NH, 4], F32)
    db32 = sb(oes, "db32", [128, NH], F32)
    selb = sb(oes, "selb", [128, 8], F32)
    selb1m = sb(oes, "selb1m", [128, 8], F32)
    cw_sb = sb(oes, "cw_sb", [128, NEC * 4], F32)
    cb_sb = sb(oes, "cb_sb", [128, NEC], F32)
    nw_sb = sb(oes, "nw_sb", [128, 16], F32)
    stat = sb(oes, "stat", [128, 40], F32)
    carry = sb(oes, "carry", [32, 1], F32)
    eps_t = sb(oes, "eps_t", [128, 1], F32)
    p12 = ExitStack()
    cadt = sb(p12, "cadt", [128, NCH * 96], F32)
    c_all = sb(p12, "c_all", [128, NCH * 128], BF16)
    psA = oes.enter_context(nc.psum_tensor("psA", [128, 512], F32))
    psB = oes.enter_context(nc.psum_tensor("psB", [128, 512], F32))
    psY1 = oes.enter_context(nc.psum_tensor("psY1", [128, 512], F32))
    psY2 = oes.enter_context(nc.psum_tensor("psY2", [128, 512], F32))
    psM = oes.enter_context(nc.psum_tensor("psM", [128, 512], F32))
    psT = oes.enter_context(nc.psum_tensor("psT", [128, 2048], BF16))
    psT2 = oes.enter_context(nc.psum_tensor("psT2", [128, 1024], BF16))
    sems = {}
    for nm in ("pe", "dv", "ac", "po", "dsp", "dpo", "cc", "dx0", "dx1",
               "dy0", "dy1", "dz0", "dz1"):
        sems[nm] = oes.enter_context(nc.semaphore(nm))

    cadt_v = cadt.rearrange("p (c w) -> p c w", w=96)
    call_v = c_all.rearrange("p (c t) -> p c t", t=128)

    rotst = {"i": 0, "tok": [None, None]}

    def rot_ps():
        i = rotst["i"]
        rotst["i"] ^= 1
        return (psA, psB)[i], rotst["tok"][i], i

    def rot_done(i, token):
        rotst["tok"][i] = token

    psT_g = [None]
    psT2_g = [None]

    s.emit("po", lambda e: e.memset(identb[:], 0.0))
    s.emit("po", lambda e: e.affine_select(out=identb[:], in_=identb[:],
           compare_op=ALU.not_equal, fill=1.0, base=0,
           pattern=[[-1, 128]], channel_multiplier=1))
    s.emit("po", lambda e: e.memset(identf[:], 0.0))
    s.emit("po", lambda e: e.affine_select(out=identf[:], in_=identf[:],
           compare_op=ALU.not_equal, fill=1.0, base=0,
           pattern=[[-1, 128]], channel_multiplier=1))
    s.emit("po", lambda e: e.memset(stepb[:], 1.0))
    s.emit("po", lambda e: e.affine_select(out=stepb[:], in_=stepb[:],
           compare_op=ALU.is_gt, fill=0.0, base=0,
           pattern=[[-1, 128]], channel_multiplier=1))
    s.emit("po", lambda e: e.memset(cmaskb[:], 1.0))
    s.emit("po", lambda e: e.affine_select(out=cmaskb[:], in_=cmaskb[:],
           compare_op=ALU.is_ge, fill=0.0, base=0,
           pattern=[[1, 128]], channel_multiplier=-1))
    s.emit("po", lambda e: e.memset(e127[:], 0.0))
    s.emit("po", lambda e: e.affine_select(out=e127[:], in_=e127[:],
           compare_op=ALU.not_equal, fill=1.0, base=-127,
           pattern=[[0, 128]], channel_multiplier=1))
    s.emit("po", lambda e: e.memset(zero32[:], 0.0))
    s.emit("po", lambda e: e.memset(onescol[:], 1.0))
    s.emit("po", lambda e: e.memset(carry[:], 0.0))
    s.emit("po", lambda e: e.memset(eps_t[:], EPS), inc=PO)
    CPO = s.tok(PO)

    s.emit("sp", lambda e: e.dma_start(out=cw_sb[:], in_=cw_in), inc=DSP, amt=16)
    s.emit("sp", lambda e: e.dma_start(out=cb_sb[:], in_=cb_in), inc=DSP, amt=16)
    s.emit("sp", lambda e: e.dma_start(out=nw_sb[:], in_=nw_in), inc=DSP, amt=16)
    s.emit("sp", lambda e: e.dma_start(out=db32[:], in_=db_in), inc=DSP, amt=16)
    s.emit("sp", lambda e: e.dma_start(out=selb[:], in_=sel_in), inc=DSP, amt=16)
    s.emit("sp", lambda e: e.dma_start(out=selb1m[:], in_=selm_in), inc=DSP, amt=16)
    s.emit("sp", lambda e: e.dma_start(out=smallcol[:, 0:1], in_=dtb_in,
           allow_slow_non_contiguous=True), inc=DSP, amt=16)
    s.emit("sp", lambda e: e.dma_start(out=smallcol[:, 1:2], in_=alog_in,
           allow_slow_non_contiguous=True), inc=DSP, amt=16)
    LD0 = s.tok(DSP)
    t = s.emit("ac", lambda e: e.activation(smallcol[:, 2:3],
               smallcol[:, 1:2], AF.Exp), waits=[LD0], inc=AC)
    A_TOK = s.emit("dv", lambda e: e.tensor_scalar(out=smallcol[:, 2:3],
                   in0=smallcol[:, 2:3], scalar1=-1.0, scalar2=None,
                   op0=ALU.mult), waits=[t], inc=DV)
    ln_prev = {}

    def ln_chunk(xin_ap, dst, par, waits):
        c0 = 16 * par
        nf = xin_ap.shape[-1]
        prev = ln_prev.get(par)
        wl = list(waits) + ([prev[0]] if prev else [])
        h = nf // 2
        t1 = s.emit("dv", lambda e: e.bn_stats(stat[:xin_ap.shape[0], c0:c0 + 6],
                    xin_ap[:, 0:h]), waits=wl, inc=DV)
        t1 = s.emit("dv", lambda e: e.bn_stats(
                    stat[:xin_ap.shape[0], c0 + 6:c0 + 12], xin_ap[:, h:nf]),
                    waits=[t1], inc=DV)
        t2 = s.emit("dv", lambda e: e.bn_aggr(
                    stat[:xin_ap.shape[0], c0 + 12:c0 + 14],
                    stat[:xin_ap.shape[0], c0:c0 + 12]), waits=[t1], inc=DV)
        t3 = s.emit("ac", lambda e: e.activation(
                    stat[:xin_ap.shape[0], c0 + 13:c0 + 14],
                    stat[:xin_ap.shape[0], c0 + 13:c0 + 14], AF.Ln,
                    bias=eps_t[:xin_ap.shape[0], 0:1]),
                    waits=[t2, CPO], inc=AC)
        t4 = s.emit("ac", lambda e: e.activation(
                    stat[:xin_ap.shape[0], c0 + 13:c0 + 14],
                    stat[:xin_ap.shape[0], c0 + 13:c0 + 14], AF.Exp,
                    scale=-0.5), waits=[t3], inc=AC)
        t5 = s.emit("dv", lambda e: e.scalar_tensor_tensor(
            out=stat[:xin_ap.shape[0], c0 + 12:c0 + 13],
            in0=stat[:xin_ap.shape[0], c0 + 12:c0 + 13], scalar=-1.0,
            in1=stat[:xin_ap.shape[0], c0 + 13:c0 + 14], op0=ALU.mult,
            op1=ALU.mult), waits=[t4], inc=DV)
        t6 = s.emit("dv", lambda e: e.tensor_scalar(
            out=dst, in0=xin_ap,
            scalar1=stat[:xin_ap.shape[0], c0 + 13:c0 + 14],
            scalar2=stat[:xin_ap.shape[0], c0 + 12:c0 + 13],
            op0=ALU.mult, op1=ALU.add), waits=[t5], inc=DV)
        ln_prev[par] = (t4, t6)
        return t6

    G = {}

    def g(*key):
        return G.get(key)

    p1 = ExitStack()
    convd = sb(p1, "convd", [128, NEC * 4 * 128], BF16)
    Hst = sb(p1, "H", [128, DIN], F32)
    hb0 = sb(p1, "hb0", [128, DIN], BF16)
    hb1 = sb(p1, "hb1", [128, DIN], BF16)
    s_ca = sb(p1, "s_ca", [32, ST], F32)
    s_dt = sb(p1, "s_dt", [32, ST], F32)
    s_dta = sb(p1, "s_dta", [32, ST], F32)
    wp1 = sb(p1, "wp1", [128, KD * P1E], BF16)
    xin0 = sb(p1, "xin0", [128, D], F32)
    xin1 = sb(p1, "xin1", [128, D], F32)
    xln0 = sb(p1, "xln0", [128, D], BF16)
    xln1 = sb(p1, "xln1", [128, D], BF16)
    xpl = sb(p1, "xpl", [3, D], BF16)
    xlf0 = sb(p1, "xlf0", [128, KD * 259], BF16)
    xlf1 = sb(p1, "xlf1", [128, KD * 259], BF16)
    xbc = sb(p1, "xbc", [128, NEC * 259], BF16)
    xc = sb(p1, "xc", [128, 16 * ST], BF16)
    b_bf = sb(p1, "b_bf", [128, ST], BF16)
    b_tma = sb(p1, "b_tma", [128, 128], BF16)
    b_tmb = sb(p1, "b_tmb", [128, 128], BF16)
    sp_scr = sb(p1, "sp_scr", [32, ST], F32)
    sp_scr2 = sb(p1, "sp_scr2", [32, ST], F32)
    sc32a = sb(p1, "sc32a", [128, 96], F32)
    sc32b = sb(p1, "sc32b", [128, 96], F32)
    eo_bfa = sb(p1, "eo_bfa", [128, NH], BF16)
    eo_bfb = sb(p1, "eo_bfb", [128, NH], BF16)
    erc_bfa = sb(p1, "erc_bfa", [128, NH], BF16)
    erc_bfb = sb(p1, "erc_bfb", [128, NH], BF16)
    gama = sb(p1, "gama", [128, NH], F32)
    gamb = sb(p1, "gamb", [128, NH], F32)
    dt_bfa = sb(p1, "dt_bfa", [128, NH], BF16)
    dt_bfb = sb(p1, "dt_bfb", [128, NH], BF16)
    dta_bfa = sb(p1, "dta_bfa", [128, NH], BF16)
    dta_bfb = sb(p1, "dta_bfb", [128, NH], BF16)
    rhsT = sb(p1, "rhsT", [128, 4096], BF16)
    expd = sb(p1, "expd", [128, 4096], BF16)
    ghat0 = sb(p1, "ghat0", [128, 4096], BF16)
    ghat1 = sb(p1, "ghat1", [128, 4096], BF16)
    gm_bfa = sb(p1, "gm_bfa", [128, 128], BF16)
    gm_bfb = sb(p1, "gm_bfb", [128, 128], BF16)
    xtm0 = sb(p1, "xtm0", [128, DIN], BF16)
    xtm1 = sb(p1, "xtm1", [128, DIN], BF16)
    xda = sb(p1, "xda", [128, DIN], BF16)
    xdb = sb(p1, "xdb", [128, DIN], BF16)
    xdza = sb(p1, "xdza", [128, DIN], BF16)
    xdzb = sb(p1, "xdzb", [128, DIN], BF16)
    ytm0 = sb(p1, "ytm0", [128, DIN], F32)
    ytm1 = sb(p1, "ytm1", [128, DIN], F32)
    yscr = sb(p1, "yscr", [128, 512], F32)
    hscr = sb(p1, "hscr", [128, 512], F32)
    atile = sb(p1, "atile", [128, NH], F32)

    hb = [hb0, hb1]
    convd_v = convd.rearrange("p (i t) -> p i t", t=128)
    s.emit("po", lambda e: e.memset(Hst[:], 0.0))
    s.emit("po", lambda e: e.memset(hb0[:], 0.0), inc=PO)
    P1Z = s.tok(PO)
    cd = None
    for i in range(NEC * 4):
        cd = s.emit("dv", lambda e, i=i: e.tensor_scalar(
            out=convd_v[:, i, :], in0=identb[:],
            scalar1=cw_sb[:, i:i + 1], scalar2=None, op0=ALU.mult),
            waits=[LD0, CPO], inc=DV)
    CONVD_TOK = cd
    wp1_v = wp1.rearrange("p (k e) -> p k e", k=KD)
    xlf_vs = [xlf0.rearrange("p (k t) -> p k t", k=KD),
              xlf1.rearrange("p (k t) -> p k t", k=KD)]
    xbc_v = xbc.rearrange("p (c t) -> p c t", t=259)
    xc_v = xc.rearrange("p (c t) -> p c t", t=ST)
    xins = [xin0, xin1]
    xlns = [xln0, xln1]
    xtms = [xtm0, xtm1]
    ytms = [ytm0, ytm1]
    ghats = [ghat0, ghat1]
    sc32s = [sc32a, sc32b]
    eo_bfs = [eo_bfa, eo_bfb]
    erc_bfs = [erc_bfa, erc_bfb]
    gams = [gama, gamb]
    dt_bfs = [dt_bfa, dt_bfb]
    dta_bfs = [dta_bfa, dta_bfb]
    gm_bfs = [gm_bfa, gm_bfb]
    b_tms = [b_tma, b_tmb]
    xds = [xda, xdb]
    xdzs = [xdza, xdzb]

    for k in range(KD):
        s.emit("po", lambda e, k=k: e.dma_start(
            out=wp1_v[:, k, :], in_=w_p1[k * 128:(k + 1) * 128, :]),
            waits=[CPO], inc=DPO, amt=16)
    WP1 = s.tok(DPO)

    s.emit("sp", lambda e: e.dma_start(out=xin0[0:3, :], in_=xprev),
           inc=DSP, amt=16)
    XPL = s.tok(DSP)
    tp6 = ln_chunk(xin0[0:3, :], xpl[0:3, :], 0, [XPL])
    lastp = None
    for k in range(KD):
        lastp = s.emit("pe", lambda e, k=k: e.transpose(
            psT2[:, k * 128:k * 128 + 3], xpl[0:3, k * 128:(k + 1) * 128],
            identb[0:3, 0:3]), waits=[tp6] if k == 0 else (),
            inc=PE if k == KD - 1 else None)
    ev = s.emit("dv", lambda e: e.tensor_copy(
        xlf_vs[0][:, :, 0:3],
        psT2.rearrange("p (k t) -> p k t", t=128)[:, 0:KD, 0:3]),
        waits=[lastp], inc=DV)
    psT2_g[0] = ev
    XPREV_EV = ev
    G[("xin_r", -2)] = tp6

    def p1_front(st):
        ch0 = st * CPS
        xlf_v = xlf_vs[st % 2]
        evs = []
        for c2 in range(CPS):
            cg = ch0 + c2
            par = cg % 2
            xin, xln = xins[par], xlns[par]
            s.emit("sp", lambda e, xin=xin, cg=cg: e.dma_start(
                out=xin[:], in_=xs[cg * CH:(cg + 1) * CH, :]),
                waits=[g("xin_r", cg - 2)], inc=DX[par], amt=16)
            xt = s.tok(DX[par])
            lt = ln_chunk(xin[:], xln[:], par, [xt])
            G[("xin_r", cg)] = lt
            wl = [lt, g("xlf_r", st - 2), psT2_g[0], g("xln_r", cg - 2)]
            last = None
            for k in range(KD):
                last = s.emit("pe", lambda e, k=k, xln=xln: e.transpose(
                    psT2[:, k * 128:(k + 1) * 128],
                    xln[:, k * 128:(k + 1) * 128], identb[:]),
                    waits=wl if k == 0 else (),
                    inc=PE if k == KD - 1 else None)
            G[("xln_r", cg)] = last
            ev = s.emit("dv", lambda e, c2=c2, xlf_v=xlf_v: e.tensor_copy(
                xlf_v[:, :, 3 + c2 * CH:3 + (c2 + 1) * CH],
                psT2.rearrange("p (k t) -> p k t", t=128)),
                waits=[last], inc=DV)
            psT2_g[0] = ev
            evs.append(ev)
        if st == 0:
            evs.append(XPREV_EV)
        G[("p1front", st)] = evs

    p1_front(0)
    for st in range(NST):
        ch0 = st * CPS
        xlf_v = xlf_vs[st % 2]
        ln_evs = G[("p1front", st)]

        margin = None
        if st > 0:
            margin = s.emit("dv", lambda e: e.tensor_copy(
                xbc_v[:, :, 0:3], xbc_v[:, :, 256:259]),
                waits=[g("conv_r", st - 1)], inc=DV)

        co = 0 if st == 0 else 3
        N = 259 if st == 0 else 256
        doff = 0 if st == 0 else 3
        evac = []
        dt_info = None
        for ei in range(NEC + 1):
            ps, guard, ri = rot_ps()
            m = NH if ei == NEC else 128
            wl = ln_evs + [WP1, guard]
            last = None
            for k in range(KD):
                last = s.emit("pe",
                    lambda e, k=k, ei=ei, ps=ps, m=m, co=co, N=N,
                    xlf_v=xlf_v: e.matmul(
                        ps[0:m, 0:N], wp1_v[:, k, ei * 128:ei * 128 + m],
                        xlf_v[:, k, co:co + N],
                        start=(k == 0), stop=(k == KD - 1)),
                    waits=wl if k == 0 else (),
                    inc=PE if k == KD - 1 else None)
            if ei < NEC:
                ev = s.emit("ac", lambda e, ei=ei, ps=ps, doff=doff, N=N:
                            e.copy(xbc_v[:, ei, doff:doff + N], ps[:, 0:N]),
                            waits=[last, margin], inc=AC)
                evac.append(ev)
                rot_done(ri, ev)
            else:
                dt_info = (ps, ri, last)
        G[("xlf_r", st)] = s.tok(PE)
        if st + 1 < NST:
            p1_front(st + 1)

        ps, ri, mmtok = dt_info
        off = 3 if st == 0 else 0
        stc = slice(0, ST)
        t1 = s.emit("dv", lambda e, ps=ps, off=off: e.tensor_scalar(
            out=sp_scr[:, :], in0=ps[0:NH, off:off + ST],
            scalar1=smallcol[0:NH, 0:1], scalar2=None, op0=ALU.add),
            waits=[mmtok, A_TOK, g("spscr_r", st - 1)], inc=DV)
        rot_done(ri, t1)
        t2 = s.emit("ac", lambda e: e.activation(sp_scr2[:, :], sp_scr[:, :],
                    AF.Abs), waits=[t1], inc=AC)
        t3 = s.emit("ac", lambda e: e.activation(sp_scr2[:, :], sp_scr2[:, :],
                    AF.Exp, scale=-1.0), waits=[t2], inc=AC)
        t4 = s.emit("ac", lambda e: e.activation(sp_scr2[:, :], sp_scr2[:, :],
                    AF.Ln, bias=1.0), waits=[t3], inc=AC)
        t5 = s.emit("dv", lambda e: e.tensor_scalar(out=sp_scr[:, :],
                    in0=sp_scr[:, :], scalar1=0.0, scalar2=None,
                    op0=ALU.max), waits=[t4], inc=DV)
        t6 = s.emit("dv", lambda e, stc=stc: e.tensor_tensor(
            s_dt[:, stc], sp_scr[:, :], sp_scr2[:, :], ALU.add),
            waits=[t5, g("stk_r", st - 1)], inc=DV)
        G[("spscr_r", st)] = t6
        t7 = s.emit("dv", lambda e, stc=stc: e.tensor_scalar(
            out=s_dta[:, stc], in0=s_dt[:, stc],
            scalar1=smallcol[0:NH, 2:3], scalar2=None, op0=ALU.mult),
            waits=[t6], inc=DV)
        t8 = s.emit("dv", lambda e, stc=stc: e.tensor_tensor_scan(
            s_ca[:, stc], onescol[:, 0:1].broadcast_to([32, ST]),
            s_dta[:, stc], carry[:, 0:1], ALU.mult, ALU.add),
            waits=[t7, CPO], inc=DV)
        SCAN = s.emit("dv", lambda e: e.tensor_copy(
            carry[:, 0:1], s_ca[:, ST - 1:ST]), waits=[t8], inc=DV)

        conv_ev = []
        conv_last = None
        for ei in range(NEC):
            ps, guard, ri = rot_ps()
            wl = [evac[ei], CONVD_TOK, guard]
            last = None
            for tp in range(4):
                last = s.emit("pe", lambda e, ei=ei, tp=tp, ps=ps: e.matmul(
                    ps[:, 0:ST], convd_v[:, ei * 4 + tp, :],
                    xbc_v[:, ei, tp:tp + ST], start=(tp == 0), stop=(tp == 3)),
                    waits=wl if tp == 0 else (), inc=PE if tp == 3 else None)
            conv_last = last
            if ei < 16:
                dst = xc_v[:, ei, :]
                wl2 = [last, g("xc_r", st - 1)]
            elif ei == 16:
                dst = b_bf[:, :]
                wl2 = [last, g("bbf_r", st - 1)]
            else:
                dst = c_all[:, st * ST:(st + 1) * ST]
                wl2 = [last]
            ev = s.emit("ac", lambda e, ps=ps, dst=dst, ei=ei: e.activation(
                dst, ps[:, 0:ST], AF.Silu, bias=cb_sb[:, ei:ei + 1]),
                waits=wl2, inc=AC)
            conv_ev.append(ev)
            rot_done(ri, ev)
        G[("conv_r", st)] = conv_last

        def p1_prep(c2):
            cg = ch0 + c2
            par = cg % 2
            ccols = slice(c2 * CH, (c2 + 1) * CH)
            csl = ccols
            xtm = xtms[par]
            ghat = ghats[par]
            sc32 = sc32s[par]
            eo_bf, erc_bf, gam = eo_bfs[par], erc_bfs[par], gams[par]
            dt_bf, dta_bf = dt_bfs[par], dta_bfs[par]
            gm_bf, b_tm = gm_bfs[par], b_tms[par]
            xd, xdz = xds[par], xdzs[par]

            s.emit("pe", lambda e, ccols=ccols: e.transpose(
                psM[:, 0:32], s_ca[:, ccols], identf[0:32, 0:32]),
                waits=[SCAN, g("psm_ra", cg - 1)])
            s.emit("pe", lambda e, ccols=ccols: e.transpose(
                psM[:, 32:64], s_dt[:, ccols], identf[0:32, 0:32]))
            tT = s.emit("pe", lambda e, ccols=ccols: e.transpose(
                psM[:, 64:96], s_dta[:, ccols], identf[0:32, 0:32]), inc=PE)
            evc = s.emit("ac", lambda e, cg=cg: e.copy(
                cadt_v[:, cg, :], psM[:, 0:96]), waits=[tT], inc=AC)
            G[("p_evc", cg)] = evc
            prev_rhs = cadt_v[:, cg - 1, 0:32] if cg > 0 else zero32[:, 0:32]
            sp1 = s.emit("pe", lambda e, prev_rhs=prev_rhs: e.matmul(
                psM[:, 96:128], e127[:], prev_rhs, start=True, stop=True),
                waits=[evc], inc=PE)
            sp2 = s.emit("pe", lambda e, cg=cg: e.matmul(
                psM[:, 128:160], e127[:], cadt_v[:, cg, 0:32],
                start=True, stop=True), waits=[evc], inc=PE)
            wl = [sp1, sp2]
            a1 = s.emit("dv", lambda e, cg=cg, sc32=sc32: e.tensor_tensor(
                sc32[:, 0:32], cadt_v[:, cg, 0:32], psM[:, 96:128],
                ALU.subtract), waits=wl, inc=DV)
            a2 = s.emit("dv", lambda e, cg=cg, sc32=sc32: e.tensor_tensor(
                sc32[:, 32:64], psM[:, 128:160], cadt_v[:, cg, 0:32],
                ALU.subtract), waits=wl, inc=DV)
            a3 = s.emit("dv", lambda e, sc32=sc32: e.tensor_tensor(
                sc32[:, 64:96], sc32[:, 0:32], sc32[:, 32:64], ALU.add),
                waits=[a1, a2], inc=DV)
            wl_e = [g("qend_dv", cg - 2), g("qend_po", cg - 2)]
            e1 = s.emit("ac", lambda e, eo_bf=eo_bf, sc32=sc32: e.activation(
                eo_bf[:], sc32[:, 0:32], AF.Exp), waits=[a1] + wl_e, inc=AC)
            e2 = s.emit("ac", lambda e, erc_bf=erc_bf, sc32=sc32: e.activation(
                erc_bf[:], sc32[:, 32:64], AF.Exp), waits=[a2] + wl_e, inc=AC)
            e3 = s.emit("ac", lambda e, gam=gam, sc32=sc32: e.activation(
                gam[:], sc32[:, 64:96], AF.Exp), waits=[a3] + wl_e, inc=AC)
            d1 = s.emit("ac", lambda e, cg=cg, dt_bf=dt_bf: e.copy(
                dt_bf[:], cadt_v[:, cg, 32:64]), waits=[evc], inc=AC)
            d2 = s.emit("ac", lambda e, cg=cg, dta_bf=dta_bf: e.copy(
                dta_bf[:], cadt_v[:, cg, 64:96]), waits=[evc], inc=AC)
            rb0 = s.emit("po", lambda e, dta_bf=dta_bf: e.tensor_tensor(
                rhsT.rearrange("p (h t) -> p h t", h=NH)[:, 0:16, :],
                dta_bf[:, 0:16, None].broadcast_to([128, 16, 128]),
                cmaskb[:, None, :].broadcast_to([128, 16, 128]),
                ALU.mult), waits=[d2, g("rhsT_r", cg - 1)], inc=PO)
            rb = s.emit("dv", lambda e, dta_bf=dta_bf: e.tensor_tensor(
                rhsT.rearrange("p (h t) -> p h t", h=NH)[:, 16:32, :],
                dta_bf[:, 16:32, None].broadcast_to([128, 16, 128]),
                cmaskb[:, None, :].broadcast_to([128, 16, 128]),
                ALU.mult), waits=[d2, g("rhsT_r", cg - 1)], inc=DV)
            gmm = s.emit("pe", lambda e, csl=csl, cg=cg: e.matmul(
                psM[:, 160:288], b_bf[:, csl], call_v[:, cg, :],
                start=True, stop=True),
                waits=[conv_ev[16], conv_ev[17], a1, a2], inc=PE)
            gmb = s.emit("dv", lambda e, gm_bf=gm_bf: e.tensor_tensor(
                gm_bf[:], psM[:, 160:288], cmaskb[:], ALU.mult),
                waits=[gmm], inc=DV)
            G[("psm_ra", cg)] = gmb
            if cg == NCH - 1:
                G[("la",)] = s.emit("ac", lambda e: e.activation(
                    atile[:], psM[:, 128:160], AF.Exp), waits=[a1, a2], inc=AC)
            gh_dv = None
            gh_po = None
            am_last = None
            for gi in range(8):
                ps, guard, ri = rot_ps()
                am = s.emit("pe", lambda e, gi=gi, ps=ps: e.matmul(
                    ps[:, 0:512], stepb[:], rhsT[:, gi * 512:(gi + 1) * 512],
                    start=True, stop=True),
                    waits=[rb0 if gi < 4 else rb, guard, CPO], inc=PE)
                am_last = am
                ex = s.emit("ac", lambda e, gi=gi, ps=ps: e.activation(
                    expd[:, gi * 512:(gi + 1) * 512], ps[:, 0:512], AF.Exp),
                    waits=[am, g("expd_r", cg - 1), g("expd_rp", cg - 1)],
                    inc=AC)
                rot_done(ri, ex)
                qeng = "dv" if gi % 2 == 0 else "po"
                gh = s.emit(qeng, lambda e, gi=gi, ghat=ghat, gm_bf=gm_bf:
                    e.tensor_tensor(
                    ghat.rearrange("p (h t) -> p h t", h=NH)[:, gi * 4:(gi + 1) * 4, :],
                    expd.rearrange("p (h t) -> p h t", h=NH)[:, gi * 4:(gi + 1) * 4, :],
                    gm_bf[:, None, :].broadcast_to([128, 4, 128]),
                    ALU.mult), waits=[ex, gmb, g("ghat_r", cg - 2)],
                    inc=DV if gi % 2 == 0 else PO)
                if gi % 2 == 0:
                    gh_dv = gh
                else:
                    gh_po = gh
            G[("rhsT_r", cg)] = am_last
            G[("expd_r", cg)] = gh_dv
            G[("expd_rp", cg)] = gh_po
            bt = s.emit("pe", lambda e, csl=csl: e.transpose(
                psT[:, 0:128], b_bf[:, csl], identb[:]),
                waits=[conv_ev[16], psT_g[0]], inc=PE)
            if c2 == CPS - 1:
                G[("bbf_r", st)] = bt
            bte = s.emit("dv", lambda e, b_tm=b_tm: e.tensor_copy(
                b_tm[:], psT[:, 0:128]),
                waits=[bt, g("xdz_r", cg - 2)], inc=DV)
            psT_g[0] = bte
            wl = [conv_ev[15], bte, g("xtm_r", cg - 2)]
            last = None
            for i in range(16):
                last = s.emit("pe", lambda e, i=i, csl=csl: e.transpose(
                    psT[:, i * 128:(i + 1) * 128], xc_v[:, i, csl], identb[:]),
                    waits=wl if i == 0 else (), inc=PE if i == 15 else None)
            if c2 == CPS - 1:
                G[("xc_r", st)] = last
            xte = s.emit("ac", lambda e, xtm=xtm: e.copy(xtm[:], psT[:]),
                         waits=[last], inc=AC)
            psT_g[0] = xte
            xdt = s.emit("dv", lambda e, xtm=xtm, xd=xd, dt_bf=dt_bf:
                e.tensor_tensor(
                xd.rearrange("p (h q) -> p h q", h=NH),
                xtm.rearrange("p (h q) -> p h q", h=NH),
                dt_bf[:, :, None].broadcast_to([128, NH, HD]),
                ALU.mult), waits=[xte, d1, g("xd_r", cg - 2)], inc=DV)
            xzt = s.emit("dv", lambda e, xd=xd, xdz=xdz, erc_bf=erc_bf:
                e.tensor_tensor(
                xdz.rearrange("p (h q) -> p h q", h=NH),
                xd.rearrange("p (h q) -> p h q", h=NH),
                erc_bf[:, :, None].broadcast_to([128, NH, HD]),
                ALU.mult), waits=[xdt, e2, g("dxbf_r", cg - 2)], inc=DV)
            G[("p_tok", cg)] = (gh_dv, gh_po, xdt, xzt, e1, e3, bte)
            if c2 == CPS - 1:
                G[("stk_r", st)] = tT

        def p1_quarters(c2):
            cg = ch0 + c2
            par = cg % 2
            xtm, ytm = xtms[par], ytms[par]
            ghat = ghats[par]
            hbr, hbw = hb[par], hb[1 - par]
            eo_bf, gam = eo_bfs[par], gams[par]
            b_tm = b_tms[par]
            xd, xdz = xds[par], xdzs[par]
            gh_dv, gh_po, xdt, xzt, e1, e3, bte = G[("p_tok", cg)]
            for q in range(4):
                qs = slice(q * 512, (q + 1) * 512)
                pa, pb = (psY1, psY2) if q % 2 == 0 else (psA, psB)
                wl1 = [gh_dv, gh_po, xdt, g("q_ya", cg * 4 + q - 2)]
                if q % 2 == 1:
                    wl1 += [rotst["tok"][0]]
                l1 = None
                for j in range(8):
                    h = q * 8 + j
                    l1 = s.emit("pe", lambda e, h=h, j=j, ghat=ghat, xd=xd,
                        pa=pa: e.matmul(
                        pa[:, j * 64:(j + 1) * 64],
                        ghat[:, h * 128:(h + 1) * 128],
                        xd[:, h * HD:(h + 1) * HD], start=True, stop=True),
                        waits=wl1 if j == 0 else (), inc=PE if j == 7 else None)
                G[("ghat_r", cg)] = l1
                G[("xd_r", cg)] = l1
                wl2 = [g("q_yb", cg * 4 + q - 2), g("hb_w", cg - 1, q), P1Z]
                if q % 2 == 1:
                    wl2 += [rotst["tok"][1]]
                l2 = None
                for j in range(8):
                    h = q * 8 + j
                    l2 = s.emit("pe", lambda e, h=h, j=j, hbr=hbr, cg=cg,
                        pb=pb: e.matmul(
                        pb[:, j * 64:(j + 1) * 64], call_v[:, cg, :],
                        hbr[:, h * HD:(h + 1) * HD], start=True, stop=True),
                        waits=wl2 if j == 0 else (), inc=PE if j == 7 else None)
                G[("hb_r", cg, q)] = l2
                l3 = None
                s_extra = []
                if q == 0:
                    other = ch0 + 1 if c2 == 0 else cg
                    s_extra = [g("psm_ra", other), g("p_evc", other),
                               g("la")]
                for j in range(8):
                    h = q * 8 + j
                    l3 = s.emit("pe", lambda e, h=h, j=j, b_tm=b_tm,
                        xdz=xdz: e.matmul(
                        psM[:, j * 64:(j + 1) * 64], b_tm[:],
                        xdz[:, h * HD:(h + 1) * HD], start=True, stop=True),
                        waits=[bte, xzt, g("q_ha", cg * 4 + q - 1)] + s_extra
                        if j == 0 else (), inc=PE if j == 7 else None)
                G[("xdz_r", cg)] = l3
                yb = s.emit("dv", lambda e, q=q, eo_bf=eo_bf, pb=pb:
                    e.tensor_tensor(
                    yscr.rearrange("p (j w) -> p j w", j=8),
                    pb.rearrange("p (j w) -> p j w", j=8),
                    eo_bf[:, q * 8:(q + 1) * 8, None].broadcast_to([128, 8, HD]),
                    ALU.mult), waits=[l2, e1, g("ytm_sp", cg - 2)], inc=DV)
                G[("q_yb", cg * 4 + q)] = yb
                ya = s.emit("dv", lambda e, ytm=ytm, qs=qs, pa=pa:
                    e.tensor_tensor(
                    ytm[:, qs], pa[:, 0:512], yscr[:, 0:512], ALU.add),
                    waits=[yb, l1], inc=DV)
                G[("q_ya", cg * 4 + q)] = ya
                hm = s.emit("po", lambda e, q=q, qs=qs, gam=gam: e.tensor_tensor(
                    hscr.rearrange("p (j w) -> p j w", j=8),
                    Hst[:, qs].rearrange("p (j w) -> p j w", j=8),
                    gam[:, q * 8:(q + 1) * 8, None].broadcast_to([128, 8, HD]),
                    ALU.mult), waits=[e3, g("hadd_prev"), P1Z], inc=PO)
                ha = s.emit("dv", lambda e, qs=qs: e.tensor_tensor(
                    Hst[:, qs], hscr[:, 0:512], psM[:, 0:512], ALU.add),
                    waits=[hm, l3], inc=DV)
                G[("q_ha", cg * 4 + q)] = ha
                G[("hadd_prev",)] = ha
                hw = s.emit("ac", lambda e, qs=qs, hbw=hbw: e.copy(
                    hbw[:, qs], Hst[:, qs]),
                    waits=[ha, g("hb_r", cg - 1, q)], inc=AC)
                G[("hb_w", cg, q)] = hw
                if cg == NCH - 1 and q == 3:
                    G[("h_final",)] = ha
            rotst["tok"][0] = G[("q_ya", cg * 4 + 3)]
            rotst["tok"][1] = G[("q_yb", cg * 4 + 3)]
            G[("psm_ra", cg)] = G[("q_ha", cg * 4 + 3)]
            G[("qend_dv", cg)] = s.tok(DV)
            dxt = s.emit("po", lambda e, xtm=xtm, xdz=xdz: e.tensor_tensor(
                xdz.rearrange("p (h w) -> p h w", h=NH),
                xtm.rearrange("p (h w) -> p h w", h=NH),
                db32[:, :, None].broadcast_to([128, NH, HD]), ALU.mult),
                waits=[LD0, g("xdz_r", cg)], inc=PO)
            G[("xtm_r", cg)] = dxt
            ydx = s.emit("po", lambda e, ytm=ytm, xdz=xdz: e.tensor_tensor(
                ytm[:], ytm[:], xdz[:], ALU.add),
                waits=[dxt, s.tok(DV)], inc=PO)
            G[("dxbf_r", cg)] = ydx
            G[("qend_po", cg)] = s.tok(PO)
            s.emit("sp", lambda e, ytm=ytm, cg=cg: e.dma_start(
                out=y_sp[cg], in_=ytm[:]), waits=[ydx], inc=DY[par], amt=16)
            G[("ytm_sp", cg)] = s.tok(DY[par])

        p1_prep(0)
        p1_prep(1)
        p1_quarters(0)
        p1_quarters(1)

    wz_v = wp1_v[:, :, 0:DIN]
    if LIMIT_PHASE >= 2:
        for k in range(KD):
            s.emit("po", lambda e, k=k: e.dma_start(
                out=wz_v[:, k, :], in_=w_z[k * 128:(k + 1) * 128, :]),
                waits=[g("xlf_r", NST - 1)], inc=DPO, amt=16)
    WZTOK = s.tok(DPO)
    s.emit("sp", lambda e: e.dma_start(out=mout[:, 0:DIN], in_=Hst[:]),
           waits=[g("h_final")], inc=DSP, amt=16)
    s.emit("sp", lambda e: e.dma_start(out=mout[:, DIN:DIN + NH],
           in_=atile[:]), waits=[g("la")], inc=DSP, amt=16)
    MOUT = s.tok(DSP)
    tap("sdt", lambda: s_dt[:], waits=[s.tok(DV)])
    tap("sca", lambda: s_ca[:], waits=[s.tok(DV)])
    tap("cadt", lambda: cadt[:], waits=[s.tok(DV)])
    p1.close()

    if LIMIT_PHASE >= 2:
        PH1 = [s.tok(sm) for sm in ("pe", "dv", "ac", "po", "dsp",
                                    "dx0", "dx1", "dy0", "dy1")]
        for eng in ("pe", "dv", "ac", "po", "sp"):
            s.emit(eng, lambda e: e.nop(), waits=PH1)
        ccd = s.emit("po", lambda e: e.collective_compute(
            "AllGather", ALU.bypass,
            replica_groups=[[0, 1, 2, 3], [4, 5, 6, 7]],
            ins=[mout.opt()], outs=[mg.opt()]), waits=[MOUT], inc=CC)

        p2 = ExitStack()
        DEAD_B = 18432 + 8192 + 4096 + 4096 + 3 * 1024
        mga = sb(p2, "mga", [128, 4 * (DIN + NH)], F32)
        hinb = sb(p2, "hinb", [128, DIN], BF16)
        used = 4 * (DIN + NH) * 4 + 4096
        assert used <= DEAD_B, used
        if DEAD_B - used:
            sb(p2, "p2pad0", [128, (DEAD_B - used) // 2], BF16)
        sb(p2, "p2pad1", [128, KD * P1E], BF16)
        wo = sb(p2, "wo", [128, 16 * D], BF16)
        hin = sb(p2, "hin", [128, DIN], F32)
        msel = sb(p2, "msel", [128, NH], F32)
        p2xin0 = sb(p2, "p2xin0", [128, D], F32)
        p2xin1 = sb(p2, "p2xin1", [128, D], F32)
        p2xln0 = sb(p2, "p2xln0", [128, D], BF16)
        p2xln1 = sb(p2, "p2xln1", [128, D], BF16)
        xlf2a = sb(p2, "xlf2a", [128, KD * 128], BF16)
        xlf2b = sb(p2, "xlf2b", [128, KD * 128], BF16)
        yld0 = sb(p2, "yld0", [128, DIN], F32)
        yld1 = sb(p2, "yld1", [128, DIN], F32)
        gtm0 = sb(p2, "gtm0", [128, DIN], F32)
        gtm1 = sb(p2, "gtm1", [128, DIN], F32)
        eog = sb(p2, "eog", [128, NH], BF16)
        yg = sb(p2, "yg", [128, DIN], F32)
        yscr2 = sb(p2, "yscr2", [128, 512], F32)
        yn = sb(p2, "yn", [128, DIN], BF16)
        ynf = sb(p2, "ynf", [128, 16 * 128], BF16)
        xmid0 = sb(p2, "xmid0", [128, D], F32)
        xmid1 = sb(p2, "xmid1", [128, D], F32)
        stat2 = sb(p2, "stat2", [128, 4], F32)

        wo_v = wo.rearrange("p (c d) -> p c d", c=16)
        ynf_v = ynf.rearrange("p (c t) -> p c t", c=16)
        xlf2_vs = [xlf2a.rearrange("p (k t) -> p k t", k=KD),
                   xlf2b.rearrange("p (k t) -> p k t", k=KD)]
        p2xins = [p2xin0, p2xin1]
        p2xlns = [p2xln0, p2xln1]
        ylds = [yld0, yld1]
        xmids = [xmid0, xmid1]
        mga_v = mga.rearrange("p (r c) -> p r c", r=4)
        hz = s.emit("po", lambda e: e.memset(hin[:], 0.0), inc=PO)

        for c in range(16):
            s.emit("po", lambda e, c=c: e.dma_start(
                out=wo_v[:, c, :], in_=w_out[c * 128:(c + 1) * 128, :]),
                inc=DPO, amt=16)
        W2 = s.tok(DPO)
        nwt = None
        for c in range(16):
            nwt = s.emit("dv", lambda e, c=c: e.tensor_scalar(
                out=wo_v[:, c, :], in0=wo_v[:, c, :],
                scalar1=nw_sb[:, c:c + 1], scalar2=None, op0=ALU.mult),
                waits=[W2, LD0], inc=DV)
        WOUT = nwt

        def p2_combine():
            s.emit("po", lambda e: e.dma_start(
                out=mga_v[:, :, :],
                in_=mg.rearrange("(r p) c -> p r c", p=128)),
                waits=[ccd], inc=DPO, amt=16)
            mjt = s.tok(DPO)
            prev = None
            for j in range(4):
                ms = s.emit("dv", lambda e, j=j: e.tensor_scalar(
                    out=msel[:], in0=mga_v[:, j, DIN:DIN + NH],
                    scalar1=selb[:, 2 * j:2 * j + 1],
                    scalar2=selb1m[:, 2 * j:2 * j + 1],
                    op0=ALU.mult, op1=ALU.add), waits=[mjt, LD0], inc=DV)
                hm2 = s.emit("dv", lambda e: e.tensor_tensor(
                    hin.rearrange("p (h w) -> p h w", h=NH),
                    hin.rearrange("p (h w) -> p h w", h=NH),
                    msel[:, :, None].broadcast_to([128, NH, HD]), ALU.mult),
                    waits=[ms, hz], inc=DV)
                prev = s.emit("dv", lambda e, j=j: e.scalar_tensor_tensor(
                    out=hin[:], in0=mga_v[:, j, 0:DIN],
                    scalar=selb[:, 2 * j + 1:2 * j + 2], in1=hin[:],
                    op0=ALU.mult, op1=ALU.add), waits=[hm2], inc=DV)
            hib = s.emit("dv", lambda e: e.tensor_copy(hinb[:], hin[:]),
                         waits=[prev], inc=DV)
            tap("hin", lambda: hin[:], waits=[prev])
            return hib

        def p2_front(cg):
            par = cg % 2
            xin, xln = p2xins[par], p2xlns[par]
            yld = ylds[par]
            s.emit("sp", lambda e, xin=xin, cg=cg: e.dma_start(
                out=xin[:], in_=xs[cg * CH:(cg + 1) * CH, :]),
                waits=[g("p2xin_r", cg - 2)], inc=DX[par], amt=16)
            s.emit("sp", lambda e, yld=yld, cg=cg: e.dma_start(
                out=yld[:], in_=y_sp[cg]),
                waits=[g("yld_r", cg - 2)], inc=DY[par], amt=16)
            G[("p2ylt", cg)] = s.tok(DY[par])
            xt = s.tok(DX[par])
            lt = ln_chunk(xin[:], xln[:], par, [xt])
            wl = [lt, psT2_g[0], g("xlf2_r", cg - 2), g("p2xln_r", cg - 2)]
            last = None
            for k in range(KD):
                last = s.emit("pe", lambda e, k=k, xln=xln: e.transpose(
                    psT2[:, k * 128:(k + 1) * 128],
                    xln[:, k * 128:(k + 1) * 128], identb[:]),
                    waits=wl if k == 0 else (), inc=PE if k == KD - 1 else None)
            G[("p2xln_r", cg)] = last
            ev = s.emit("dv", lambda e, cg=cg: e.tensor_copy(
                xlf2_vs[cg % 2][:, :, :],
                psT2.rearrange("p (k t) -> p k t", t=128)),
                waits=[last], inc=DV)
            psT2_g[0] = ev
            G[("p2ev", cg)] = ev

        def p2_z(cg):
            par = cg % 2
            xlf2_v = xlf2_vs[par]
            ev = G[("p2ev", cg)]
            gtm = [gtm0, gtm1][par]
            zev = []
            for gi in range(4):
                ps, guard, ri = rot_ps()
                lastz = None
                for k in range(KD):
                    lastz = s.emit("pe", lambda e, k=k, gi=gi, ps=ps,
                                   xlf2_v=xlf2_v: e.matmul(
                        ps[:, 0:512], xlf2_v[:, k, :],
                        wz_v[:, k, gi * 512:(gi + 1) * 512],
                        start=(k == 0), stop=(k == KD - 1)),
                        waits=[ev, WZTOK, guard] if k == 0 else (),
                        inc=PE if k == KD - 1 else None)
                zv = s.emit("ac", lambda e, gi=gi, ps=ps, gtm=gtm: e.activation(
                    gtm[:, gi * 512:(gi + 1) * 512], ps[:, 0:512], AF.Silu),
                    waits=[lastz, g("gtm_r", cg - 2)], inc=AC)
                zev.append(zv)
                rot_done(ri, zv)
            G[("xlf2_r", cg)] = s.tok(PE)
            G[("p2zev", cg)] = zev

        DYPS = (psY1, psY2, psM, psY2)

        def dy_q(cg, q, wl):
            psq = DYPS[q]
            l2 = None
            for j in range(8):
                h = q * 8 + j
                l2 = s.emit("pe", lambda e, h=h, j=j, psq=psq, cg=cg:
                    e.matmul(psq[:, j * 64:(j + 1) * 64], call_v[:, cg, :],
                    hinb[:, h * HD:(h + 1) * HD], start=True, stop=True),
                    waits=wl if j == 0 else (), inc=PE if j == 7 else None)
            G[("dy", cg, q)] = l2

        def dy_pre(cg, hib):
            G[("eg", cg)] = s.emit("ac", lambda e, cg=cg: e.activation(
                eog[:], cadt_v[:, cg, 0:32], AF.Exp),
                waits=[g("eog_r", cg - 1)], inc=AC)
            dy_q(cg, 0, [hib, g("t1", cg - 1, 0)])
            dy_q(cg, 1, [hib, g("t1", cg - 1, 3)])
            dy_q(cg, 2, [hib, g("t1", cg - 1, 2)])

        p2_front(0)
        p2_z(0)
        p2_front(1)
        p2_z(1)
        hib = p2_combine()
        dy_pre(0, hib)
        for cg in range(NCH):
            par = cg % 2
            xin = p2xins[par]
            yld = ylds[par]
            xmid = xmids[par]
            gtm = [gtm0, gtm1][par]
            ev = G[("p2ev", cg)]
            ylt = G[("p2ylt", cg)]
            zev = G[("p2zev", cg)]
            if cg + 1 < NCH and ("p2zev", cg + 1) not in G:
                p2_z(cg + 1)
            eg = G[("eg", cg)]
            ygt = None
            sql = None
            for q in range(4):
                qs = slice(q * 512, (q + 1) * 512)
                psq = DYPS[q]
                t1q = s.emit("dv", lambda e, q=q, psq=psq: e.tensor_tensor(
                    yscr2.rearrange("p (j w) -> p j w", j=8),
                    psq.rearrange("p (j w) -> p j w", j=8),
                    eog[:, q * 8:(q + 1) * 8, None].broadcast_to([128, 8, HD]),
                    ALU.mult), waits=[G[("dy", cg, q)], eg], inc=DV)
                G[("t1", cg, q)] = t1q
                t2q = s.emit("dv", lambda e, qs=qs, yld=yld: e.tensor_tensor(
                    yscr2[:, 0:512], yscr2[:, 0:512], yld[:, qs], ALU.add),
                    waits=[t1q, ylt], inc=DV)
                ygt = s.emit("dv", lambda e, qs=qs, gtm=gtm: e.tensor_tensor(
                    yn[:, qs], yscr2[:, 0:512], gtm[:, qs], ALU.mult),
                    waits=[t2q, zev[q], g("yn_r", cg - 1)], inc=DV)
                sql = s.emit("ac", lambda e, q=q, qs=qs: e.activation(
                    yg[:, qs], yn[:, qs], AF.Square,
                    accum_out=stat[:, 34 + q:35 + q]), waits=[ygt], inc=AC)
                if q == 1:
                    dy_q(cg, 3, [G[("t1", cg, 1)]])
            G[("yld_r", cg)] = ygt
            G[("eog_r", cg)] = G[("t1", cg, 3)]
            G[("gtm_r", cg)] = ygt
            if cg + 1 < NCH:
                dy_pre(cg + 1, hib)
            acc = s.emit("ac", lambda e: e.activation(
                stat2[:, 0:4], stat[:, 34:38], AF.Copy,
                accum_out=stat[:, 32:33]), waits=[sql], inc=AC)
            r2 = s.emit("ac", lambda e: e.activation(stat[:, 33:34],
                        stat[:, 32:33], AF.Ln, scale=1.0 / DIN,
                        bias=eps_t[:, 0:1]),
                        waits=[acc, g("rms_r", cg - 1)], inc=AC)
            r3 = s.emit("ac", lambda e: e.activation(stat[:, 33:34],
                        stat[:, 33:34], AF.Exp, scale=-0.5), waits=[r2], inc=AC)
            wl = [ygt, psT_g[0]]
            last = None
            for i in range(16):
                last = s.emit("pe", lambda e, i=i: e.transpose(
                    psT[:, i * 128:(i + 1) * 128], yn[:, i * 128:(i + 1) * 128],
                    identb[:]), waits=wl if i == 0 else (),
                    inc=PE if i == 15 else None)
            G[("yn_r", cg)] = last
            evy = s.emit("dv", lambda e: e.tensor_copy(ynf[:], psT[:]),
                         waits=[last], inc=DV)
            psT_g[0] = evy
            xr = None
            for gi in range(2):
                ps, guard, ri = rot_ps()
                lasto = None
                for c in range(16):
                    lasto = s.emit("pe", lambda e, c=c, gi=gi, ps=ps: e.matmul(
                        ps[:, 0:512], ynf_v[:, c, :],
                        wo_v[:, c, gi * 512:(gi + 1) * 512],
                        start=(c == 0), stop=(c == 15)),
                        waits=[evy, WOUT, guard] if c == 0 else (),
                        inc=PE if c == 15 else None)
                xr = s.emit("dv", lambda e, gi=gi, ps=ps, xmid=xmid, xin=xin:
                    e.scalar_tensor_tensor(
                    out=xmid[:, gi * 512:(gi + 1) * 512], in0=ps[:, 0:512],
                    scalar=stat[:, 33:34], in1=xin[:, gi * 512:(gi + 1) * 512],
                    op0=ALU.mult, op1=ALU.add),
                    waits=[lasto, r3, g("xmid_sp", cg - 2)], inc=DV)
                rot_done(ri, xr)
            G[("p2xin_r", cg)] = xr
            G[("rms_r", cg)] = xr
            s.emit("sp", lambda e, xmid=xmid, cg=cg: e.dma_start(
                out=xm_sp[cg], in_=xmid[:]), waits=[xr], inc=DZ[par], amt=16)
            G[("xmid_sp", cg)] = s.tok(DZ[par])
            if cg + 2 < NCH:
                p2_front(cg + 2)
        tap("gtm", lambda: gtm1[:], waits=[s.tok(AC)])
        tap("yg", lambda: yg[:], waits=[s.tok(DV)])
        p2.close()
        p12.close()

        if LIMIT_PHASE >= 3:
            PH2 = [s.tok(sm) for sm in ("pe", "dv", "ac", "po", "dsp",
                                        "dx0", "dx1", "dy0", "dy1",
                                        "dz0", "dz1")]
            for eng in ("pe", "dv", "ac", "po", "sp"):
                s.emit(eng, lambda e: e.nop(), waits=PH2)
            p3 = ExitStack()
            wfc = sb(p3, "wfc", [128, KD * DFF], BF16)
            wpr = sb(p3, "wpr", [128, NFC * D], BF16)
            xm0 = sb(p3, "xm0", [128, D], F32)
            xm1 = sb(p3, "xm1", [128, D], F32)
            xm2 = sb(p3, "xm2", [128, D], F32)
            xm3 = sb(p3, "xm3", [128, D], F32)
            xl20 = sb(p3, "xl20", [128, D], BF16)
            xl21 = sb(p3, "xl21", [128, D], BF16)
            xlf3a = sb(p3, "xlf3a", [128, KD * ST], BF16)
            xlf3b = sb(p3, "xlf3b", [128, KD * ST], BF16)
            rlb = sb(p3, "rlb", [128, ST], BF16)
            hbuf = sb(p3, "hbuf", [128, NFC * ST], BF16)
            y2 = sb(p3, "y2", [128, D], F32)
            spka = sb(p3, "spka", [128, D], BF16)
            spkb = sb(p3, "spkb", [128, D], BF16)
            outb0 = sb(p3, "outb0", [128, D], F32)
            outb1 = sb(p3, "outb1", [128, D], F32)

            wfc_v = wfc.rearrange("p (k f) -> p k f", k=KD)
            wpr_v = wpr.rearrange("p (f d) -> p f d", f=NFC)
            xlf3_vs = [xlf3a.rearrange("p (k t) -> p k t", k=KD),
                       xlf3b.rearrange("p (k t) -> p k t", k=KD)]
            hbuf_v = hbuf.rearrange("p (f t) -> p f t", f=NFC)
            xms = [xm0, xm1, xm2, xm3]
            xl2s = [xl20, xl21]
            outbs = [outb0, outb1]

            for k in range(KD):
                s.emit("po", lambda e, k=k: e.dma_start(
                    out=wfc_v[:, k, :], in_=w_fc[k * 128:(k + 1) * 128, :]),
                    inc=DPO, amt=16)
            WFC = s.tok(DPO)
            for f in range(NFC):
                s.emit("po", lambda e, f=f: e.dma_start(
                    out=wpr_v[:, f, :], in_=w_pr[f * 128:(f + 1) * 128, :]),
                    inc=DPO, amt=16)
            WPR = s.tok(DPO)

            def p3_front(st):
                ch0 = st * CPS
                xlf3_v = xlf3_vs[st % 2]
                evs = []
                for c2 in range(CPS):
                    cg = ch0 + c2
                    par = cg % 2
                    xm, xl2 = xms[cg % 4], xl2s[par]
                    s.emit("sp", lambda e, xm=xm, cg=cg: e.dma_start(
                        out=xm[:], in_=xm_sp[cg]),
                        waits=[g("xm_r", cg - 4)], inc=DY[par], amt=16)
                    xt = s.tok(DY[par])
                    lt = ln_chunk(xm[:], xl2[:], par, [xt])
                    wl = [lt, psT2_g[0], g("xlf3_r", st - 2), g("xl2_r", cg - 2)]
                    last = None
                    for k in range(KD):
                        last = s.emit("pe", lambda e, k=k, xl2=xl2: e.transpose(
                            psT2[:, k * 128:(k + 1) * 128],
                            xl2[:, k * 128:(k + 1) * 128], identb[:]),
                            waits=wl if k == 0 else (),
                            inc=PE if k == KD - 1 else None)
                    G[("xl2_r", cg)] = last
                    ev = s.emit("dv", lambda e, c2=c2, xlf3_v=xlf3_v:
                        e.tensor_copy(
                        xlf3_v[:, :, c2 * CH:(c2 + 1) * CH],
                        psT2.rearrange("p (k t) -> p k t", t=128)),
                        waits=[last], inc=DV)
                    psT2_g[0] = ev
                    evs.append(ev)
                G[("p3front", st)] = evs

            p3_front(0)
            for st in range(NST):
                ch0 = st * CPS
                xlf3_v = xlf3_vs[st % 2]
                evs = G[("p3front", st)]
                for f in range(NFC):
                    ps, guard, ri = rot_ps()
                    lastf = None
                    for k in range(KD):
                        lastf = s.emit("pe", lambda e, k=k, f=f, ps=ps,
                                       xlf3_v=xlf3_v: e.matmul(
                            ps[:, 0:ST], wfc_v[:, k, f * 128:(f + 1) * 128],
                            xlf3_v[:, k, :], start=(k == 0), stop=(k == KD - 1)),
                            waits=evs + [WFC, guard] if k == 0 else (),
                            inc=PE if k == KD - 1 else None)
                    rl = s.emit("ac", lambda e, ps=ps: e.activation(
                        rlb[:], ps[:, 0:ST], AF.Relu),
                        waits=[lastf, g("rlb_r")], inc=AC)
                    hev = s.emit("dv", lambda e, f=f: e.tensor_tensor(
                        hbuf_v[:, f, :], rlb[:], rlb[:], ALU.mult),
                        waits=[rl, g("hbuf_r", st - 1)], inc=DV)
                    G[("rlb_r",)] = hev
                    rot_done(ri, hev)
                if st + 1 < NST:
                    p3_front(st + 1)
                HEV = s.tok(DV)
                for c2 in range(CPS):
                    cg = ch0 + c2
                    par = cg % 2
                    xm = xms[cg % 4]
                    outb = outbs[par]
                    for gi in range(2):
                        psq = psY1 if gi == 0 else psY2
                        lasto = None
                        for f in range(NFC):
                            lasto = s.emit("pe",
                                lambda e, f=f, gi=gi, psq=psq, c2=c2: e.matmul(
                                    psq[:, 0:512],
                                    hbuf_v[:, f, c2 * CH:(c2 + 1) * CH],
                                    wpr_v[:, f, gi * 512:(gi + 1) * 512],
                                    start=(f == 0), stop=(f == NFC - 1)),
                                waits=[HEV, WPR, g("p3ps_r", gi)]
                                if f == 0 else (),
                                inc=PE if f == NFC - 1 else None)
                        yv = s.emit("ac", lambda e, gi=gi, psq=psq: e.copy(
                            y2[:, gi * 512:(gi + 1) * 512], psq[:, 0:512]),
                            waits=[lasto, g("y2_r", cg - 1)], inc=AC)
                        G[("p3ps_r", gi)] = yv
                    Y2 = s.tok(AC)
                    sp0 = s.emit("dv", lambda e: e.tensor_scalar(
                        out=spka[:], in0=y2[:], scalar1=SPIKE_B[0],
                        scalar2=None, op0=ALU.is_gt),
                        waits=[Y2, g("spka_r", cg - 2)], inc=DV)
                    acc = sp0
                    for j in range(1, 5):
                        sj = s.emit("dv", lambda e, j=j: e.tensor_scalar(
                            out=spkb[:], in0=y2[:], scalar1=SPIKE_B[j],
                            scalar2=None, op0=ALU.is_gt),
                            waits=[Y2, g("spkb_r")], inc=DV)
                        acc = s.emit("dv", lambda e: e.tensor_tensor(
                            spka[:], spka[:], spkb[:], ALU.add),
                            waits=[sj, acc], inc=DV)
                        G[("spkb_r",)] = acc
                    G[("y2_r", cg)] = acc
                    ob = s.emit("dv", lambda e, outb=outb, xm=xm:
                        e.scalar_tensor_tensor(out=outb[:], in0=spka[:],
                        scalar=0.2, in1=xm[:], op0=ALU.mult, op1=ALU.add),
                        waits=[acc, g("outb_sp", cg - 2)], inc=DV)
                    G[("spka_r", cg)] = ob
                    G[("xm_r", cg)] = ob
                    s.emit("sp", lambda e, outb=outb, cg=cg: e.dma_start(
                        out=out[cg * CH:(cg + 1) * CH, :], in_=outb[:]),
                        waits=[ob], inc=DZ[par], amt=16)
                    G[("outb_sp", cg)] = s.tok(DZ[par])
                G[("xlf3_r", st)] = s.tok(PE)
                G[("hbuf_r", st)] = s.tok(PE)
            tap("y2t", lambda: y2[:], waits=[s.tok(AC)])
            tap("spk", lambda: spka[:], waits=[s.tok(DV)], q="po")
            tap("hbt", lambda: hbuf[:], waits=[s.tok(DV)], q="po")
            p3.close()

    if LIMIT_PHASE < 2:
        p12.close()
    fin = [s.tok(sm) for sm in ("dsp", "dx0", "dx1", "dy0", "dy1",
                                "dz0", "dz1", "dpo")]
    if LIMIT_PHASE >= 2:
        fin.append(s.tok("cc"))
    s.emit("sp", lambda e: e.nop(), waits=fin)

    with nc.Block() as block:
        def mk(eng_name):
            def body(eng):
                for fn, waits, inc, amt in s.ops[eng_name]:
                    for sm, v in waits:
                        eng.wait_ge(sems[sm], v)
                    r = fn(eng)
                    if inc is not None:
                        r.then_inc(sems[inc], amt)
            return body
        block.tensor(mk("pe"))
        block.vector(mk("dv"))
        block.scalar(mk("ac"))
        block.gpsimd(mk("po"))
        block.sync(mk("sp"))

    oes.close()
    return nc


_CACHE = {}


def _get_nc():
    key = (LIMIT_PHASE, tuple(sorted(map(str, TAPS))))
    if key not in _CACHE:
        _CACHE[key] = build_nc()
    return _CACHE[key]


def kernel(L0_attn_X, L0_attn_W_in, L0_attn_conv_w, L0_attn_conv_b,
           L0_attn_dt_bias, L0_attn_A_log, L0_attn_D, L0_attn_norm_w,
           L0_attn_W_out, L0_ffn_W_fc, L0_ffn_W_proj):
    X = np.asarray(L0_attn_X, np.float32)
    W_in = np.asarray(L0_attn_W_in, np.float32)
    conv_w = np.asarray(L0_attn_conv_w, np.float32)
    conv_b = np.asarray(L0_attn_conv_b, np.float32)
    dt_bias = np.asarray(L0_attn_dt_bias, np.float32)
    A_log = np.asarray(L0_attn_A_log, np.float32)
    Dv = np.asarray(L0_attn_D, np.float32)
    norm_w = np.asarray(L0_attn_norm_w, np.float32)
    W_out = np.asarray(L0_attn_W_out, np.float32)
    W_fc = np.asarray(L0_ffn_W_fc, np.float32)
    W_proj = np.asarray(L0_ffn_W_proj, np.float32)

    w_p1h = np.ascontiguousarray(W_in[DIN:, :].T)
    w_zh = np.ascontiguousarray(W_in[:DIN, :].T)
    w_oh = np.ascontiguousarray(W_out.T)
    w_fh = np.ascontiguousarray(W_fc.T)
    w_ph = np.ascontiguousarray(W_proj.T)
    cw = np.ascontiguousarray(
        conv_w.reshape(NEC, 128, 4).transpose(1, 0, 2).reshape(128, NEC * 4))
    cb = np.ascontiguousarray(conv_b.reshape(NEC, 128).T)
    dtb = np.ascontiguousarray(dt_bias.reshape(NH, 1))
    alog = np.ascontiguousarray(A_log.reshape(NH, 1))
    dvec = np.ascontiguousarray(np.broadcast_to(Dv[None, :], (128, NH)))
    nw = np.ascontiguousarray(norm_w.reshape(16, 128).T)

    in_maps = []
    for k in range(NCORES):
        b, q = divmod(k, 4)
        xs_k = np.ascontiguousarray(X[b, q * TPC:(q + 1) * TPC])
        if q == 0:
            xp = np.zeros((3, D), np.float32)
        else:
            xp = np.ascontiguousarray(X[b, q * TPC - 3:q * TPC])
        sel = np.zeros((1, 8), np.float32)
        for j in range(4):
            if j < q:
                sel[0, 2 * j] = 1.0
                sel[0, 2 * j + 1] = 1.0
        selm = 1.0 - sel
        in_maps.append({
            "xs": xs_k, "xprev": xp, "w_p1": w_p1h, "w_z": w_zh,
            "w_out": w_oh, "w_fc": w_fh, "w_pr": w_ph, "cw": cw, "cb": cb,
            "dtb": dtb, "alog": alog, "dvec": dvec, "nw": nw,
            "sel": np.ascontiguousarray(np.broadcast_to(sel, (128, 8))),
            "selm": np.ascontiguousarray(np.broadcast_to(selm, (128, 8))),
        })

    nc = _get_nc()
    res = run_bass_kernel_spmd(nc, in_maps, core_ids=list(range(NCORES)))
    globals()["LAST_RES"] = res
    outp = np.empty((B, S, D), np.float32)
    for k in range(NCORES):
        b, q = divmod(k, 4)
        outp[b, q * TPC:(q + 1) * TPC] = res.results[k]["out"]
    return outp


# version 18
# speedup vs baseline: 1.1271x; 1.0414x over previous
from contextlib import ExitStack

import numpy as np
import concourse.bass as bass
import concourse.mybir as mybir
from concourse.bass_utils import run_bass_kernel_spmd

F32 = mybir.dt.float32
BF16 = mybir.dt.bfloat16
AF = mybir.ActivationFunctionType
ALU = mybir.AluOpType

B, S, D = 2, 8192, 1024
NCORES = 8
TPC = 2048
CH = 128
NCH = TPC // CH
ST = 256
NST = TPC // ST
CPS = ST // CH
DIN = 2048
NH = 32
HD = 64
NEC = 18
P1E = NEC * 128 + NH
KD = D // 128
DFF = 4096
NFC = DFF // 128
EPS = 1e-5

TAPS = {}
LIMIT_PHASE = 3


def _spike_thresholds():
    def count(y):
        y = np.float32(y)
        mem = np.float32(0.0)
        c = 0
        for _ in range(5):
            reset = np.float32(1.0) if mem > np.float32(1.0) else np.float32(0.0)
            mem = np.float32(np.float32(np.float32(0.9) * mem) + y) - reset
            c += 1 if mem > np.float32(1.0) else 0
        return c

    bs = []
    for j in range(1, 6):
        lo, hi = -4.0, 4.0
        for _ in range(60):
            mid = (lo + hi) / 2
            if count(mid) >= j:
                hi = mid
            else:
                lo = mid
        bs.append(float(np.float32(lo)))
    return bs


SPIKE_B = _spike_thresholds()


class Sched:

    ENG = ("pe", "dv", "ac", "po", "sp")

    def __init__(self):
        self.ops = {e: [] for e in self.ENG}
        self.counts = {}
        self.waited = {e: {} for e in self.ENG}

    def sem(self, name):
        self.counts.setdefault(name, 0)
        return name

    def emit(self, eng, fn, waits=(), inc=None, amt=1):
        pruned = []
        for w in waits:
            if w is None:
                continue
            sm, v = w
            assert v <= self.counts[sm], f"wait on future {sm}:{v}>{self.counts[sm]}"
            if v > self.waited[eng].get(sm, 0):
                self.waited[eng][sm] = v
                pruned.append((sm, v))
        self.ops[eng].append((fn, pruned, inc, amt))
        if inc is not None:
            self.counts[inc] += amt
            return (inc, self.counts[inc])
        return None

    def tok(self, sem):
        return (sem, self.counts[sem])


def build_nc():
    nc = bass.Bass("TRN2", target_bir_lowering=False, debug=False)
    s = Sched()
    oes = ExitStack()

    xs = nc.dram_tensor("xs", [TPC, D], F32, kind="ExternalInput").ap()
    xprev = nc.dram_tensor("xprev", [3, D], F32, kind="ExternalInput").ap()
    w_p1 = nc.dram_tensor("w_p1", [D, P1E], F32, kind="ExternalInput").ap()
    w_z = nc.dram_tensor("w_z", [D, DIN], F32, kind="ExternalInput").ap()
    w_out = nc.dram_tensor("w_out", [DIN, D], F32, kind="ExternalInput").ap()
    w_fc = nc.dram_tensor("w_fc", [D, DFF], F32, kind="ExternalInput").ap()
    w_pr = nc.dram_tensor("w_pr", [DFF, D], F32, kind="ExternalInput").ap()
    cw_in = nc.dram_tensor("cw", [128, NEC * 4], F32, kind="ExternalInput").ap()
    cb_in = nc.dram_tensor("cb", [128, NEC], F32, kind="ExternalInput").ap()
    dtb_in = nc.dram_tensor("dtb", [NH, 1], F32, kind="ExternalInput").ap()
    alog_in = nc.dram_tensor("alog", [NH, 1], F32, kind="ExternalInput").ap()
    db_in = nc.dram_tensor("dvec", [128, NH], F32, kind="ExternalInput").ap()
    nw_in = nc.dram_tensor("nw", [128, 16], F32, kind="ExternalInput").ap()
    sel_in = nc.dram_tensor("sel", [128, 8], F32, kind="ExternalInput").ap()
    selm_in = nc.dram_tensor("selm", [128, 8], F32, kind="ExternalInput").ap()
    out = nc.dram_tensor("out", [TPC, D], F32, kind="ExternalOutput").ap()

    spk = "ExternalOutput" if "spills" in TAPS else "Internal"
    y_sp = nc.dram_tensor("y_sp", [NCH, 128, DIN], F32, kind=spk).ap()
    xm_sp = nc.dram_tensor("xm_sp", [NCH, 128, D], F32, kind=spk).ap()
    mout = nc.dram_tensor("mout", [128, DIN + NH], F32,
                          kind="ExternalOutput" if "mout" in TAPS else "Internal").ap()
    mg = nc.dram_tensor("mg", [4 * 128, DIN + NH], F32, kind="Internal").ap()

    tapd = {}
    for name, shape in TAPS.items():
        if name in ("spills", "mout"):
            continue
        tapd[name] = nc.dram_tensor("tap_" + name, list(shape), F32,
                                    kind="ExternalOutput").ap()

    PE = s.sem("pe"); DV = s.sem("dv"); AC = s.sem("ac"); PO = s.sem("po")
    DSP = s.sem("dsp"); DPO = s.sem("dpo"); CC = s.sem("cc")
    DX = [s.sem("dx0"), s.sem("dx1")]
    DY = [s.sem("dy0"), s.sem("dy1")]
    DZ = [s.sem("dz0"), s.sem("dz1")]

    def tap(name, src_fn, waits=(), q="sp"):
        if name in tapd:
            s.emit(q, lambda e, n=name: e.dma_start(out=tapd[n], in_=src_fn()),
                   waits=waits, inc=DSP if q == "sp" else DPO, amt=16)

    def sb(es, name, shape, dt):
        return es.enter_context(nc.sbuf_tensor(name, shape, dt))

    identb = sb(oes, "identb", [128, 128], BF16)
    identf = sb(oes, "identf", [128, 128], F32)
    stepb = sb(oes, "stepb", [128, 128], BF16)
    cmaskb = sb(oes, "cmaskb", [128, 128], BF16)
    e127 = sb(oes, "e127", [128, 128], F32)
    zero32 = sb(oes, "zero32", [128, 32], F32)
    onescol = sb(oes, "onescol", [32, 1], F32)
    smallcol = sb(oes, "smallcol", [NH, 4], F32)
    db32 = sb(oes, "db32", [128, NH], F32)
    selb = sb(oes, "selb", [128, 8], F32)
    selb1m = sb(oes, "selb1m", [128, 8], F32)
    cw_sb = sb(oes, "cw_sb", [128, NEC * 4], F32)
    cb_sb = sb(oes, "cb_sb", [128, NEC], F32)
    nw_sb = sb(oes, "nw_sb", [128, 16], F32)
    stat = sb(oes, "stat", [128, 40], F32)
    carry = sb(oes, "carry", [32, 1], F32)
    eps_t = sb(oes, "eps_t", [128, 1], F32)
    p12 = ExitStack()
    cadt = sb(p12, "cadt", [128, NCH * 96], F32)
    c_all = sb(p12, "c_all", [128, NCH * 128], BF16)
    psA = oes.enter_context(nc.psum_tensor("psA", [128, 512], F32))
    psB = oes.enter_context(nc.psum_tensor("psB", [128, 512], F32))
    psY1 = oes.enter_context(nc.psum_tensor("psY1", [128, 512], F32))
    psY2 = oes.enter_context(nc.psum_tensor("psY2", [128, 512], F32))
    psM = oes.enter_context(nc.psum_tensor("psM", [128, 512], F32))
    psT = oes.enter_context(nc.psum_tensor("psT", [128, 2048], BF16))
    psT2 = oes.enter_context(nc.psum_tensor("psT2", [128, 1024], BF16))
    sems = {}
    for nm in ("pe", "dv", "ac", "po", "dsp", "dpo", "cc", "dx0", "dx1",
               "dy0", "dy1", "dz0", "dz1"):
        sems[nm] = oes.enter_context(nc.semaphore(nm))

    cadt_v = cadt.rearrange("p (c w) -> p c w", w=96)
    call_v = c_all.rearrange("p (c t) -> p c t", t=128)

    rotst = {"i": 0, "tok": [None, None]}

    def rot_ps():
        i = rotst["i"]
        rotst["i"] ^= 1
        return (psA, psB)[i], rotst["tok"][i], i

    def rot_done(i, token):
        rotst["tok"][i] = token

    psT_g = [None]
    psT2_g = [None]

    s.emit("po", lambda e: e.memset(identb[:], 0.0))
    s.emit("po", lambda e: e.affine_select(out=identb[:], in_=identb[:],
           compare_op=ALU.not_equal, fill=1.0, base=0,
           pattern=[[-1, 128]], channel_multiplier=1))
    s.emit("po", lambda e: e.memset(identf[:], 0.0))
    s.emit("po", lambda e: e.affine_select(out=identf[:], in_=identf[:],
           compare_op=ALU.not_equal, fill=1.0, base=0,
           pattern=[[-1, 128]], channel_multiplier=1))
    s.emit("po", lambda e: e.memset(stepb[:], 1.0))
    s.emit("po", lambda e: e.affine_select(out=stepb[:], in_=stepb[:],
           compare_op=ALU.is_gt, fill=0.0, base=0,
           pattern=[[-1, 128]], channel_multiplier=1))
    s.emit("po", lambda e: e.memset(cmaskb[:], 1.0))
    s.emit("po", lambda e: e.affine_select(out=cmaskb[:], in_=cmaskb[:],
           compare_op=ALU.is_ge, fill=0.0, base=0,
           pattern=[[1, 128]], channel_multiplier=-1))
    s.emit("po", lambda e: e.memset(e127[:], 0.0))
    s.emit("po", lambda e: e.affine_select(out=e127[:], in_=e127[:],
           compare_op=ALU.not_equal, fill=1.0, base=-127,
           pattern=[[0, 128]], channel_multiplier=1))
    s.emit("po", lambda e: e.memset(zero32[:], 0.0))
    s.emit("po", lambda e: e.memset(onescol[:], 1.0))
    s.emit("po", lambda e: e.memset(carry[:], 0.0))
    s.emit("po", lambda e: e.memset(eps_t[:], EPS), inc=PO)
    CPO = s.tok(PO)

    s.emit("sp", lambda e: e.dma_start(out=cw_sb[:], in_=cw_in), inc=DSP, amt=16)
    s.emit("sp", lambda e: e.dma_start(out=cb_sb[:], in_=cb_in), inc=DSP, amt=16)
    s.emit("sp", lambda e: e.dma_start(out=nw_sb[:], in_=nw_in), inc=DSP, amt=16)
    s.emit("sp", lambda e: e.dma_start(out=db32[:], in_=db_in), inc=DSP, amt=16)
    s.emit("sp", lambda e: e.dma_start(out=selb[:], in_=sel_in), inc=DSP, amt=16)
    s.emit("sp", lambda e: e.dma_start(out=selb1m[:], in_=selm_in), inc=DSP, amt=16)
    s.emit("sp", lambda e: e.dma_start(out=smallcol[:, 0:1], in_=dtb_in,
           allow_slow_non_contiguous=True), inc=DSP, amt=16)
    s.emit("sp", lambda e: e.dma_start(out=smallcol[:, 1:2], in_=alog_in,
           allow_slow_non_contiguous=True), inc=DSP, amt=16)
    LD0 = s.tok(DSP)
    t = s.emit("ac", lambda e: e.activation(smallcol[:, 2:3],
               smallcol[:, 1:2], AF.Exp), waits=[LD0], inc=AC)
    A_TOK = s.emit("dv", lambda e: e.tensor_scalar(out=smallcol[:, 2:3],
                   in0=smallcol[:, 2:3], scalar1=-1.0, scalar2=None,
                   op0=ALU.mult), waits=[t], inc=DV)
    ln_prev = {}

    def ln_chunk(xin_ap, dst, par, waits):
        c0 = 16 * par
        nf = xin_ap.shape[-1]
        prev = ln_prev.get(par)
        wl = list(waits) + ([prev[0]] if prev else [])
        h = nf // 2
        t1 = s.emit("dv", lambda e: e.bn_stats(stat[:xin_ap.shape[0], c0:c0 + 6],
                    xin_ap[:, 0:h]), waits=wl, inc=DV)
        t1 = s.emit("dv", lambda e: e.bn_stats(
                    stat[:xin_ap.shape[0], c0 + 6:c0 + 12], xin_ap[:, h:nf]),
                    waits=[t1], inc=DV)
        t2 = s.emit("dv", lambda e: e.bn_aggr(
                    stat[:xin_ap.shape[0], c0 + 12:c0 + 14],
                    stat[:xin_ap.shape[0], c0:c0 + 12]), waits=[t1], inc=DV)
        t3 = s.emit("ac", lambda e: e.activation(
                    stat[:xin_ap.shape[0], c0 + 13:c0 + 14],
                    stat[:xin_ap.shape[0], c0 + 13:c0 + 14], AF.Ln,
                    bias=eps_t[:xin_ap.shape[0], 0:1]),
                    waits=[t2, CPO], inc=AC)
        t4 = s.emit("ac", lambda e: e.activation(
                    stat[:xin_ap.shape[0], c0 + 13:c0 + 14],
                    stat[:xin_ap.shape[0], c0 + 13:c0 + 14], AF.Exp,
                    scale=-0.5), waits=[t3], inc=AC)
        t5 = s.emit("dv", lambda e: e.scalar_tensor_tensor(
            out=stat[:xin_ap.shape[0], c0 + 12:c0 + 13],
            in0=stat[:xin_ap.shape[0], c0 + 12:c0 + 13], scalar=-1.0,
            in1=stat[:xin_ap.shape[0], c0 + 13:c0 + 14], op0=ALU.mult,
            op1=ALU.mult), waits=[t4], inc=DV)
        t6 = s.emit("dv", lambda e: e.tensor_scalar(
            out=dst, in0=xin_ap,
            scalar1=stat[:xin_ap.shape[0], c0 + 13:c0 + 14],
            scalar2=stat[:xin_ap.shape[0], c0 + 12:c0 + 13],
            op0=ALU.mult, op1=ALU.add), waits=[t5], inc=DV)
        ln_prev[par] = (t4, t6)
        return t6

    G = {}

    def g(*key):
        return G.get(key)

    p1 = ExitStack()
    convd = sb(p1, "convd", [128, NEC * 4 * 128], BF16)
    Hst = sb(p1, "H", [128, DIN], F32)
    hb0 = sb(p1, "hb0", [128, DIN], BF16)
    hb1 = sb(p1, "hb1", [128, DIN], BF16)
    s_ca = sb(p1, "s_ca", [32, ST], F32)
    s_dt = sb(p1, "s_dt", [32, ST], F32)
    s_dta = sb(p1, "s_dta", [32, ST], F32)
    wp1 = sb(p1, "wp1", [128, KD * P1E], BF16)
    xin0 = sb(p1, "xin0", [128, D], F32)
    xin1 = sb(p1, "xin1", [128, D], F32)
    xln0 = sb(p1, "xln0", [128, D], BF16)
    xln1 = sb(p1, "xln1", [128, D], BF16)
    xpl = sb(p1, "xpl", [3, D], BF16)
    xlf0 = sb(p1, "xlf0", [128, KD * 259], BF16)
    xlf1 = sb(p1, "xlf1", [128, KD * 259], BF16)
    xbc = sb(p1, "xbc", [128, NEC * 259], BF16)
    xc = sb(p1, "xc", [128, 16 * ST], BF16)
    b_bf = sb(p1, "b_bf", [128, ST], BF16)
    b_tma = sb(p1, "b_tma", [128, 128], BF16)
    b_tmb = sb(p1, "b_tmb", [128, 128], BF16)
    sp_scr = sb(p1, "sp_scr", [32, ST], F32)
    sp_scr2 = sb(p1, "sp_scr2", [32, ST], F32)
    sc32a = sb(p1, "sc32a", [128, 96], F32)
    sc32b = sb(p1, "sc32b", [128, 96], F32)
    eo_bfa = sb(p1, "eo_bfa", [128, NH], BF16)
    eo_bfb = sb(p1, "eo_bfb", [128, NH], BF16)
    erc_bfa = sb(p1, "erc_bfa", [128, NH], BF16)
    erc_bfb = sb(p1, "erc_bfb", [128, NH], BF16)
    gama = sb(p1, "gama", [128, NH], F32)
    gamb = sb(p1, "gamb", [128, NH], F32)
    dt_bfa = sb(p1, "dt_bfa", [128, NH], BF16)
    dt_bfb = sb(p1, "dt_bfb", [128, NH], BF16)
    dta_bfa = sb(p1, "dta_bfa", [128, NH], BF16)
    dta_bfb = sb(p1, "dta_bfb", [128, NH], BF16)
    rhsT = sb(p1, "rhsT", [128, 4096], BF16)
    expd = sb(p1, "expd", [128, 4096], BF16)
    ghat0 = sb(p1, "ghat0", [128, 4096], BF16)
    ghat1 = sb(p1, "ghat1", [128, 4096], BF16)
    gm_bfa = sb(p1, "gm_bfa", [128, 128], BF16)
    gm_bfb = sb(p1, "gm_bfb", [128, 128], BF16)
    xtm0 = sb(p1, "xtm0", [128, DIN], BF16)
    xtm1 = sb(p1, "xtm1", [128, DIN], BF16)
    xda = sb(p1, "xda", [128, DIN], BF16)
    xdb = sb(p1, "xdb", [128, DIN], BF16)
    xdza = sb(p1, "xdza", [128, DIN], BF16)
    xdzb = sb(p1, "xdzb", [128, DIN], BF16)
    ytm0 = sb(p1, "ytm0", [128, DIN], F32)
    ytm1 = sb(p1, "ytm1", [128, DIN], F32)
    yscr = sb(p1, "yscr", [128, 512], F32)
    hscr = sb(p1, "hscr", [128, 512], F32)
    atile = sb(p1, "atile", [128, NH], F32)

    hb = [hb0, hb1]
    convd_v = convd.rearrange("p (i t) -> p i t", t=128)
    s.emit("po", lambda e: e.memset(Hst[:], 0.0))
    s.emit("po", lambda e: e.memset(hb0[:], 0.0), inc=PO)
    P1Z = s.tok(PO)
    cd = None
    for i in range(NEC * 4):
        cd = s.emit("dv", lambda e, i=i: e.tensor_scalar(
            out=convd_v[:, i, :], in0=identb[:],
            scalar1=cw_sb[:, i:i + 1], scalar2=None, op0=ALU.mult),
            waits=[LD0, CPO], inc=DV)
    CONVD_TOK = cd
    wp1_v = wp1.rearrange("p (k e) -> p k e", k=KD)
    xlf_vs = [xlf0.rearrange("p (k t) -> p k t", k=KD),
              xlf1.rearrange("p (k t) -> p k t", k=KD)]
    xbc_v = xbc.rearrange("p (c t) -> p c t", t=259)
    xc_v = xc.rearrange("p (c t) -> p c t", t=ST)
    xins = [xin0, xin1]
    xlns = [xln0, xln1]
    xtms = [xtm0, xtm1]
    ytms = [ytm0, ytm1]
    ghats = [ghat0, ghat1]
    sc32s = [sc32a, sc32b]
    eo_bfs = [eo_bfa, eo_bfb]
    erc_bfs = [erc_bfa, erc_bfb]
    gams = [gama, gamb]
    dt_bfs = [dt_bfa, dt_bfb]
    dta_bfs = [dta_bfa, dta_bfb]
    gm_bfs = [gm_bfa, gm_bfb]
    b_tms = [b_tma, b_tmb]
    xds = [xda, xdb]
    xdzs = [xdza, xdzb]

    for k in range(KD):
        s.emit("po", lambda e, k=k: e.dma_start(
            out=wp1_v[:, k, :], in_=w_p1[k * 128:(k + 1) * 128, :]),
            waits=[CPO], inc=DPO, amt=16)
    WP1 = s.tok(DPO)

    s.emit("sp", lambda e: e.dma_start(out=xin0[0:3, :], in_=xprev),
           inc=DSP, amt=16)
    XPL = s.tok(DSP)
    tp6 = ln_chunk(xin0[0:3, :], xpl[0:3, :], 0, [XPL])
    lastp = None
    for k in range(KD):
        lastp = s.emit("pe", lambda e, k=k: e.transpose(
            psT2[:, k * 128:k * 128 + 3], xpl[0:3, k * 128:(k + 1) * 128],
            identb[0:3, 0:3]), waits=[tp6] if k == 0 else (),
            inc=PE if k == KD - 1 else None)
    ev = s.emit("dv", lambda e: e.tensor_copy(
        xlf_vs[0][:, :, 0:3],
        psT2.rearrange("p (k t) -> p k t", t=128)[:, 0:KD, 0:3]),
        waits=[lastp], inc=DV)
    psT2_g[0] = ev
    XPREV_EV = ev
    G[("xin_r", -2)] = tp6

    def p1_front(st):
        ch0 = st * CPS
        xlf_v = xlf_vs[st % 2]
        evs = []
        for c2 in range(CPS):
            cg = ch0 + c2
            par = cg % 2
            xin, xln = xins[par], xlns[par]
            s.emit("sp", lambda e, xin=xin, cg=cg: e.dma_start(
                out=xin[:], in_=xs[cg * CH:(cg + 1) * CH, :]),
                waits=[g("xin_r", cg - 2)], inc=DX[par], amt=16)
            xt = s.tok(DX[par])
            lt = ln_chunk(xin[:], xln[:], par, [xt])
            G[("xin_r", cg)] = lt
            wl = [lt, g("xlf_r", st - 2), psT2_g[0], g("xln_r", cg - 2)]
            last = None
            for k in range(KD):
                last = s.emit("pe", lambda e, k=k, xln=xln: e.transpose(
                    psT2[:, k * 128:(k + 1) * 128],
                    xln[:, k * 128:(k + 1) * 128], identb[:]),
                    waits=wl if k == 0 else (),
                    inc=PE if k == KD - 1 else None)
            G[("xln_r", cg)] = last
            ev = s.emit("dv", lambda e, c2=c2, xlf_v=xlf_v: e.tensor_copy(
                xlf_v[:, :, 3 + c2 * CH:3 + (c2 + 1) * CH],
                psT2.rearrange("p (k t) -> p k t", t=128)),
                waits=[last], inc=DV)
            psT2_g[0] = ev
            evs.append(ev)
        if st == 0:
            evs.append(XPREV_EV)
        G[("p1front", st)] = evs

    p1_front(0)
    for st in range(NST):
        ch0 = st * CPS
        xlf_v = xlf_vs[st % 2]
        ln_evs = G[("p1front", st)]

        margin = None
        if st > 0:
            margin = s.emit("dv", lambda e: e.tensor_copy(
                xbc_v[:, :, 0:3], xbc_v[:, :, 256:259]),
                waits=[g("conv_r", st - 1)], inc=DV)

        co = 0 if st == 0 else 3
        N = 259 if st == 0 else 256
        doff = 0 if st == 0 else 3
        evac = []
        dt_info = None
        for ei in range(NEC + 1):
            ps, guard, ri = rot_ps()
            m = NH if ei == NEC else 128
            wl = ln_evs + [WP1, guard]
            last = None
            for k in range(KD):
                last = s.emit("pe",
                    lambda e, k=k, ei=ei, ps=ps, m=m, co=co, N=N,
                    xlf_v=xlf_v: e.matmul(
                        ps[0:m, 0:N], wp1_v[:, k, ei * 128:ei * 128 + m],
                        xlf_v[:, k, co:co + N],
                        start=(k == 0), stop=(k == KD - 1)),
                    waits=wl if k == 0 else (),
                    inc=PE if k == KD - 1 else None)
            if ei < NEC:
                ev = s.emit("ac", lambda e, ei=ei, ps=ps, doff=doff, N=N:
                            e.copy(xbc_v[:, ei, doff:doff + N], ps[:, 0:N]),
                            waits=[last, margin], inc=AC)
                evac.append(ev)
                rot_done(ri, ev)
            else:
                dt_info = (ps, ri, last)
        G[("xlf_r", st)] = s.tok(PE)
        if st + 1 < NST:
            p1_front(st + 1)

        ps, ri, mmtok = dt_info
        off = 3 if st == 0 else 0
        stc = slice(0, ST)
        t1 = s.emit("dv", lambda e, ps=ps, off=off: e.tensor_scalar(
            out=sp_scr[:, :], in0=ps[0:NH, off:off + ST],
            scalar1=smallcol[0:NH, 0:1], scalar2=None, op0=ALU.add),
            waits=[mmtok, A_TOK, g("spscr_r", st - 1)], inc=DV)
        rot_done(ri, t1)
        t2 = s.emit("ac", lambda e: e.activation(sp_scr2[:, :], sp_scr[:, :],
                    AF.Abs), waits=[t1], inc=AC)
        t3 = s.emit("ac", lambda e: e.activation(sp_scr2[:, :], sp_scr2[:, :],
                    AF.Exp, scale=-1.0), waits=[t2], inc=AC)
        t4 = s.emit("ac", lambda e: e.activation(sp_scr2[:, :], sp_scr2[:, :],
                    AF.Ln, bias=1.0), waits=[t3], inc=AC)
        t5 = s.emit("dv", lambda e: e.tensor_scalar(out=sp_scr[:, :],
                    in0=sp_scr[:, :], scalar1=0.0, scalar2=None,
                    op0=ALU.max), waits=[t4], inc=DV)
        t6 = s.emit("dv", lambda e, stc=stc: e.tensor_tensor(
            s_dt[:, stc], sp_scr[:, :], sp_scr2[:, :], ALU.add),
            waits=[t5, g("stk_r", st - 1)], inc=DV)
        G[("spscr_r", st)] = t6
        t7 = s.emit("dv", lambda e, stc=stc: e.tensor_scalar(
            out=s_dta[:, stc], in0=s_dt[:, stc],
            scalar1=smallcol[0:NH, 2:3], scalar2=None, op0=ALU.mult),
            waits=[t6], inc=DV)
        t8 = s.emit("dv", lambda e, stc=stc: e.tensor_tensor_scan(
            s_ca[:, stc], onescol[:, 0:1].broadcast_to([32, ST]),
            s_dta[:, stc], carry[:, 0:1], ALU.mult, ALU.add),
            waits=[t7, CPO], inc=DV)
        SCAN = s.emit("dv", lambda e: e.tensor_copy(
            carry[:, 0:1], s_ca[:, ST - 1:ST]), waits=[t8], inc=DV)

        conv_ev = []
        conv_last = None
        for ei in range(NEC):
            ps, guard, ri = rot_ps()
            wl = [evac[ei], CONVD_TOK, guard]
            last = None
            for tp in range(4):
                last = s.emit("pe", lambda e, ei=ei, tp=tp, ps=ps: e.matmul(
                    ps[:, 0:ST], convd_v[:, ei * 4 + tp, :],
                    xbc_v[:, ei, tp:tp + ST], start=(tp == 0), stop=(tp == 3)),
                    waits=wl if tp == 0 else (), inc=PE if tp == 3 else None)
            conv_last = last
            if ei < 16:
                dst = xc_v[:, ei, :]
                wl2 = [last, g("xc_r", st - 1)]
            elif ei == 16:
                dst = b_bf[:, :]
                wl2 = [last, g("bbf_r", st - 1)]
            else:
                dst = c_all[:, st * ST:(st + 1) * ST]
                wl2 = [last]
            ev = s.emit("ac", lambda e, ps=ps, dst=dst, ei=ei: e.activation(
                dst, ps[:, 0:ST], AF.Silu, bias=cb_sb[:, ei:ei + 1]),
                waits=wl2, inc=AC)
            conv_ev.append(ev)
            rot_done(ri, ev)
        G[("conv_r", st)] = conv_last

        def p1_prep(c2):
            cg = ch0 + c2
            par = cg % 2
            ccols = slice(c2 * CH, (c2 + 1) * CH)
            csl = ccols
            xtm = xtms[par]
            ghat = ghats[par]
            sc32 = sc32s[par]
            eo_bf, erc_bf, gam = eo_bfs[par], erc_bfs[par], gams[par]
            dt_bf, dta_bf = dt_bfs[par], dta_bfs[par]
            gm_bf, b_tm = gm_bfs[par], b_tms[par]
            xd, xdz = xds[par], xdzs[par]

            s.emit("pe", lambda e, ccols=ccols: e.transpose(
                psM[:, 0:32], s_ca[:, ccols], identf[0:32, 0:32]),
                waits=[SCAN, g("psm_ra", cg - 1)])
            s.emit("pe", lambda e, ccols=ccols: e.transpose(
                psM[:, 32:64], s_dt[:, ccols], identf[0:32, 0:32]))
            tT = s.emit("pe", lambda e, ccols=ccols: e.transpose(
                psM[:, 64:96], s_dta[:, ccols], identf[0:32, 0:32]), inc=PE)
            evc = s.emit("ac", lambda e, cg=cg: e.copy(
                cadt_v[:, cg, :], psM[:, 0:96]), waits=[tT], inc=AC)
            G[("p_evc", cg)] = evc
            prev_rhs = cadt_v[:, cg - 1, 0:32] if cg > 0 else zero32[:, 0:32]
            sp1 = s.emit("pe", lambda e, prev_rhs=prev_rhs: e.matmul(
                psM[:, 96:128], e127[:], prev_rhs, start=True, stop=True),
                waits=[evc], inc=PE)
            sp2 = s.emit("pe", lambda e, cg=cg: e.matmul(
                psM[:, 128:160], e127[:], cadt_v[:, cg, 0:32],
                start=True, stop=True), waits=[evc], inc=PE)
            wl = [sp1, sp2]
            a1 = s.emit("dv", lambda e, cg=cg, sc32=sc32: e.tensor_tensor(
                sc32[:, 0:32], cadt_v[:, cg, 0:32], psM[:, 96:128],
                ALU.subtract), waits=wl, inc=DV)
            a2 = s.emit("dv", lambda e, cg=cg, sc32=sc32: e.tensor_tensor(
                sc32[:, 32:64], psM[:, 128:160], cadt_v[:, cg, 0:32],
                ALU.subtract), waits=wl, inc=DV)
            a3 = s.emit("dv", lambda e, sc32=sc32: e.tensor_tensor(
                sc32[:, 64:96], sc32[:, 0:32], sc32[:, 32:64], ALU.add),
                waits=[a1, a2], inc=DV)
            wl_e = [g("qend_dv", cg - 2), g("qend_po", cg - 2)]
            e1 = s.emit("ac", lambda e, eo_bf=eo_bf, sc32=sc32: e.activation(
                eo_bf[:], sc32[:, 0:32], AF.Exp), waits=[a1] + wl_e, inc=AC)
            e2 = s.emit("ac", lambda e, erc_bf=erc_bf, sc32=sc32: e.activation(
                erc_bf[:], sc32[:, 32:64], AF.Exp), waits=[a2] + wl_e, inc=AC)
            e3 = s.emit("ac", lambda e, gam=gam, sc32=sc32: e.activation(
                gam[:], sc32[:, 64:96], AF.Exp), waits=[a3] + wl_e, inc=AC)
            d1 = s.emit("ac", lambda e, cg=cg, dt_bf=dt_bf: e.copy(
                dt_bf[:], cadt_v[:, cg, 32:64]), waits=[evc], inc=AC)
            d2 = s.emit("ac", lambda e, cg=cg, dta_bf=dta_bf: e.copy(
                dta_bf[:], cadt_v[:, cg, 64:96]), waits=[evc], inc=AC)
            rb0 = s.emit("po", lambda e, dta_bf=dta_bf: e.tensor_tensor(
                rhsT.rearrange("p (h t) -> p h t", h=NH)[:, 0:16, :],
                dta_bf[:, 0:16, None].broadcast_to([128, 16, 128]),
                cmaskb[:, None, :].broadcast_to([128, 16, 128]),
                ALU.mult), waits=[d2, g("rhsT_r", cg - 1)], inc=PO)
            rb = s.emit("dv", lambda e, dta_bf=dta_bf: e.tensor_tensor(
                rhsT.rearrange("p (h t) -> p h t", h=NH)[:, 16:32, :],
                dta_bf[:, 16:32, None].broadcast_to([128, 16, 128]),
                cmaskb[:, None, :].broadcast_to([128, 16, 128]),
                ALU.mult), waits=[d2, g("rhsT_r", cg - 1)], inc=DV)
            gmm = s.emit("pe", lambda e, csl=csl, cg=cg: e.matmul(
                psM[:, 160:288], b_bf[:, csl], call_v[:, cg, :],
                start=True, stop=True),
                waits=[conv_ev[16], conv_ev[17], a1, a2], inc=PE)
            gmb = s.emit("dv", lambda e, gm_bf=gm_bf: e.tensor_tensor(
                gm_bf[:], psM[:, 160:288], cmaskb[:], ALU.mult),
                waits=[gmm], inc=DV)
            G[("psm_ra", cg)] = gmb
            if cg == NCH - 1:
                G[("la",)] = s.emit("ac", lambda e: e.activation(
                    atile[:], psM[:, 128:160], AF.Exp), waits=[a1, a2], inc=AC)
            gh_dv = None
            gh_po = None
            am_last = None
            for gi in range(8):
                ps, guard, ri = rot_ps()
                am = s.emit("pe", lambda e, gi=gi, ps=ps: e.matmul(
                    ps[:, 0:512], stepb[:], rhsT[:, gi * 512:(gi + 1) * 512],
                    start=True, stop=True),
                    waits=[rb0 if gi < 4 else rb, guard, CPO], inc=PE)
                am_last = am
                ex = s.emit("ac", lambda e, gi=gi, ps=ps: e.activation(
                    expd[:, gi * 512:(gi + 1) * 512], ps[:, 0:512], AF.Exp),
                    waits=[am, g("expd_r", cg - 1), g("expd_rp", cg - 1)],
                    inc=AC)
                rot_done(ri, ex)
                qeng = "dv" if gi % 2 == 0 else "po"
                gh = s.emit(qeng, lambda e, gi=gi, ghat=ghat, gm_bf=gm_bf:
                    e.tensor_tensor(
                    ghat.rearrange("p (h t) -> p h t", h=NH)[:, gi * 4:(gi + 1) * 4, :],
                    expd.rearrange("p (h t) -> p h t", h=NH)[:, gi * 4:(gi + 1) * 4, :],
                    gm_bf[:, None, :].broadcast_to([128, 4, 128]),
                    ALU.mult), waits=[ex, gmb, g("ghat_r", cg - 2)],
                    inc=DV if gi % 2 == 0 else PO)
                if gi % 2 == 0:
                    gh_dv = gh
                else:
                    gh_po = gh
            G[("rhsT_r", cg)] = am_last
            G[("expd_r", cg)] = gh_dv
            G[("expd_rp", cg)] = gh_po
            bt = s.emit("pe", lambda e, csl=csl: e.transpose(
                psT[:, 0:128], b_bf[:, csl], identb[:]),
                waits=[conv_ev[16], psT_g[0]], inc=PE)
            if c2 == CPS - 1:
                G[("bbf_r", st)] = bt
            bte = s.emit("dv", lambda e, b_tm=b_tm: e.tensor_copy(
                b_tm[:], psT[:, 0:128]),
                waits=[bt, g("xdz_r", cg - 2)], inc=DV)
            psT_g[0] = bte
            wl = [conv_ev[15], bte, g("xtm_r", cg - 2)]
            last = None
            for i in range(16):
                last = s.emit("pe", lambda e, i=i, csl=csl: e.transpose(
                    psT[:, i * 128:(i + 1) * 128], xc_v[:, i, csl], identb[:]),
                    waits=wl if i == 0 else (), inc=PE if i == 15 else None)
            if c2 == CPS - 1:
                G[("xc_r", st)] = last
            xte = s.emit("ac", lambda e, xtm=xtm: e.copy(xtm[:], psT[:]),
                         waits=[last], inc=AC)
            psT_g[0] = xte
            xdt = s.emit("dv", lambda e, xtm=xtm, xd=xd, dt_bf=dt_bf:
                e.tensor_tensor(
                xd.rearrange("p (h q) -> p h q", h=NH),
                xtm.rearrange("p (h q) -> p h q", h=NH),
                dt_bf[:, :, None].broadcast_to([128, NH, HD]),
                ALU.mult), waits=[xte, d1, g("xd_r", cg - 2)], inc=DV)
            xzt = s.emit("dv", lambda e, xd=xd, xdz=xdz, erc_bf=erc_bf:
                e.tensor_tensor(
                xdz.rearrange("p (h q) -> p h q", h=NH),
                xd.rearrange("p (h q) -> p h q", h=NH),
                erc_bf[:, :, None].broadcast_to([128, NH, HD]),
                ALU.mult), waits=[xdt, e2, g("dxbf_r", cg - 2)], inc=DV)
            G[("p_tok", cg)] = (gh_dv, gh_po, xdt, xzt, e1, e3, bte)
            if c2 == CPS - 1:
                G[("stk_r", st)] = tT

        def p1_quarters(c2):
            cg = ch0 + c2
            par = cg % 2
            xtm, ytm = xtms[par], ytms[par]
            ghat = ghats[par]
            hbr, hbw = hb[par], hb[1 - par]
            eo_bf, gam = eo_bfs[par], gams[par]
            b_tm = b_tms[par]
            xd, xdz = xds[par], xdzs[par]
            gh_dv, gh_po, xdt, xzt, e1, e3, bte = G[("p_tok", cg)]
            for q in range(4):
                qs = slice(q * 512, (q + 1) * 512)
                pa, pb = (psY1, psY2) if q % 2 == 0 else (psA, psB)
                wl1 = [gh_dv, gh_po, xdt, g("q_ya", cg * 4 + q - 2)]
                if q % 2 == 1:
                    wl1 += [rotst["tok"][0]]
                l1 = None
                for j in range(8):
                    h = q * 8 + j
                    l1 = s.emit("pe", lambda e, h=h, j=j, ghat=ghat, xd=xd,
                        pa=pa: e.matmul(
                        pa[:, j * 64:(j + 1) * 64],
                        ghat[:, h * 128:(h + 1) * 128],
                        xd[:, h * HD:(h + 1) * HD], start=True, stop=True),
                        waits=wl1 if j == 0 else (), inc=PE if j == 7 else None)
                G[("ghat_r", cg)] = l1
                G[("xd_r", cg)] = l1
                wl2 = [g("q_yb", cg * 4 + q - 2), g("hb_w", cg - 1, q), P1Z]
                if q % 2 == 1:
                    wl2 += [rotst["tok"][1]]
                l2 = None
                for j in range(8):
                    h = q * 8 + j
                    l2 = s.emit("pe", lambda e, h=h, j=j, hbr=hbr, cg=cg,
                        pb=pb: e.matmul(
                        pb[:, j * 64:(j + 1) * 64], call_v[:, cg, :],
                        hbr[:, h * HD:(h + 1) * HD], start=True, stop=True),
                        waits=wl2 if j == 0 else (), inc=PE if j == 7 else None)
                G[("hb_r", cg, q)] = l2
                l3 = None
                s_extra = []
                if q == 0:
                    other = ch0 + 1 if c2 == 0 else cg
                    s_extra = [g("psm_ra", other), g("p_evc", other),
                               g("la")]
                for j in range(8):
                    h = q * 8 + j
                    l3 = s.emit("pe", lambda e, h=h, j=j, b_tm=b_tm,
                        xdz=xdz: e.matmul(
                        psM[:, j * 64:(j + 1) * 64], b_tm[:],
                        xdz[:, h * HD:(h + 1) * HD], start=True, stop=True),
                        waits=[bte, xzt, g("q_ha", cg * 4 + q - 1)] + s_extra
                        if j == 0 else (), inc=PE if j == 7 else None)
                G[("xdz_r", cg)] = l3
                yb = s.emit("dv", lambda e, q=q, eo_bf=eo_bf, pb=pb:
                    e.tensor_tensor(
                    yscr.rearrange("p (j w) -> p j w", j=8),
                    pb.rearrange("p (j w) -> p j w", j=8),
                    eo_bf[:, q * 8:(q + 1) * 8, None].broadcast_to([128, 8, HD]),
                    ALU.mult), waits=[l2, e1, g("ytm_sp", cg - 2)], inc=DV)
                G[("q_yb", cg * 4 + q)] = yb
                ya = s.emit("dv", lambda e, ytm=ytm, qs=qs, pa=pa:
                    e.tensor_tensor(
                    ytm[:, qs], pa[:, 0:512], yscr[:, 0:512], ALU.add),
                    waits=[yb, l1], inc=DV)
                G[("q_ya", cg * 4 + q)] = ya
                hm = s.emit("po", lambda e, q=q, qs=qs, gam=gam: e.tensor_tensor(
                    hscr.rearrange("p (j w) -> p j w", j=8),
                    Hst[:, qs].rearrange("p (j w) -> p j w", j=8),
                    gam[:, q * 8:(q + 1) * 8, None].broadcast_to([128, 8, HD]),
                    ALU.mult), waits=[e3, g("hadd_prev"), P1Z], inc=PO)
                ha = s.emit("dv", lambda e, qs=qs: e.tensor_tensor(
                    Hst[:, qs], hscr[:, 0:512], psM[:, 0:512], ALU.add),
                    waits=[hm, l3], inc=DV)
                G[("q_ha", cg * 4 + q)] = ha
                G[("hadd_prev",)] = ha
                hw = s.emit("ac", lambda e, qs=qs, hbw=hbw: e.copy(
                    hbw[:, qs], Hst[:, qs]),
                    waits=[ha, g("hb_r", cg - 1, q)], inc=AC)
                G[("hb_w", cg, q)] = hw
                if cg == NCH - 1 and q == 3:
                    G[("h_final",)] = ha
            rotst["tok"][0] = G[("q_ya", cg * 4 + 3)]
            rotst["tok"][1] = G[("q_yb", cg * 4 + 3)]
            G[("psm_ra", cg)] = G[("q_ha", cg * 4 + 3)]
            G[("qend_dv", cg)] = s.tok(DV)
            dxt = s.emit("po", lambda e, xtm=xtm, xdz=xdz: e.tensor_tensor(
                xdz.rearrange("p (h w) -> p h w", h=NH),
                xtm.rearrange("p (h w) -> p h w", h=NH),
                db32[:, :, None].broadcast_to([128, NH, HD]), ALU.mult),
                waits=[LD0, g("xdz_r", cg)], inc=PO)
            G[("xtm_r", cg)] = dxt
            ydx = s.emit("po", lambda e, ytm=ytm, xdz=xdz: e.tensor_tensor(
                ytm[:], ytm[:], xdz[:], ALU.add),
                waits=[dxt, s.tok(DV)], inc=PO)
            G[("dxbf_r", cg)] = ydx
            G[("qend_po", cg)] = s.tok(PO)
            s.emit("sp", lambda e, ytm=ytm, cg=cg: e.dma_start(
                out=y_sp[cg], in_=ytm[:]), waits=[ydx], inc=DY[par], amt=16)
            G[("ytm_sp", cg)] = s.tok(DY[par])

        p1_prep(0)
        p1_prep(1)
        p1_quarters(0)
        p1_quarters(1)

    wz_v = wp1_v[:, :, 0:DIN]
    if LIMIT_PHASE >= 2:
        for k in range(KD):
            s.emit("po", lambda e, k=k: e.dma_start(
                out=wz_v[:, k, :], in_=w_z[k * 128:(k + 1) * 128, :]),
                waits=[g("xlf_r", NST - 1)], inc=DPO, amt=16)
    WZTOK = s.tok(DPO)
    s.emit("sp", lambda e: e.dma_start(out=mout[:, 0:DIN], in_=Hst[:]),
           waits=[g("h_final")], inc=DSP, amt=16)
    s.emit("sp", lambda e: e.dma_start(out=mout[:, DIN:DIN + NH],
           in_=atile[:]), waits=[g("la")], inc=DSP, amt=16)
    MOUT = s.tok(DSP)
    tap("sdt", lambda: s_dt[:], waits=[s.tok(DV)])
    tap("sca", lambda: s_ca[:], waits=[s.tok(DV)])
    tap("cadt", lambda: cadt[:], waits=[s.tok(DV)])
    p1.close()

    if LIMIT_PHASE >= 2:
        PH1 = [s.tok(sm) for sm in ("pe", "dv", "ac", "po", "dsp",
                                    "dx0", "dx1", "dy0", "dy1")]
        for eng in ("pe", "dv", "ac", "po", "sp"):
            s.emit(eng, lambda e: e.nop(), waits=PH1)
        ccd = s.emit("po", lambda e: e.collective_compute(
            "AllGather", ALU.bypass,
            replica_groups=[[0, 1, 2, 3], [4, 5, 6, 7]],
            ins=[mout.opt()], outs=[mg.opt()]), waits=[MOUT], inc=CC)

        p2 = ExitStack()
        DEAD_B = 18432 + 8192 + 4096 + 4096 + 3 * 1024
        mga = sb(p2, "mga", [128, 4 * (DIN + NH)], F32)
        hinb = sb(p2, "hinb", [128, DIN], BF16)
        used = 4 * (DIN + NH) * 4 + 4096
        assert used <= DEAD_B, used
        if DEAD_B - used:
            sb(p2, "p2pad0", [128, (DEAD_B - used) // 2], BF16)
        sb(p2, "p2pad1", [128, KD * P1E], BF16)
        wo = sb(p2, "wo", [128, 16 * D], BF16)
        hin = sb(p2, "hin", [128, DIN], F32)
        msel = sb(p2, "msel", [128, NH], F32)
        p2xin0 = sb(p2, "p2xin0", [128, D], F32)
        p2xin1 = sb(p2, "p2xin1", [128, D], F32)
        p2xin2 = sb(p2, "p2xin2", [128, D], F32)
        p2xin3 = sb(p2, "p2xin3", [128, D], F32)
        p2xln0 = sb(p2, "p2xln0", [128, D], BF16)
        p2xln1 = sb(p2, "p2xln1", [128, D], BF16)
        xlf2a = sb(p2, "xlf2a", [128, KD * 128], BF16)
        xlf2b = sb(p2, "xlf2b", [128, KD * 128], BF16)
        yld0 = sb(p2, "yld0", [128, DIN], F32)
        yld1 = sb(p2, "yld1", [128, DIN], F32)
        gtm0 = sb(p2, "gtm0", [128, DIN], F32)
        gtm1 = sb(p2, "gtm1", [128, DIN], F32)
        eog = sb(p2, "eog", [128, NH], BF16)
        yg = sb(p2, "yg", [128, 512], F32)
        yscr2 = sb(p2, "yscr2", [128, 512], F32)
        yn = sb(p2, "yn", [128, DIN], BF16)
        ynf = sb(p2, "ynf", [128, 16 * 128], BF16)
        xmid0 = sb(p2, "xmid0", [128, D], F32)
        xmid1 = sb(p2, "xmid1", [128, D], F32)
        stat2 = sb(p2, "stat2", [128, 4], F32)

        wo_v = wo.rearrange("p (c d) -> p c d", c=16)
        ynf_v = ynf.rearrange("p (c t) -> p c t", c=16)
        xlf2_vs = [xlf2a.rearrange("p (k t) -> p k t", k=KD),
                   xlf2b.rearrange("p (k t) -> p k t", k=KD)]
        p2xins = [p2xin0, p2xin1, p2xin2, p2xin3]
        p2xlns = [p2xln0, p2xln1]
        ylds = [yld0, yld1]
        xmids = [xmid0, xmid1]
        mga_v = mga.rearrange("p (r c) -> p r c", r=4)
        hz = s.emit("po", lambda e: e.memset(hin[:], 0.0), inc=PO)

        for c in range(16):
            s.emit("po", lambda e, c=c: e.dma_start(
                out=wo_v[:, c, :], in_=w_out[c * 128:(c + 1) * 128, :]),
                inc=DPO, amt=16)
        W2 = s.tok(DPO)
        nwt = None
        for c in range(16):
            nwt = s.emit("dv", lambda e, c=c: e.tensor_scalar(
                out=wo_v[:, c, :], in0=wo_v[:, c, :],
                scalar1=nw_sb[:, c:c + 1], scalar2=None, op0=ALU.mult),
                waits=[W2, LD0], inc=DV)
        WOUT = nwt

        def p2_combine():
            s.emit("po", lambda e: e.dma_start(
                out=mga_v[:, :, :],
                in_=mg.rearrange("(r p) c -> p r c", p=128)),
                waits=[ccd], inc=DPO, amt=16)
            mjt = s.tok(DPO)
            prev = None
            for j in range(4):
                ms = s.emit("dv", lambda e, j=j: e.tensor_scalar(
                    out=msel[:], in0=mga_v[:, j, DIN:DIN + NH],
                    scalar1=selb[:, 2 * j:2 * j + 1],
                    scalar2=selb1m[:, 2 * j:2 * j + 1],
                    op0=ALU.mult, op1=ALU.add), waits=[mjt, LD0], inc=DV)
                hm2 = s.emit("dv", lambda e: e.tensor_tensor(
                    hin.rearrange("p (h w) -> p h w", h=NH),
                    hin.rearrange("p (h w) -> p h w", h=NH),
                    msel[:, :, None].broadcast_to([128, NH, HD]), ALU.mult),
                    waits=[ms, hz], inc=DV)
                prev = s.emit("dv", lambda e, j=j: e.scalar_tensor_tensor(
                    out=hin[:], in0=mga_v[:, j, 0:DIN],
                    scalar=selb[:, 2 * j + 1:2 * j + 2], in1=hin[:],
                    op0=ALU.mult, op1=ALU.add), waits=[hm2], inc=DV)
            hib = s.emit("dv", lambda e: e.tensor_copy(hinb[:], hin[:]),
                         waits=[prev], inc=DV)
            tap("hin", lambda: hin[:], waits=[prev])
            return hib

        def p2_front(cg):
            par = cg % 2
            xin, xln = p2xins[cg % 4], p2xlns[par]
            yld = ylds[par]
            s.emit("po", lambda e, xin=xin, cg=cg: e.dma_start(
                out=xin[:], in_=xs[cg * CH:(cg + 1) * CH, :]),
                waits=[g("p2xin_r", cg - 4)], inc=DX[par], amt=16)
            s.emit("po", lambda e, yld=yld, cg=cg: e.dma_start(
                out=yld[:], in_=y_sp[cg]),
                waits=[g("yld_r", cg - 2)], inc=DY[par], amt=16)
            G[("p2ylt", cg)] = s.tok(DY[par])
            xt = s.tok(DX[par])
            lt = ln_chunk(xin[:], xln[:], par, [xt])
            wl = [lt, psT2_g[0], g("xlf2_r", cg - 2), g("p2xln_r", cg - 2)]
            last = None
            for k in range(KD):
                last = s.emit("pe", lambda e, k=k, xln=xln: e.transpose(
                    psT2[:, k * 128:(k + 1) * 128],
                    xln[:, k * 128:(k + 1) * 128], identb[:]),
                    waits=wl if k == 0 else (), inc=PE if k == KD - 1 else None)
            G[("p2xln_r", cg)] = last
            ev = s.emit("dv", lambda e, cg=cg: e.tensor_copy(
                xlf2_vs[cg % 2][:, :, :],
                psT2.rearrange("p (k t) -> p k t", t=128)),
                waits=[last], inc=DV)
            psT2_g[0] = ev
            G[("p2ev", cg)] = ev

        def p2_z(cg):
            par = cg % 2
            xlf2_v = xlf2_vs[par]
            ev = G[("p2ev", cg)]
            gtm = [gtm0, gtm1][par]
            zev = []
            for gi in range(4):
                ps, guard, ri = rot_ps()
                lastz = None
                for k in range(KD):
                    lastz = s.emit("pe", lambda e, k=k, gi=gi, ps=ps,
                                   xlf2_v=xlf2_v: e.matmul(
                        ps[:, 0:512], xlf2_v[:, k, :],
                        wz_v[:, k, gi * 512:(gi + 1) * 512],
                        start=(k == 0), stop=(k == KD - 1)),
                        waits=[ev, WZTOK, guard] if k == 0 else (),
                        inc=PE if k == KD - 1 else None)
                zv = s.emit("ac", lambda e, gi=gi, ps=ps, gtm=gtm: e.activation(
                    gtm[:, gi * 512:(gi + 1) * 512], ps[:, 0:512], AF.Silu),
                    waits=[lastz, g("gtm_r", cg - 2)], inc=AC)
                zev.append(zv)
                rot_done(ri, zv)
            G[("xlf2_r", cg)] = s.tok(PE)
            G[("p2zev", cg)] = zev

        DYPS = (psY1, psY2, psM, psY2)

        def dy_q(cg, q, wl):
            psq = DYPS[q]
            l2 = None
            for j in range(8):
                h = q * 8 + j
                l2 = s.emit("pe", lambda e, h=h, j=j, psq=psq, cg=cg:
                    e.matmul(psq[:, j * 64:(j + 1) * 64], call_v[:, cg, :],
                    hinb[:, h * HD:(h + 1) * HD], start=True, stop=True),
                    waits=wl if j == 0 else (), inc=PE if j == 7 else None)
            G[("dy", cg, q)] = l2

        def dy_pre(cg, hib):
            G[("eg", cg)] = s.emit("ac", lambda e, cg=cg: e.activation(
                eog[:], cadt_v[:, cg, 0:32], AF.Exp),
                waits=[g("eog_r", cg - 1)], inc=AC)
            dy_q(cg, 0, [hib, g("t1", cg - 1, 0)])
            dy_q(cg, 1, [hib, g("t1", cg - 1, 3)])
            dy_q(cg, 2, [hib, g("t1", cg - 1, 2)])

        p2_front(0)
        p2_z(0)
        p2_front(1)
        p2_z(1)
        hib = p2_combine()
        dy_pre(0, hib)
        for cg in range(NCH):
            par = cg % 2
            xin = p2xins[cg % 4]
            yld = ylds[par]
            xmid = xmids[par]
            gtm = [gtm0, gtm1][par]
            ev = G[("p2ev", cg)]
            ylt = G[("p2ylt", cg)]
            zev = G[("p2zev", cg)]
            if cg + 1 < NCH and ("p2zev", cg + 1) not in G:
                p2_z(cg + 1)
            eg = G[("eg", cg)]
            ygt = None
            sql = None
            for q in range(4):
                qs = slice(q * 512, (q + 1) * 512)
                psq = DYPS[q]
                t1q = s.emit("dv", lambda e, q=q, psq=psq: e.tensor_tensor(
                    yscr2.rearrange("p (j w) -> p j w", j=8),
                    psq.rearrange("p (j w) -> p j w", j=8),
                    eog[:, q * 8:(q + 1) * 8, None].broadcast_to([128, 8, HD]),
                    ALU.mult), waits=[G[("dy", cg, q)], eg], inc=DV)
                G[("t1", cg, q)] = t1q
                t2q = s.emit("dv", lambda e, qs=qs, yld=yld: e.tensor_tensor(
                    yscr2[:, 0:512], yscr2[:, 0:512], yld[:, qs], ALU.add),
                    waits=[t1q, ylt], inc=DV)
                ygt = s.emit("dv", lambda e, qs=qs, gtm=gtm: e.tensor_tensor(
                    yn[:, qs], yscr2[:, 0:512], gtm[:, qs], ALU.mult),
                    waits=[t2q, zev[q], g("yn_r", cg - 1)], inc=DV)
                sql = s.emit("ac", lambda e, q=q, qs=qs: e.activation(
                    yg[:, 0:512], yn[:, qs], AF.Square,
                    accum_out=stat[:, 34 + q:35 + q]), waits=[ygt], inc=AC)
                if q == 1:
                    dy_q(cg, 3, [G[("t1", cg, 1)]])
            G[("yld_r", cg)] = ygt
            G[("eog_r", cg)] = G[("t1", cg, 3)]
            G[("gtm_r", cg)] = ygt
            if cg + 1 < NCH:
                dy_pre(cg + 1, hib)
            acc = s.emit("ac", lambda e: e.activation(
                stat2[:, 0:4], stat[:, 34:38], AF.Copy,
                accum_out=stat[:, 32:33]), waits=[sql], inc=AC)
            r2 = s.emit("ac", lambda e: e.activation(stat[:, 33:34],
                        stat[:, 32:33], AF.Ln, scale=1.0 / DIN,
                        bias=eps_t[:, 0:1]),
                        waits=[acc, g("rms_r", cg - 1)], inc=AC)
            r3 = s.emit("ac", lambda e: e.activation(stat[:, 33:34],
                        stat[:, 33:34], AF.Exp, scale=-0.5), waits=[r2], inc=AC)
            wl = [ygt, psT_g[0]]
            last = None
            for i in range(16):
                last = s.emit("pe", lambda e, i=i: e.transpose(
                    psT[:, i * 128:(i + 1) * 128], yn[:, i * 128:(i + 1) * 128],
                    identb[:]), waits=wl if i == 0 else (),
                    inc=PE if i == 15 else None)
            G[("yn_r", cg)] = last
            evy = s.emit("dv", lambda e: e.tensor_copy(ynf[:], psT[:]),
                         waits=[last], inc=DV)
            psT_g[0] = evy
            xr = None
            for gi in range(2):
                ps, guard, ri = rot_ps()
                lasto = None
                for c in range(16):
                    lasto = s.emit("pe", lambda e, c=c, gi=gi, ps=ps: e.matmul(
                        ps[:, 0:512], ynf_v[:, c, :],
                        wo_v[:, c, gi * 512:(gi + 1) * 512],
                        start=(c == 0), stop=(c == 15)),
                        waits=[evy, WOUT, guard] if c == 0 else (),
                        inc=PE if c == 15 else None)
                xr = s.emit("dv", lambda e, gi=gi, ps=ps, xmid=xmid, xin=xin:
                    e.scalar_tensor_tensor(
                    out=xmid[:, gi * 512:(gi + 1) * 512], in0=ps[:, 0:512],
                    scalar=stat[:, 33:34], in1=xin[:, gi * 512:(gi + 1) * 512],
                    op0=ALU.mult, op1=ALU.add),
                    waits=[lasto, r3, g("xmid_sp", cg - 2)], inc=DV)
                rot_done(ri, xr)
            G[("p2xin_r", cg)] = xr
            G[("rms_r", cg)] = xr
            s.emit("sp", lambda e, xmid=xmid, cg=cg: e.dma_start(
                out=xm_sp[cg], in_=xmid[:]), waits=[xr], inc=DZ[par], amt=16)
            G[("xmid_sp", cg)] = s.tok(DZ[par])
            if cg + 2 < NCH:
                p2_front(cg + 2)
        tap("gtm", lambda: gtm1[:], waits=[s.tok(AC)])
        tap("yg", lambda: yg[:], waits=[s.tok(DV)])
        p2.close()
        p12.close()

        if LIMIT_PHASE >= 3:
            PH2 = [s.tok(sm) for sm in ("pe", "dv", "ac", "po", "dsp",
                                        "dx0", "dx1", "dy0", "dy1",
                                        "dz0", "dz1")]
            for eng in ("pe", "dv", "ac", "po", "sp"):
                s.emit(eng, lambda e: e.nop(), waits=PH2)
            p3 = ExitStack()
            wfc = sb(p3, "wfc", [128, KD * DFF], BF16)
            wpr = sb(p3, "wpr", [128, NFC * D], BF16)
            xm0 = sb(p3, "xm0", [128, D], F32)
            xm1 = sb(p3, "xm1", [128, D], F32)
            xm2 = sb(p3, "xm2", [128, D], F32)
            xm3 = sb(p3, "xm3", [128, D], F32)
            xl20 = sb(p3, "xl20", [128, D], BF16)
            xl21 = sb(p3, "xl21", [128, D], BF16)
            xlf3a = sb(p3, "xlf3a", [128, KD * ST], BF16)
            xlf3b = sb(p3, "xlf3b", [128, KD * ST], BF16)
            rlb = sb(p3, "rlb", [128, ST], BF16)
            hbuf = sb(p3, "hbuf", [128, NFC * ST], BF16)
            y2 = sb(p3, "y2", [128, D], F32)
            spka = sb(p3, "spka", [128, D], BF16)
            spkb = sb(p3, "spkb", [128, D], BF16)
            outb0 = sb(p3, "outb0", [128, D], F32)
            outb1 = sb(p3, "outb1", [128, D], F32)

            wfc_v = wfc.rearrange("p (k f) -> p k f", k=KD)
            wpr_v = wpr.rearrange("p (f d) -> p f d", f=NFC)
            xlf3_vs = [xlf3a.rearrange("p (k t) -> p k t", k=KD),
                       xlf3b.rearrange("p (k t) -> p k t", k=KD)]
            hbuf_v = hbuf.rearrange("p (f t) -> p f t", f=NFC)
            xms = [xm0, xm1, xm2, xm3]
            xl2s = [xl20, xl21]
            outbs = [outb0, outb1]

            for k in range(KD):
                s.emit("po", lambda e, k=k: e.dma_start(
                    out=wfc_v[:, k, :], in_=w_fc[k * 128:(k + 1) * 128, :]),
                    inc=DPO, amt=16)
            WFC = s.tok(DPO)
            for f in range(NFC):
                s.emit("po", lambda e, f=f: e.dma_start(
                    out=wpr_v[:, f, :], in_=w_pr[f * 128:(f + 1) * 128, :]),
                    inc=DPO, amt=16)
            WPR = s.tok(DPO)

            def p3_front(st):
                ch0 = st * CPS
                xlf3_v = xlf3_vs[st % 2]
                evs = []
                for c2 in range(CPS):
                    cg = ch0 + c2
                    par = cg % 2
                    xm, xl2 = xms[cg % 4], xl2s[par]
                    s.emit("sp", lambda e, xm=xm, cg=cg: e.dma_start(
                        out=xm[:], in_=xm_sp[cg]),
                        waits=[g("xm_r", cg - 4)], inc=DY[par], amt=16)
                    xt = s.tok(DY[par])
                    lt = ln_chunk(xm[:], xl2[:], par, [xt])
                    wl = [lt, psT2_g[0], g("xlf3_r", st - 2), g("xl2_r", cg - 2)]
                    last = None
                    for k in range(KD):
                        last = s.emit("pe", lambda e, k=k, xl2=xl2: e.transpose(
                            psT2[:, k * 128:(k + 1) * 128],
                            xl2[:, k * 128:(k + 1) * 128], identb[:]),
                            waits=wl if k == 0 else (),
                            inc=PE if k == KD - 1 else None)
                    G[("xl2_r", cg)] = last
                    ev = s.emit("dv", lambda e, c2=c2, xlf3_v=xlf3_v:
                        e.tensor_copy(
                        xlf3_v[:, :, c2 * CH:(c2 + 1) * CH],
                        psT2.rearrange("p (k t) -> p k t", t=128)),
                        waits=[last], inc=DV)
                    psT2_g[0] = ev
                    evs.append(ev)
                G[("p3front", st)] = evs

            p3_front(0)
            for st in range(NST):
                ch0 = st * CPS
                xlf3_v = xlf3_vs[st % 2]
                evs = G[("p3front", st)]
                for f in range(NFC):
                    ps, guard, ri = rot_ps()
                    lastf = None
                    for k in range(KD):
                        lastf = s.emit("pe", lambda e, k=k, f=f, ps=ps,
                                       xlf3_v=xlf3_v: e.matmul(
                            ps[:, 0:ST], wfc_v[:, k, f * 128:(f + 1) * 128],
                            xlf3_v[:, k, :], start=(k == 0), stop=(k == KD - 1)),
                            waits=evs + [WFC, guard] if k == 0 else (),
                            inc=PE if k == KD - 1 else None)
                    rl = s.emit("ac", lambda e, ps=ps: e.activation(
                        rlb[:], ps[:, 0:ST], AF.Relu),
                        waits=[lastf, g("rlb_r")], inc=AC)
                    hev = s.emit("dv", lambda e, f=f: e.tensor_tensor(
                        hbuf_v[:, f, :], rlb[:], rlb[:], ALU.mult),
                        waits=[rl, g("hbuf_r", st - 1)], inc=DV)
                    G[("rlb_r",)] = hev
                    rot_done(ri, hev)
                if st + 1 < NST:
                    p3_front(st + 1)
                HEV = s.tok(DV)
                for c2 in range(CPS):
                    cg = ch0 + c2
                    par = cg % 2
                    xm = xms[cg % 4]
                    outb = outbs[par]
                    for gi in range(2):
                        psq = psY1 if gi == 0 else psY2
                        lasto = None
                        for f in range(NFC):
                            lasto = s.emit("pe",
                                lambda e, f=f, gi=gi, psq=psq, c2=c2: e.matmul(
                                    psq[:, 0:512],
                                    hbuf_v[:, f, c2 * CH:(c2 + 1) * CH],
                                    wpr_v[:, f, gi * 512:(gi + 1) * 512],
                                    start=(f == 0), stop=(f == NFC - 1)),
                                waits=[HEV, WPR, g("p3ps_r", gi)]
                                if f == 0 else (),
                                inc=PE if f == NFC - 1 else None)
                        yv = s.emit("ac", lambda e, gi=gi, psq=psq: e.copy(
                            y2[:, gi * 512:(gi + 1) * 512], psq[:, 0:512]),
                            waits=[lasto, g("y2_r", cg - 1)], inc=AC)
                        G[("p3ps_r", gi)] = yv
                    Y2 = s.tok(AC)
                    sp0 = s.emit("dv", lambda e: e.tensor_scalar(
                        out=spka[:], in0=y2[:], scalar1=SPIKE_B[0],
                        scalar2=None, op0=ALU.is_gt),
                        waits=[Y2, g("spka_r", cg - 2)], inc=DV)
                    acc = sp0
                    for j in range(1, 5):
                        sj = s.emit("dv", lambda e, j=j: e.tensor_scalar(
                            out=spkb[:], in0=y2[:], scalar1=SPIKE_B[j],
                            scalar2=None, op0=ALU.is_gt),
                            waits=[Y2, g("spkb_r")], inc=DV)
                        acc = s.emit("dv", lambda e: e.tensor_tensor(
                            spka[:], spka[:], spkb[:], ALU.add),
                            waits=[sj, acc], inc=DV)
                        G[("spkb_r",)] = acc
                    G[("y2_r", cg)] = acc
                    ob = s.emit("dv", lambda e, outb=outb, xm=xm:
                        e.scalar_tensor_tensor(out=outb[:], in0=spka[:],
                        scalar=0.2, in1=xm[:], op0=ALU.mult, op1=ALU.add),
                        waits=[acc, g("outb_sp", cg - 2)], inc=DV)
                    G[("spka_r", cg)] = ob
                    G[("xm_r", cg)] = ob
                    s.emit("sp", lambda e, outb=outb, cg=cg: e.dma_start(
                        out=out[cg * CH:(cg + 1) * CH, :], in_=outb[:]),
                        waits=[ob], inc=DZ[par], amt=16)
                    G[("outb_sp", cg)] = s.tok(DZ[par])
                G[("xlf3_r", st)] = s.tok(PE)
                G[("hbuf_r", st)] = s.tok(PE)
            tap("y2t", lambda: y2[:], waits=[s.tok(AC)])
            tap("spk", lambda: spka[:], waits=[s.tok(DV)], q="po")
            tap("hbt", lambda: hbuf[:], waits=[s.tok(DV)], q="po")
            p3.close()

    if LIMIT_PHASE < 2:
        p12.close()
    fin = [s.tok(sm) for sm in ("dsp", "dx0", "dx1", "dy0", "dy1",
                                "dz0", "dz1", "dpo")]
    if LIMIT_PHASE >= 2:
        fin.append(s.tok("cc"))
    s.emit("sp", lambda e: e.nop(), waits=fin)

    with nc.Block() as block:
        def mk(eng_name):
            def body(eng):
                for fn, waits, inc, amt in s.ops[eng_name]:
                    for sm, v in waits:
                        eng.wait_ge(sems[sm], v)
                    r = fn(eng)
                    if inc is not None:
                        r.then_inc(sems[inc], amt)
            return body
        block.tensor(mk("pe"))
        block.vector(mk("dv"))
        block.scalar(mk("ac"))
        block.gpsimd(mk("po"))
        block.sync(mk("sp"))

    oes.close()
    return nc


_CACHE = {}


def _get_nc():
    key = (LIMIT_PHASE, tuple(sorted(map(str, TAPS))))
    if key not in _CACHE:
        _CACHE[key] = build_nc()
    return _CACHE[key]


def kernel(L0_attn_X, L0_attn_W_in, L0_attn_conv_w, L0_attn_conv_b,
           L0_attn_dt_bias, L0_attn_A_log, L0_attn_D, L0_attn_norm_w,
           L0_attn_W_out, L0_ffn_W_fc, L0_ffn_W_proj):
    X = np.asarray(L0_attn_X, np.float32)
    W_in = np.asarray(L0_attn_W_in, np.float32)
    conv_w = np.asarray(L0_attn_conv_w, np.float32)
    conv_b = np.asarray(L0_attn_conv_b, np.float32)
    dt_bias = np.asarray(L0_attn_dt_bias, np.float32)
    A_log = np.asarray(L0_attn_A_log, np.float32)
    Dv = np.asarray(L0_attn_D, np.float32)
    norm_w = np.asarray(L0_attn_norm_w, np.float32)
    W_out = np.asarray(L0_attn_W_out, np.float32)
    W_fc = np.asarray(L0_ffn_W_fc, np.float32)
    W_proj = np.asarray(L0_ffn_W_proj, np.float32)

    w_p1h = np.ascontiguousarray(W_in[DIN:, :].T)
    w_zh = np.ascontiguousarray(W_in[:DIN, :].T)
    w_oh = np.ascontiguousarray(W_out.T)
    w_fh = np.ascontiguousarray(W_fc.T)
    w_ph = np.ascontiguousarray(W_proj.T)
    cw = np.ascontiguousarray(
        conv_w.reshape(NEC, 128, 4).transpose(1, 0, 2).reshape(128, NEC * 4))
    cb = np.ascontiguousarray(conv_b.reshape(NEC, 128).T)
    dtb = np.ascontiguousarray(dt_bias.reshape(NH, 1))
    alog = np.ascontiguousarray(A_log.reshape(NH, 1))
    dvec = np.ascontiguousarray(np.broadcast_to(Dv[None, :], (128, NH)))
    nw = np.ascontiguousarray(norm_w.reshape(16, 128).T)

    in_maps = []
    for k in range(NCORES):
        b, q = divmod(k, 4)
        xs_k = np.ascontiguousarray(X[b, q * TPC:(q + 1) * TPC])
        if q == 0:
            xp = np.zeros((3, D), np.float32)
        else:
            xp = np.ascontiguousarray(X[b, q * TPC - 3:q * TPC])
        sel = np.zeros((1, 8), np.float32)
        for j in range(4):
            if j < q:
                sel[0, 2 * j] = 1.0
                sel[0, 2 * j + 1] = 1.0
        selm = 1.0 - sel
        in_maps.append({
            "xs": xs_k, "xprev": xp, "w_p1": w_p1h, "w_z": w_zh,
            "w_out": w_oh, "w_fc": w_fh, "w_pr": w_ph, "cw": cw, "cb": cb,
            "dtb": dtb, "alog": alog, "dvec": dvec, "nw": nw,
            "sel": np.ascontiguousarray(np.broadcast_to(sel, (128, 8))),
            "selm": np.ascontiguousarray(np.broadcast_to(selm, (128, 8))),
        })

    nc = _get_nc()
    res = run_bass_kernel_spmd(nc, in_maps, core_ids=list(range(NCORES)))
    globals()["LAST_RES"] = res
    outp = np.empty((B, S, D), np.float32)
    for k in range(NCORES):
        b, q = divmod(k, 4)
        outp[b, q * TPC:(q + 1) * TPC] = res.results[k]["out"]
    return outp
